# version 17
# baseline (speedup 1.0000x reference)
import numpy as np
import ml_dtypes

import concourse.bass as bass
import concourse.mybir as mybir
from concourse.bass_utils import run_bass_kernel_spmd
from contextlib import ExitStack
from concourse.bass_primitives import MemorySpace

F32 = mybir.dt.float32
BF16 = mybir.dt.bfloat16
I8 = mybir.dt.int8
AF = mybir.ActivationFunctionType
ALU = mybir.AluOpType
AX = mybir.AxisListType

D_MODEL = 4096
SEQ = 2048
NH = 32
NKV = 8
HD = 128
NC_ = 32
DS = (8, 16)
S4 = float(128.0 ** -0.25)
EPS = 1e-5

ENGS = ("sync", "gpsimd", "tensor", "vector", "scalar")


class Sem:
    __slots__ = ("name", "count", "handle")

    def __init__(self, name):
        self.name = name
        self.count = 0
        self.handle = None


class Op:
    __slots__ = ("method", "args", "kwargs", "waits", "incs")

    def __init__(self, method, args, kwargs, waits, incs):
        self.method = method
        self.args = args
        self.kwargs = kwargs
        self.waits = waits
        self.incs = incs


class Plan:
    def __init__(self):
        self.ops = {e: [] for e in ENGS}
        self.sems = {}
        self.wm = {e: {} for e in ENGS}
        self.psem = {
            "sync": self.sem("sLD"),
            "gpsimd": self.sem("sGP"),
            "tensor": self.sem("sPE"),
            "vector": self.sem("sDV"),
            "scalar": self.sem("sAC"),
        }
        self.sst = self.sem("sST")
        self.store_sems = set()

    def sem(self, name):
        if name not in self.sems:
            self.sems[name] = Sem(name)
        return self.sems[name]

    def emit(self, eng, method, *args, waits=(), inc=None, inc_n=1, **kwargs):
        wl = []
        wm = self.wm[eng]
        for wv in waits:
            if wv is None:
                continue
            s, v = wv
            if wm.get(s.name, 0) >= v:
                continue
            wm[s.name] = v
            wl.append((s, v))
        il = []
        tok = None
        if inc is not None:
            s = self.psem[eng] if inc is True else inc
            s.count += inc_n
            il.append((s, inc_n))
            tok = (s, s.count)
        self.ops[eng].append(Op(method, args, kwargs, wl, il))
        return tok

    def dma(self, eng, out, in_, waits=(), sem=None, **kw):
        if sem is None:
            is_store = out.space != MemorySpace.SBUF
            name = (in_ if is_store else out).tensor.name
            sem = self.sem("b_" + name)
            if is_store:
                self.store_sems.add(sem.name)
        return self.emit(eng, "dma_start", waits=waits, inc=sem, inc_n=16,
                         out=out, in_=in_, **kw)

    def barrier(self):
        for e in ENGS:
            ops = self.ops[e]
            last = None
            for op in reversed(ops):
                if op.method not in ("wait_ge", "wait_ge_final"):
                    last = op
                    break
            if last is None:
                continue
            if not last.incs:
                s = self.psem[e]
                s.count += 1
                last.incs.append((s, 1))
        targets = [(s, s.count) for s in self.sems.values() if s.count > 0]
        for e in ENGS:
            wm = self.wm[e]
            for (s, v) in targets:
                if v > 0 and wm.get(s.name, 0) < v:
                    wm[s.name] = v
                    self.ops[e].append(Op("wait_ge", (), {}, [(s, v)], []))


def _replay(plan, eng_name, eng):
    for op in plan.ops[eng_name]:
        for (s, v) in op.waits:
            eng.wait_ge(s.handle, v)
        if op.method == "wait_ge":
            continue
        inst = getattr(eng, op.method)(*op.args, **op.kwargs)
        for (s, n) in op.incs:
            inst.then_inc(s.handle, n)


def build_nc():
    nc = bass.Bass("TRN2", target_bir_lowering=False)
    P = Plan()
    sLD = P.psem["sync"]
    sGP = P.psem["gpsimd"]
    sPE = P.psem["tensor"]
    sDV = P.psem["vector"]
    sAC = P.psem["scalar"]
    sST = P.sst

    xh = nc.dram_tensor("xh", [2, 1024, D_MODEL], F32, kind="ExternalInput").ap()
    wqkv = nc.dram_tensor("wqkv", [D_MODEL, 6144], F32, kind="ExternalInput").ap()
    wo = nc.dram_tensor("wo", [D_MODEL, D_MODEL], F32, kind="ExternalInput").ap()
    wf = nc.dram_tensor("wf", [D_MODEL, D_MODEL], I8, kind="ExternalInput").ap()
    bias8 = nc.dram_tensor("bias8", [128, D_MODEL], I8, kind="ExternalInput").ap()
    masks = nc.dram_tensor("masks", [2, 8, 128, 512], BF16, kind="ExternalInput").ap()
    ropek = nc.dram_tensor("ropek", [2, 2, 128, 1024], F32, kind="ExternalInput").ap()
    ropeq = nc.dram_tensor("ropeq", [2, 2, 128, 512], F32, kind="ExternalInput").ap()
    cid = nc.dram_tensor("cid", [3, 128, 128], BF16, kind="ExternalInput").ap()
    onesf_d = nc.dram_tensor("onesf", [128, 128], F32, kind="ExternalInput").ap()
    out_sh = nc.dram_tensor("out_sh", [1024, D_MODEL], F32, kind="ExternalOutput").ap()
    attn_scr = nc.dram_tensor("attn_scr", [NH, 128, 1024], BF16, kind="Internal").ap()
    x1_scr = nc.dram_tensor("x1_scr", [1024, D_MODEL], F32, kind="Internal").ap()
    h1own_scr = nc.dram_tensor("h1own_scr", [128, NC_ * 512], BF16, kind="Internal").ap()
    wqkv_bf = nc.dram_tensor("wqkv_bf", [D_MODEL, 6144], BF16, kind="Internal").ap()
    wo_bf = nc.dram_tensor("wo_bf", [D_MODEL, D_MODEL], BF16, kind="Internal").ap()
    wf_bf = nc.dram_tensor("wf_bf", [D_MODEL, D_MODEL], BF16, kind="Internal").ap()

    ctx = ExitStack()

    def sb(name, shape, dt):
        return ctx.enter_context(nc.sbuf_tensor(name, shape, dt))

    idn = sb("idn", [128, 128], BF16)
    onesb = sb("onesb", [128, 128], BF16)
    onesf = sb("onesf_t", [128, 128], F32)
    rcomb = sb("rcomb", [128, 128], BF16)
    epscol = sb("epscol", [128, 1], F32)
    biascol = sb("biascol", [128, NH], F32)
    mk2row = sb("mk2row", [1, NKV], F32)
    rrow = sb("rrow", [1, 512], F32)
    brow = sb("brow", [1, 4], F32)

    s_pckv = P.sem("pc_kv")
    s_pcq = P.sem("pc_q")
    s_pcwo = P.sem("pc_wo")
    s_pcwf = P.sem("pc_wf")
    for i in range(2):
        P.dma("gpsimd", wqkv_bf[:, 4096 + i * 1024:4096 + (i + 1) * 1024],
              wqkv[:, 4096 + i * 1024:4096 + (i + 1) * 1024], sem=s_pckv)
    t_pckv = (s_pckv, s_pckv.count)
    t_pcq = (s_pcq, 16 * 16)
    t_pcwo = (s_pcwo, 16 * 16)
    t_pcwf = (s_pcwf, 16 * 16)

    def precast_q(i):
        P.dma("gpsimd", wqkv_bf[:, i * 256:(i + 1) * 256],
              wqkv[:, i * 256:(i + 1) * 256], sem=s_pcq)

    def precast_wowf(i):
        if i < 16:
            P.dma("gpsimd", wo_bf[:, i * 256:(i + 1) * 256],
                  wo[:, i * 256:(i + 1) * 256], sem=s_pcwo)
        elif i < 32:
            j = i - 16
            P.dma("gpsimd", wf_bf[:, j * 256:(j + 1) * 256],
                  wf[:, j * 256:(j + 1) * 256], sem=s_pcwf)

    t_idn = P.dma("sync", idn[:], cid[0])
    t_ones = P.dma("sync", onesb[:], cid[1])
    t_rcomb = P.dma("sync", rcomb[:], cid[2])
    t_onesf = P.dma("sync", onesf[:], onesf_d)
    P.emit("vector", "memset", epscol[:], EPS, inc=True)

    ctx1 = ExitStack()

    def sb1(name, shape, dt):
        return ctx1.enter_context(nc.sbuf_tensor(name, shape, dt))

    h1_fm = sb1("h1_fm", [128, NC_, 1024], BF16)
    kfm = sb1("kfm", [128, NKV, 2048], BF16)
    vtok = sb1("vtok", [128, 16, 1024], BF16)
    ropek_sb = sb1("ropek_sb", [128, 2, 1024], F32)
    tmpb = sb1("tmpb", [128, 2, 512], BF16)

    rope_st = {"c1": None, "c2": None}

    def rope_emit(pb, csA, csB, dst_ev, dst_od, first_waits, pcA, pcB, freeA, freeB):
        st = rope_st
        w1 = list(first_waits)
        if st["c1"]:
            w1.append(st["c1"])
        tA = P.emit("vector", "tensor_mul", tmpb[:, 0, :], pb[:], csA,
                    waits=w1, inc=True)
        w2 = [st["c2"]] if st["c2"] else []
        t4 = P.emit("vector", "tensor_mul", tmpb[:, 1, :], pb[:], csB,
                    waits=w2, inc=True)
        wc = [tA, t4, t_rcomb] + [t for t in freeA if t]
        mm1 = P.emit("tensor", "matmul", pcA[:], rcomb[:], tmpb[:, 0, :],
                     start=True, stop=True, waits=wc, inc=True)
        mm2 = P.emit("tensor", "matmul", pcB[:], rcomb[:], tmpb[:, 1, :],
                     start=True, stop=True,
                     waits=[t4] + [t for t in freeB if t], inc=True)
        e1 = P.emit("vector", "tensor_copy", dst_ev, pcA[0:64, :],
                    waits=[mm1], inc=True)
        e2 = P.emit("vector", "tensor_copy", dst_od, pcB[64:128, :],
                    waits=[mm2], inc=True)
        st.update(c1=mm1, c2=mm2)
        return t4, (e1, e2)

    def ln_phase(src_rows_ap, n_tiles, dst_fm, dst_cols_base, sub, after_tile=None):
        xt = sub["xt"]
        h1t = sub["h1t"]
        nmr = sub["nmr"]
        nmean = sub["nmean"]
        varsum = sub["varsum"]
        stdv = sub["stdv"]
        rstd = sub["rstd"]
        pA = sub["pA"]
        toks = sub.setdefault("toks", {})
        for i in range(n_tiles):
            bs = i % 2
            src = src_rows_ap(i)
            prev = toks.get(("free", bs))
            waits = prev if prev else []
            tl = P.dma("sync", xt[bs][:], src, waits=waits)
            t1 = P.emit("vector", "reduce_sum", nmr[:, bs:bs + 1], xt[bs][:],
                        axis=AX.X, negate=True, waits=[tl], inc=True)
            t2 = P.emit("vector", "tensor_scalar_mul", nmean[:, bs:bs + 1],
                        nmr[:, bs:bs + 1], 1.0 / D_MODEL, waits=[t1], inc=True)
            wsq = [t2]
            if ("sqfree", bs) in toks:
                wsq.append(toks[("sqfree", bs)])
            t3 = P.emit("scalar", "activation", h1t[bs][:, 0:4096], xt[bs][:],
                        AF.Square, bias=nmean[:, bs:bs + 1], scale=1.0,
                        accum_out=varsum[:, bs:bs + 1], waits=wsq, inc=True)
            t4 = P.emit("scalar", "activation", stdv[:, bs:bs + 1],
                        varsum[:, bs:bs + 1], AF.Sqrt,
                        bias=epscol[:, 0:1], scale=1.0 / D_MODEL,
                        waits=[t3], inc=True)
            t5 = P.emit("vector", "reciprocal", rstd[:, bs:bs + 1],
                        stdv[:, bs:bs + 1], waits=[t4], inc=True)
            t6 = P.emit("vector", "tensor_scalar", h1t[bs][:, 0:4096], xt[bs][:],
                        nmean[:, bs:bs + 1], rstd[:, bs:bs + 1],
                        ALU.add, ALU.mult, waits=[t5, t3], inc=True)
            toks[("free", bs)] = [t6, t3]
            for cc in range(4):
                pa = pA[cc % 2]
                wtp = [t6]
                if ("pafree", cc % 2) in toks:
                    wtp.append(toks[("pafree", cc % 2)])
                tp = None
                for c8 in range(8):
                    c = cc * 8 + c8
                    tp = P.emit("tensor", "transpose",
                                pa[:, c8 * 128:(c8 + 1) * 128],
                                h1t[bs][:, c * 128:(c + 1) * 128], idn[:],
                                waits=(wtp + [t_idn]) if c8 == 0 else (),
                                inc=True if c8 == 7 else None)
                col = dst_cols_base + i * 128
                te = P.emit("vector", "tensor_copy",
                            dst_fm[:, cc * 8:(cc + 1) * 8, col:col + 128],
                            pa[:].rearrange("p (c m) -> p c m", c=8),
                            waits=[tp], inc=True)
                toks[("pafree", cc % 2)] = te
                toks[("evac", i, cc)] = te
            toks[("sqfree", bs)] = toks[("evac", i, 3)]
            if after_tile is not None:
                after_tile(i, toks)
        return toks

    kh_w_tok = {}
    v_w_tok = {}
    last_b_tokens = []

    for h in range(2):
        ctxA = ExitStack()
        subA = {
            "xt": [ctxA.enter_context(nc.sbuf_tensor(f"xtA{h}_{i}", [128, 4096], F32)) for i in range(2)],
            "h1t": [ctxA.enter_context(nc.sbuf_tensor(f"h1tA{h}_{i}", [128, 4096], BF16)) for i in range(2)],
            "nmr": ctxA.enter_context(nc.sbuf_tensor(f"nmrA{h}", [128, 2], F32)),
            "nmean": ctxA.enter_context(nc.sbuf_tensor(f"nmeanA{h}", [128, 2], F32)),
            "varsum": ctxA.enter_context(nc.sbuf_tensor(f"varsumA{h}", [128, 2], F32)),
            "stdv": ctxA.enter_context(nc.sbuf_tensor(f"stdvA{h}", [128, 2], F32)),
            "rstd": ctxA.enter_context(nc.sbuf_tensor(f"rstdA{h}", [128, 2], F32)),
            "pA": [ctxA.enter_context(nc.psum_tensor(f"pAA{h}_{i}", [128, 1024], BF16)) for i in range(2)],
        }
        toksA = ln_phase(lambda i: xh[h, i * 128:(i + 1) * 128, :], 8,
                         h1_fm, 0, subA)
        if h == 0:
            w = [toksA[("evac", i, cc)] for i in range(4) for cc in range(4)]
            P.dma("sync", h1own_scr.rearrange("p (c m) -> p c m", c=NC_),
                  h1_fm[:, :, 0:512], waits=w)
        P.barrier()
        ctxA.close()

        ctxB = ExitStack()
        wkb = [ctxB.enter_context(nc.sbuf_tensor(f"wkb{h}_{i}", [128, NC_, 128], BF16)) for i in range(2)]
        wvb = [ctxB.enter_context(nc.sbuf_tensor(f"wvb{h}_{i}", [128, NC_, 256], BF16)) for i in range(2)]
        ksq = ctxB.enter_context(nc.sbuf_tensor(f"ksq{h}", [128, 512], BF16))
        pB = [ctxB.enter_context(nc.psum_tensor(f"pB{h}_{i}", [128, 512], F32)) for i in range(2)]
        pK2 = ctxB.enter_context(nc.psum_tensor(f"pK2{h}", [128, 512], F32))
        pRC = ctxB.enter_context(nc.psum_tensor(f"pRC{h}", [128, 1024], F32))

        t_rk = P.dma("sync", ropek_sb[:], ropek[h].rearrange("c p t -> p c t"))

        if h == 0:
            P.emit("vector", "memset", mk2row[:], 0.0, inc=True)

        kl_tok = {}
        for kh in range(NKV):
            wb = wkb[kh % 2]
            wfree = kl_tok.get(("wfree", kh % 2))
            tw = P.dma("gpsimd", wb[:],
                       wqkv_bf[:, 4096 + kh * 128:4096 + (kh + 1) * 128]
                       .rearrange("(c p) m -> p c m", p=128),
                       waits=[t_pckv] + ([wfree] if wfree else []))
            for g in range(2):
                pb = pB[(kh * 2 + g) % 2]
                pfree = kl_tok.get(("pfree", (kh * 2 + g) % 2))
                mm = None
                for c in range(NC_):
                    mm = P.emit("tensor", "matmul", pb[:],
                                wb[:, c, :], h1_fm[:, c, g * 512:(g + 1) * 512],
                                start=(c == 0), stop=(c == NC_ - 1),
                                waits=([tw] + ([pfree] if pfree else [])) if c == 0 else (),
                                inc=True if c == NC_ - 1 else None)
                kl_tok[("wfree", kh % 2)] = mm
                if h == 0:
                    precast_q(kh * 2 + g)
                csA = ropek_sb[:, 0, g * 512:(g + 1) * 512]
                csB = ropek_sb[:, 1, g * 512:(g + 1) * 512]
                col = h * 1024 + g * 512
                kc = kfm[:, kh, col:col + 512]
                kev = kfm[0:64, kh, col:col + 512]
                kod = kfm[64:128, kh, col:col + 512]
                rcfree = kl_tok.get(("rcfree",), (None, None))
                t4, (e1, e2) = rope_emit(pb, csA, csB, kev, kod, [mm, t_rk],
                                         pRC[:, 0:512], pRC[:, 512:1024],
                                         [rcfree[0]], [rcfree[1]])
                kl_tok[("pfree", (kh * 2 + g) % 2)] = t4
                kl_tok[("rcfree",)] = (e1, e2)
                t7 = P.emit("vector", "tensor_mul", ksq[:], kc, kc,
                            waits=[e1, e2], inc=True)
                pk2free = kl_tok.get(("pk2free",))
                t8 = P.emit("tensor", "matmul", pK2[0:1, :], onesb[:, 0:1], ksq[:],
                            start=True, stop=True,
                            waits=[t7, t_ones] + ([pk2free] if pk2free else []),
                            inc=True)
                t9 = P.emit("vector", "reduce_max", brow[0:1, 0:1], pK2[0:1, :],
                            axis=AX.X, waits=[t8], inc=True)
                t10 = P.emit("vector", "tensor_max", mk2row[0:1, kh:kh + 1],
                             mk2row[0:1, kh:kh + 1], brow[0:1, 0:1], waits=[t9], inc=True)
                kl_tok[("k2", kh, g)] = t10
                kl_tok[("pk2free",)] = t9

        for g in range(4):
            wb = wvb[g % 2]
            wfree = v_w_tok.get(("wfree", h, g % 2))
            tw = P.dma("gpsimd", wb[:],
                       wqkv_bf[:, 5120 + g * 256:5120 + (g + 1) * 256]
                       .rearrange("(c p) m -> p c m", p=128),
                       waits=[t_pckv] + ([wfree] if wfree else []))
            for m in range(8):
                pb = pB[(g * 8 + m) % 2]
                pfree = v_w_tok.get(("pfree", h, (g * 8 + m) % 2))
                mm = None
                for c in range(NC_):
                    mm = P.emit("tensor", "matmul", pb[:, 0:256],
                                h1_fm[:, c, m * 128:(m + 1) * 128],
                                wb[:, c, :],
                                start=(c == 0), stop=(c == NC_ - 1),
                                waits=([tw] + ([pfree] if pfree else [])) if c == 0 else (),
                                inc=True if c == NC_ - 1 else None)
                v_w_tok[("wfree", h, g % 2)] = mm
                te = P.emit("vector", "tensor_copy",
                            vtok[:, h * 8 + m, g * 256:(g + 1) * 256],
                            pb[:, 0:256], waits=[mm], inc=True)
                v_w_tok[("pfree", h, (g * 8 + m) % 2)] = te
                last_b_tokens.append(te)
        P.barrier()
        ctxB.close()

    ctxC = ExitStack()

    def sbC(name, shape, dt):
        return ctxC.enter_context(nc.sbuf_tensor(name, shape, dt))

    masks_sb = sbC("masks_sb", [128, 2, 8, 512], BF16)
    ropeq_sb = sbC("ropeq_sb", [128, 2, 2, 512], F32)
    wqb = [sbC(f"wqb{i}", [128, NC_, 128], BF16) for i in range(2)]
    qbuf = [sbC(f"qbuf{i}", [128, 1024], BF16) for i in range(2)]
    qsq = sbC("qsq", [128, 1024], BF16)
    NPS = 4
    probs = [sbC(f"probs{i}", [128, 512], BF16) for i in range(NPS)]
    rb = sbC("rb", [128, 512], F32)
    atout = [sbC(f"atout{i}", [128, 512], BF16) for i in range(2)]
    psco = [ctxC.enter_context(nc.psum_tensor(f"psco{i}", [128, 512], F32)) for i in range(NPS)]
    pat = ctxC.enter_context(nc.psum_tensor("pat", [128, 512], F32))
    prs = ctxC.enter_context(nc.psum_tensor("prs", [128, 512], F32))
    pQ = [ctxC.enter_context(nc.psum_tensor(f"pQ{i}", [128, 512], F32)) for i in range(2)]

    t_mask = P.dma("sync", masks_sb[:], masks.rearrange("s t p m -> p s t m"))
    t_rq = P.dma("sync", ropeq_sb[:], ropeq.rearrange("s c p m -> p s c m"))
    t_h1o = P.dma("sync", h1_fm[:, :, 512:1024],
                  h1own_scr.rearrange("p (c m) -> p c m", c=NC_))

    ct = {}

    def emit_qprod(hd):
        kh = hd // 4
        wb = wqb[hd % 2]
        qb = qbuf[hd % 2]
        wfree = ct.get(("wqfree", hd % 2))
        tw = P.dma("gpsimd", wb[:],
                   wqkv_bf[:, hd * 128:(hd + 1) * 128]
                   .rearrange("(c p) m -> p c m", p=128),
                   waits=[t_pcq] + ([wfree] if wfree else []))
        precast_wowf(hd)
        yield
        qg_tok = []
        ropetoks = []
        for g in range(2):
            pq = pQ[g]
            pqfree = ct.get(("pqfree", g))
            pcA = pQ[1 - g]
            pcA_free = ct.get(("pqA_free", 1 - g))
            mm = None
            for cc in range(4):
                for c8 in range(8):
                    c = cc * 8 + c8
                    rhs = h1_fm[:, c, 512:1024] if g == 0 else h1_fm[:, c, 0:512]
                    mm = P.emit("tensor", "matmul", pq[:], wb[:, c, :], rhs,
                                start=(c == 0), stop=(c == NC_ - 1),
                                waits=([tw, t_h1o] + ([pqfree] if pqfree else [])) if c == 0 else (),
                                inc=True if c == NC_ - 1 else None)
                yield
            ct[("wqfree", hd % 2)] = mm
            csA = ropeq_sb[:, g, 0, :]
            csB = ropeq_sb[:, g, 1, :]
            qev = qb[0:64, g * 512:(g + 1) * 512]
            qod = qb[64:128, g * 512:(g + 1) * 512]
            qfree = ct.get(("qfree", hd % 2))
            w0 = [mm, t_rq] + ([qfree] if qfree and g == 0 else [])
            t4, (e1, e2) = rope_emit(pq, csA, csB, qev, qod, w0,
                                     pcA, pq, [pcA_free], [])
            yield
            ropetoks.append(t4)
            qg_tok += [e1, e2]
            if g == 0:
                ct[("pqfree", 1)] = e1
            ct[("eA", g)] = e1
            ct[("eB", g)] = e2
            ct[("qready", hd, g)] = (e1, e2)

        tq1 = P.emit("vector", "tensor_mul", qsq[:], qb[:], qb[:],
                     waits=qg_tok, inc=True)
        ta = P.emit("tensor", "matmul", pQ[0][0:1, :], onesb[:, 0:1], qsq[:, 0:512],
                    start=True, stop=True, waits=[tq1, ct[("eA", 1)]], inc=True)
        tb = P.emit("tensor", "matmul", pQ[1][0:1, :], onesb[:, 0:1], qsq[:, 512:1024],
                    start=True, stop=True, waits=[tq1, ct[("eB", 1)]], inc=True)
        yield
        tm1 = P.emit("vector", "reduce_max", brow[0:1, 0:1], pQ[0][0:1, :],
                     axis=AX.X, waits=[ta], inc=True)
        tm2 = P.emit("vector", "reduce_max", brow[0:1, 1:2], pQ[1][0:1, :],
                     axis=AX.X, waits=[tb], inc=True)
        tm3 = P.emit("vector", "tensor_max", brow[0:1, 0:1], brow[0:1, 0:1],
                     brow[0:1, 1:2], waits=[tm2], inc=True)
        tm4 = P.emit("vector", "tensor_scalar", brow[0:1, 2:3], brow[0:1, 0:1],
                     mk2row[0:1, kh:kh + 1], -0.5, ALU.add, ALU.mult,
                     waits=[tm3], inc=True)
        tm6 = P.emit("vector", "tensor_scalar_max", brow[0:1, 2:3], brow[0:1, 2:3],
                     -80.0, waits=[tm4], inc=True)
        tbc = P.emit("tensor", "matmul", pQ[1][:, 0:1], onesf[0:1, 0:128],
                     brow[0:1, 2:3], start=True, stop=True,
                     waits=[tm6, t_onesf, tm2], inc=True)
        tbc2 = P.emit("vector", "tensor_copy", biascol[:, hd:hd + 1],
                      pQ[1][:, 0:1], waits=[tbc], inc=True)
        ct[("pqfree", 0)] = tm1
        ct[("pqA_free", 1)] = tbc2
        ct[("tbc2", hd)] = tbc2

    def emit_scores(hd, s, t, qb):
        kh = hd // 4
        pc = psco[t % NPS]
        pcfree = ct.get(("pscofree", t % NPS))
        tbc2 = ct[("tbc2", hd)]
        w0 = [tbc2] + list(ct[("qready", hd, s)])
        if pcfree is not None:
            w0.append(pcfree)
        return P.emit("tensor", "matmul", pc[:],
                      kfm[:, kh, t * 128:(t + 1) * 128],
                      qb[:, s * 512:(s + 1) * 512],
                      start=True, stop=True, waits=w0, inc=True)

    def emit_attention(hd, feed=None):
        kh = hd // 4
        qb = qbuf[hd % 2]
        LOOK = 3
        vlast = None
        for s in range(2):
            D = DS[s]
            sc_tok = {}
            for t in range(min(LOOK + 1, D)):
                sc_tok[t] = emit_scores(hd, s, t, qb)
            rs_tok = None
            for t in range(D):
                if feed is not None:
                    next(feed, None)
                pc = psco[t % NPS]
                pbuf = probs[t % NPS]
                pbfree = ct.get(("probsfree", t % NPS))
                wexp = [sc_tok[t]]
                if pbfree is not None:
                    wexp.append(pbfree)
                te = P.emit("scalar", "activation", pbuf[:], pc[:], AF.Exp,
                            bias=biascol[:, hd:hd + 1], scale=1.0,
                            waits=wexp, inc=True)
                ct[("pscofree", t % NPS)] = te
                if t + LOOK + 1 < D:
                    sc_tok[t + LOOK + 1] = emit_scores(hd, s, t + LOOK + 1, qb)
                masked = (s == 0) or (t >= 8)
                tok_ready = te
                if masked:
                    moff = 0 if s == 0 else 8
                    wm = [te, t_mask]
                    if pbfree is not None:
                        wm.append(pbfree)
                    tok_ready = P.emit("vector", "tensor_mul", pbuf[:], pbuf[:],
                                       masks_sb[:, s, t - moff, :],
                                       waits=wm, inc=True)
                w0 = [tok_ready, te]
                if t == 0:
                    if ("patfree2",) in ct:
                        w0.append(ct[("patfree2",)])
                    if ("prsfree2",) in ct:
                        w0.append(ct[("prsfree2",)])
                vlast = P.emit("tensor", "matmul", pat[:],
                               vtok[:, t, kh * 128:(kh + 1) * 128],
                               pbuf[:],
                               start=(t == 0), stop=(t == D - 1),
                               waits=w0,
                               inc=True if t == D - 1 else None)
                rs_tok = P.emit("tensor", "matmul", prs[:],
                                onesb[:], pbuf[:],
                                start=(t == 0), stop=(t == D - 1),
                                inc=True)
                ct[("probsfree", t % NPS)] = rs_tok
            tr = P.emit("vector", "reciprocal", rrow[0:1, :], prs[0:1, :],
                        waits=[rs_tok], inc=True)
            ct[("prsfree2",)] = tr
            bb = D % NPS
            tb2 = P.emit("tensor", "matmul", psco[bb][:], onesf[0:1, 0:128],
                         rrow[0:1, :], start=True, stop=True,
                         waits=[tr, ct[("pscofree", bb)]], inc=True)
            tc = P.emit("vector", "tensor_copy", rb[:], psco[bb][:],
                        waits=[tb2], inc=True)
            ct[("pscofree", bb)] = tc
            ao = atout[s]
            aofree = ct.get(("aofree", s))
            tm = P.emit("vector", "tensor_mul", ao[:], pat[:], rb[:],
                        waits=[tc, vlast] + ([aofree] if aofree else []), inc=True)
            ct[("patfree2",)] = tm
            ts_ = P.dma("sync", attn_scr[hd, :, s * 512:(s + 1) * 512], ao[:],
                        waits=[tm])
            ct[("aofree", s)] = ts_
        ct[("qfree", hd % 2)] = vlast

    gen = emit_qprod(0)
    for _ in gen:
        pass
    for hd in range(NH):
        feed = emit_qprod(hd + 1) if hd + 1 < NH else None
        emit_attention(hd, feed)
        if feed is not None:
            for _ in feed:
                pass
    P.barrier()
    ctxC.close()
    ctx1.close()

    ctxD = ExitStack()

    def sbD(name, shape, dt):
        return ctxD.enter_context(nc.sbuf_tensor(name, shape, dt))

    wob = [sbD(f"wob{i}", [128, NC_, 512], BF16) for i in range(3)]
    atb = [sbD(f"atb{i}", [128, NC_, 512], BF16) for i in range(2)]
    xrb = [sbD(f"xrb{i}", [128, 512], F32) for i in range(2)]
    x1b = [sbD(f"x1b{i}", [128, 512], F32) for i in range(2)]
    pD = [ctxD.enter_context(nc.psum_tensor(f"pD{i}", [128, 512], F32)) for i in range(2)]

    dt_ = {}
    it = 0
    tal = [P.dma("sync", atb[hf][:],
                 attn_scr[:, :, hf * 512:(hf + 1) * 512]
                 .rearrange("c p m -> p c m")) for hf in range(2)]
    for n in range(8):
        wb = wob[n % 3]
        wfree = dt_.get(("wofree", n % 3))
        tws = [P.dma("gpsimd", wb[:],
                     wo_bf[:, n * 512:(n + 1) * 512]
                     .rearrange("(c p) m -> p c m", p=128),
                     waits=[t_pcwo] + ([wfree] if wfree else []))]
        for hf in range(2):
            ab = atb[hf]
            for ml in range(4):
                m = hf * 4 + ml
                pb = pD[it % 2]
                pfree = dt_.get(("pDfree", it % 2))
                mm = None
                for c in range(NC_):
                    w0 = []
                    if c == 0:
                        w0 = [tal[hf], tws[-1]] + ([pfree] if pfree else [])
                    mm = P.emit("tensor", "matmul", pb[:],
                                ab[:, c, ml * 128:(ml + 1) * 128],
                                wb[:, c, :],
                                start=(c == 0), stop=(c == NC_ - 1),
                                waits=w0, inc=True if c == NC_ - 1 else None)
                dt_[("wofree", n % 3)] = mm
                xb_ = xrb[it % 2]
                xfree = dt_.get(("xrfree", it % 2))
                tx = P.dma("sync", xb_[:],
                           xh[hf, ml * 128:(ml + 1) * 128, n * 512:(n + 1) * 512],
                           waits=[xfree] if xfree else [])
                ob = x1b[it % 2]
                ofree = dt_.get(("x1free", it % 2))
                tadd = P.emit("vector", "tensor_add", ob[:], pb[:], xb_[:],
                              waits=[mm, tx] + ([ofree] if ofree else []), inc=True)
                dt_[("pDfree", it % 2)] = tadd
                dt_[("xrfree", it % 2)] = tadd
                tst = P.dma("sync", x1_scr[m * 128:(m + 1) * 128, n * 512:(n + 1) * 512],
                            ob[:], waits=[tadd])
                dt_[("x1free", it % 2)] = tst
                it += 1
    P.barrier()
    ctxD.close()

    ctxE = ExitStack()

    def sbE(name, shape, dt):
        return ctxE.enter_context(nc.sbuf_tensor(name, shape, dt))

    h2_fm = sbE("h2_fm", [128, NC_, 1024], BF16)
    subE = {
        "xt": [sbE(f"xtE{i}", [128, 4096], F32) for i in range(2)],
        "h1t": [sbE(f"h1tE{i}", [128, 4096], BF16) for i in range(2)],
        "nmr": sbE("nmrE", [128, 2], F32),
        "nmean": sbE("nmeanE", [128, 2], F32),
        "varsum": sbE("varsumE", [128, 2], F32),
        "stdv": sbE("stdvE", [128, 2], F32),
        "rstd": sbE("rstdE", [128, 2], F32),
        "pA": [ctxE.enter_context(nc.psum_tensor(f"pAE{i}", [128, 1024], BF16)) for i in range(2)],
    }
    wfb = [sbE(f"wfb{i}", [128, NC_, 512], BF16) for i in range(2)]
    biasb = sbE("biasb", [128, 4096], BF16)
    x1r = [sbE(f"x1r{i}", [128, 512], F32) for i in range(2)]
    outb = [sbE(f"outb{i}", [128, 512], F32) for i in range(2)]
    pE = [ctxE.enter_context(nc.psum_tensor(f"pE{i}", [128, 512], F32)) for i in range(2)]

    t_bias = P.dma("gpsimd", biasb[:], bias8)
    et = {"it": 0}

    def load_wf(n):
        wfree = et.get(("wffree", n % 2))
        return P.dma("gpsimd", wfb[n % 2][:],
                     wf_bf[:, n * 512:(n + 1) * 512]
                     .rearrange("(c p) m -> p c m", p=128),
                     waits=[t_pcwf] + ([wfree] if wfree else []))

    def emit_e2(n, m, tw, toks):
        it = et["it"]
        wb = wfb[n % 2]
        pb = pE[it % 2]
        pfree = et.get(("pEfree", it % 2))
        mm = None
        for c in range(NC_):
            w0 = []
            if c == 0:
                w0 = [tw]
                if pfree:
                    w0.append(pfree)
                w0 += [toks[("evac", m, cc)] for cc in range(4)]
            mm = P.emit("tensor", "matmul", pb[:],
                        h2_fm[:, c, m * 128:(m + 1) * 128],
                        wb[:, c, :],
                        start=(c == 0), stop=False,
                        waits=w0, inc=None)
        mm = P.emit("tensor", "matmul", pb[:], onesb[0:1, :],
                    biasb[0:1, n * 512:(n + 1) * 512],
                    start=False, stop=True, waits=[t_bias], inc=True)
        et[("wffree", n % 2)] = mm
        xb_ = x1r[it % 2]
        xfree = et.get(("x1rfree", it % 2))
        tx = P.dma("sync", xb_[:],
                   x1_scr[m * 128:(m + 1) * 128, n * 512:(n + 1) * 512],
                   waits=[xfree] if xfree else [])
        ob = outb[it % 2]
        ofree = et.get(("outfree", it % 2))
        tst2 = P.emit("vector", "scalar_tensor_tensor", ob[:], pb[:], 0.01,
                      xb_[:], ALU.mult, ALU.add,
                      waits=[mm, tx] + ([ofree] if ofree else []), inc=True)
        et[("pEfree", it % 2)] = tst2
        et[("x1rfree", it % 2)] = tst2
        tso = P.dma("sync", out_sh[m * 128:(m + 1) * 128, n * 512:(n + 1) * 512],
                    ob[:], waits=[tst2])
        et[("outfree", it % 2)] = tso
        et["it"] = it + 1

    tw0 = load_wf(0)
    toksE = ln_phase(lambda i: x1_scr[i * 128:(i + 1) * 128, :], 8, h2_fm, 0,
                     subE, after_tile=lambda i, toks: emit_e2(0, i, tw0, toks))
    for n in range(1, 8):
        tw = load_wf(n)
        for m in range(8):
            emit_e2(n, m, tw, toksE)

    P.emit("sync", "wait_ge_final")

    for s in P.sems.values():
        s.handle = nc.alloc_semaphore(name=s.name)

    final_waits = [(P.sems[n], P.sems[n].count) for n in sorted(P.store_sems)]

    with nc.Block() as block:
        @block.sync
        def _(sync):
            for op in P.ops["sync"]:
                if op.method == "wait_ge_final":
                    for (s, v) in final_waits:
                        sync.wait_ge(s.handle, v)
                    continue
                for (s, v) in op.waits:
                    sync.wait_ge(s.handle, v)
                if op.method == "wait_ge":
                    continue
                inst = getattr(sync, op.method)(*op.args, **op.kwargs)
                for (s, nn) in op.incs:
                    inst.then_inc(s.handle, nn)

        @block.gpsimd
        def _(gpsimd):
            _replay(P, "gpsimd", gpsimd)

        @block.tensor
        def _(tensor):
            _replay(P, "tensor", tensor)

        @block.vector
        def _(vector):
            _replay(P, "vector", vector)

        @block.scalar
        def _(scalar):
            _replay(P, "scalar", scalar)

    ctxE.close()
    ctx.close()
    return nc



def _perm_cols():
    perm = []
    for hd in range(NH):
        base = hd * HD
        perm += [base + 2 * j for j in range(64)] + [base + 2 * j + 1 for j in range(64)]
    for kh in range(NKV):
        base = 4096 + kh * HD
        perm += [base + 2 * j for j in range(64)] + [base + 2 * j + 1 for j in range(64)]
    perm += list(range(5120, 6144))
    return np.array(perm, dtype=np.int64)


def _core_meta(p):
    own = [0, 3] if p == 0 else [1, 2]
    other = [1, 2] if p == 0 else [0, 3]
    return own, other


def _rope_tables(pos):
    j = np.arange(64, dtype=np.float64)
    inv = 1.0 / (10000.0 ** (j / 64.0))
    ang = pos[:, None].astype(np.float64) * inv[None, :]
    cos = (np.cos(ang).T * S4).astype(np.float32)
    sin = (np.sin(ang).T * S4).astype(np.float32)
    csA = np.concatenate([cos, sin], axis=0)
    csB = np.concatenate([sin, cos], axis=0)
    return np.stack([csA, csB])


def kernel(L0_attn_x, L0_attn_w_qkv, L0_attn_w_o, L0_ffn_quantized_weight,
           L0_ffn_quantized_bias):
    x = np.asarray(L0_attn_x, dtype=np.float32)
    wqkv = np.ascontiguousarray(np.asarray(L0_attn_w_qkv, np.float32)[:, _perm_cols()])
    wo = np.ascontiguousarray(np.asarray(L0_attn_w_o, np.float32))
    wf = np.ascontiguousarray(np.asarray(L0_ffn_quantized_weight, np.int8).T)
    bias8 = np.ascontiguousarray(
        np.broadcast_to(np.asarray(L0_ffn_quantized_bias, np.int8).reshape(1, 4096),
                        (128, 4096)))
    rcomb = np.zeros((128, 128), np.float32)
    for j in range(64):
        rcomb[j, j] = 1.0
        rcomb[j + 64, j] = -1.0
        rcomb[j, j + 64] = 1.0
        rcomb[j + 64, j + 64] = 1.0
    cid = np.stack([np.eye(128), np.ones((128, 128)), rcomb]).astype(ml_dtypes.bfloat16)
    onesf = np.ones((128, 128), np.float32)

    nc = build_nc()
    in_maps = []
    metas = []
    for c in range(8):
        b, p = c >> 1, c & 1
        own, other = _core_meta(p)
        xb = x[b]
        xh = np.stack([
            np.concatenate([xb[own[h] * 512:(own[h] + 1) * 512],
                            xb[other[h] * 512:(other[h] + 1) * 512]])
            for h in range(2)])
        posk = np.concatenate([
            own[0] * 512 + np.arange(512), other[0] * 512 + np.arange(512),
            own[1] * 512 + np.arange(512), other[1] * 512 + np.arange(512)])
        masks = np.zeros((2, 8, 128, 512), np.float32)
        for s in range(2):
            posq = own[s] * 512 + np.arange(512)
            t0 = 0 if s == 0 else 8
            for jr in range(8):
                t = t0 + jr
                masks[s, jr] = (posk[t * 128:(t + 1) * 128, None] <= posq[None, :])
        ropek = np.stack([_rope_tables(posk[:1024]), _rope_tables(posk[1024:])])
        ropeq = np.stack([_rope_tables(own[s] * 512 + np.arange(512)) for s in range(2)])
        in_maps.append({
            "xh": np.ascontiguousarray(xh),
            "wqkv": wqkv, "wo": wo, "wf": wf, "bias8": bias8,
            "masks": masks.astype(ml_dtypes.bfloat16),
            "ropek": np.ascontiguousarray(ropek),
            "ropeq": np.ascontiguousarray(ropeq),
            "cid": cid, "onesf": onesf,
        })
        metas.append((b, own))

    res = run_bass_kernel_spmd(nc, in_maps, core_ids=list(range(8)))
    out = np.empty((4, SEQ, D_MODEL), np.float32)
    for c in range(8):
        b, own = metas[c]
        sh = res.results[c]["out_sh"]
        out[b, own[0] * 512:(own[0] + 1) * 512] = sh[0:512]
        out[b, own[1] * 512:(own[1] + 1) * 512] = sh[512:1024]
    return out


# version 18
# speedup vs baseline: 1.1806x; 1.1806x over previous
import numpy as np
import ml_dtypes

import concourse.bass as bass
import concourse.mybir as mybir
from concourse.bass_utils import run_bass_kernel_spmd
from contextlib import ExitStack
from concourse.bass_primitives import MemorySpace

F32 = mybir.dt.float32
BF16 = mybir.dt.bfloat16
I8 = mybir.dt.int8
AF = mybir.ActivationFunctionType
ALU = mybir.AluOpType
AX = mybir.AxisListType

D_MODEL = 4096
SEQ = 2048
NH = 32
NKV = 8
HD = 128
NC_ = 32
DS = (8, 16)
S4 = float(128.0 ** -0.25)
EPS = 1e-5

ENGS = ("sync", "gpsimd", "tensor", "vector", "scalar")


class Sem:
    __slots__ = ("name", "count", "handle")

    def __init__(self, name):
        self.name = name
        self.count = 0
        self.handle = None


class Op:
    __slots__ = ("method", "args", "kwargs", "waits", "incs")

    def __init__(self, method, args, kwargs, waits, incs):
        self.method = method
        self.args = args
        self.kwargs = kwargs
        self.waits = waits
        self.incs = incs


class Plan:
    def __init__(self):
        self.ops = {e: [] for e in ENGS}
        self.sems = {}
        self.wm = {e: {} for e in ENGS}
        self.psem = {
            "sync": self.sem("sLD"),
            "gpsimd": self.sem("sGP"),
            "tensor": self.sem("sPE"),
            "vector": self.sem("sDV"),
            "scalar": self.sem("sAC"),
        }
        self.sst = self.sem("sST")
        self.store_sems = set()

    def sem(self, name):
        if name not in self.sems:
            self.sems[name] = Sem(name)
        return self.sems[name]

    def emit(self, eng, method, *args, waits=(), inc=None, inc_n=1, **kwargs):
        wl = []
        wm = self.wm[eng]
        for wv in waits:
            if wv is None:
                continue
            s, v = wv
            if wm.get(s.name, 0) >= v:
                continue
            wm[s.name] = v
            wl.append((s, v))
        il = []
        tok = None
        if inc is not None:
            s = self.psem[eng] if inc is True else inc
            s.count += inc_n
            il.append((s, inc_n))
            tok = (s, s.count)
        self.ops[eng].append(Op(method, args, kwargs, wl, il))
        return tok

    def dma(self, eng, out, in_, waits=(), sem=None, **kw):
        if sem is None:
            is_store = out.space != MemorySpace.SBUF
            name = (in_ if is_store else out).tensor.name
            sem = self.sem("b_" + name)
            if is_store:
                self.store_sems.add(sem.name)
        return self.emit(eng, "dma_start", waits=waits, inc=sem, inc_n=16,
                         out=out, in_=in_, **kw)

    def barrier(self):
        for e in ENGS:
            ops = self.ops[e]
            last = None
            for op in reversed(ops):
                if op.method not in ("wait_ge", "wait_ge_final"):
                    last = op
                    break
            if last is None:
                continue
            if not last.incs:
                s = self.psem[e]
                s.count += 1
                last.incs.append((s, 1))
        targets = [(s, s.count) for s in self.sems.values() if s.count > 0]
        for e in ENGS:
            wm = self.wm[e]
            for (s, v) in targets:
                if v > 0 and wm.get(s.name, 0) < v:
                    wm[s.name] = v
                    self.ops[e].append(Op("wait_ge", (), {}, [(s, v)], []))


def _replay(plan, eng_name, eng):
    for op in plan.ops[eng_name]:
        for (s, v) in op.waits:
            eng.wait_ge(s.handle, v)
        if op.method == "wait_ge":
            continue
        inst = getattr(eng, op.method)(*op.args, **op.kwargs)
        for (s, n) in op.incs:
            inst.then_inc(s.handle, n)


def build_nc():
    nc = bass.Bass("TRN2", target_bir_lowering=False)
    P = Plan()
    sLD = P.psem["sync"]
    sGP = P.psem["gpsimd"]
    sPE = P.psem["tensor"]
    sDV = P.psem["vector"]
    sAC = P.psem["scalar"]
    sST = P.sst

    xh = nc.dram_tensor("xh", [2, 1024, D_MODEL], F32, kind="ExternalInput").ap()
    wqkv = nc.dram_tensor("wqkv", [D_MODEL, 6144], F32, kind="ExternalInput").ap()
    wo = nc.dram_tensor("wo", [D_MODEL, D_MODEL], F32, kind="ExternalInput").ap()
    wf = nc.dram_tensor("wf", [D_MODEL, D_MODEL], I8, kind="ExternalInput").ap()
    bias8 = nc.dram_tensor("bias8", [128, D_MODEL], I8, kind="ExternalInput").ap()
    masks = nc.dram_tensor("masks", [2, 8, 128, 512], BF16, kind="ExternalInput").ap()
    ropek = nc.dram_tensor("ropek", [2, 2, 128, 1024], F32, kind="ExternalInput").ap()
    ropeq = nc.dram_tensor("ropeq", [2, 2, 128, 512], F32, kind="ExternalInput").ap()
    cid = nc.dram_tensor("cid", [3, 128, 128], BF16, kind="ExternalInput").ap()
    onesf_d = nc.dram_tensor("onesf", [128, 128], F32, kind="ExternalInput").ap()
    out_sh = nc.dram_tensor("out_sh", [1024, D_MODEL], F32, kind="ExternalOutput").ap()
    attn_scr = nc.dram_tensor("attn_scr", [NH, 128, 1024], BF16, kind="Internal").ap()
    x1_scr = nc.dram_tensor("x1_scr", [1024, D_MODEL], F32, kind="Internal").ap()
    h1own_scr = nc.dram_tensor("h1own_scr", [128, NC_ * 512], BF16, kind="Internal").ap()
    wqkv_bf = nc.dram_tensor("wqkv_bf", [D_MODEL, 6144], BF16, kind="Internal").ap()
    wo_bf = nc.dram_tensor("wo_bf", [D_MODEL, D_MODEL], BF16, kind="Internal").ap()
    wf_bf = nc.dram_tensor("wf_bf", [D_MODEL, D_MODEL], BF16, kind="Internal").ap()

    ctx = ExitStack()

    def sb(name, shape, dt):
        return ctx.enter_context(nc.sbuf_tensor(name, shape, dt))

    idn = sb("idn", [128, 128], BF16)
    onesb = sb("onesb", [128, 128], BF16)
    onesf = sb("onesf_t", [128, 128], F32)
    rcomb = sb("rcomb", [128, 128], BF16)
    epscol = sb("epscol", [128, 1], F32)
    biascol = sb("biascol", [128, NH], F32)
    mk2row = sb("mk2row", [1, NKV], F32)
    rrow = sb("rrow", [1, 512], F32)
    brow = sb("brow", [1, 4], F32)

    s_pckv = P.sem("pc_kv")
    s_pcq = P.sem("pc_q")
    s_pcwo = P.sem("pc_wo")
    s_pcwf = P.sem("pc_wf")
    for i in range(2):
        P.dma("gpsimd", wqkv_bf[:, 4096 + i * 1024:4096 + (i + 1) * 1024],
              wqkv[:, 4096 + i * 1024:4096 + (i + 1) * 1024], sem=s_pckv)
    t_pckv = (s_pckv, s_pckv.count)
    t_pcq = (s_pcq, 16 * 16)
    t_pcwo = (s_pcwo, 16 * 16)
    t_pcwf = (s_pcwf, 16 * 16)

    def precast_q(i):
        P.dma("gpsimd", wqkv_bf[:, i * 256:(i + 1) * 256],
              wqkv[:, i * 256:(i + 1) * 256], sem=s_pcq)

    def precast_wowf(i):
        if i < 16:
            P.dma("gpsimd", wo_bf[:, i * 256:(i + 1) * 256],
                  wo[:, i * 256:(i + 1) * 256], sem=s_pcwo)
        elif i < 32:
            j = i - 16
            P.dma("gpsimd", wf_bf[:, j * 256:(j + 1) * 256],
                  wf[:, j * 256:(j + 1) * 256], sem=s_pcwf)

    t_idn = P.dma("sync", idn[:], cid[0])
    t_ones = P.dma("sync", onesb[:], cid[1])
    t_rcomb = P.dma("sync", rcomb[:], cid[2])
    t_onesf = P.dma("sync", onesf[:], onesf_d)
    P.emit("vector", "memset", epscol[:], EPS, inc=True)

    ctx1 = ExitStack()

    def sb1(name, shape, dt):
        return ctx1.enter_context(nc.sbuf_tensor(name, shape, dt))

    h1_fm = sb1("h1_fm", [128, NC_, 1024], BF16)
    kfm = sb1("kfm", [128, NKV, 2048], BF16)
    vtok = sb1("vtok", [128, 16, 1024], BF16)
    ropek_sb = sb1("ropek_sb", [128, 2, 1024], F32)
    tmpb = sb1("tmpb", [128, 2, 512], BF16)

    rope_st = {"c1": None, "c2": None}

    def rope_emit(pb, csA, csB, dst_ev, dst_od, first_waits, pcA, pcB, freeA, freeB):
        st = rope_st
        w1 = list(first_waits)
        if st["c1"]:
            w1.append(st["c1"])
        tA = P.emit("vector", "tensor_mul", tmpb[:, 0, :], pb[:], csA,
                    waits=w1, inc=True)
        w2 = [st["c2"]] if st["c2"] else []
        t4 = P.emit("vector", "tensor_mul", tmpb[:, 1, :], pb[:], csB,
                    waits=w2, inc=True)
        wc = [tA, t4, t_rcomb] + [t for t in freeA if t]
        mm1 = P.emit("tensor", "matmul", pcA[:], rcomb[:], tmpb[:, 0, :],
                     start=True, stop=True, waits=wc, inc=True)
        mm2 = P.emit("tensor", "matmul", pcB[:], rcomb[:], tmpb[:, 1, :],
                     start=True, stop=True,
                     waits=[t4] + [t for t in freeB if t], inc=True)
        e1 = P.emit("scalar", "copy", dst_ev, pcA[0:64, :],
                    waits=[mm1], inc=True)
        e2 = P.emit("scalar", "copy", dst_od, pcB[64:128, :],
                    waits=[mm2], inc=True)
        st.update(c1=mm1, c2=mm2)
        return t4, (e1, e2)

    def ln_phase(src_rows_ap, n_tiles, dst_fm, dst_cols_base, sub, after_tile=None):
        xt = sub["xt"]
        h1t = sub["h1t"]
        nmr = sub["nmr"]
        nmean = sub["nmean"]
        varsum = sub["varsum"]
        stdv = sub["stdv"]
        rstd = sub["rstd"]
        pA = sub["pA"]
        toks = sub.setdefault("toks", {})
        for i in range(n_tiles):
            bs = i % 2
            src = src_rows_ap(i)
            prev = toks.get(("free", bs))
            waits = prev if prev else []
            tl = P.dma("sync", xt[bs][:], src, waits=waits)
            t1 = P.emit("vector", "reduce_sum", nmr[:, bs:bs + 1], xt[bs][:],
                        axis=AX.X, negate=True, waits=[tl], inc=True)
            t2 = P.emit("vector", "tensor_scalar_mul", nmean[:, bs:bs + 1],
                        nmr[:, bs:bs + 1], 1.0 / D_MODEL, waits=[t1], inc=True)
            wsq = [t2]
            if ("sqfree", bs) in toks:
                wsq.append(toks[("sqfree", bs)])
            t3 = P.emit("scalar", "activation", h1t[bs][:, 0:4096], xt[bs][:],
                        AF.Square, bias=nmean[:, bs:bs + 1], scale=1.0,
                        accum_out=varsum[:, bs:bs + 1], waits=wsq, inc=True)
            t4 = P.emit("scalar", "activation", stdv[:, bs:bs + 1],
                        varsum[:, bs:bs + 1], AF.Sqrt,
                        bias=epscol[:, 0:1], scale=1.0 / D_MODEL,
                        waits=[t3], inc=True)
            t5 = P.emit("vector", "reciprocal", rstd[:, bs:bs + 1],
                        stdv[:, bs:bs + 1], waits=[t4], inc=True)
            t6 = P.emit("vector", "tensor_scalar", h1t[bs][:, 0:4096], xt[bs][:],
                        nmean[:, bs:bs + 1], rstd[:, bs:bs + 1],
                        ALU.add, ALU.mult, waits=[t5, t3], inc=True)
            toks[("free", bs)] = [t6, t3]
            for cc in range(4):
                pa = pA[cc % 2]
                wtp = [t6]
                if ("pafree", cc % 2) in toks:
                    wtp.append(toks[("pafree", cc % 2)])
                tp = None
                for c8 in range(8):
                    c = cc * 8 + c8
                    tp = P.emit("tensor", "transpose",
                                pa[:, c8 * 128:(c8 + 1) * 128],
                                h1t[bs][:, c * 128:(c + 1) * 128], idn[:],
                                waits=(wtp + [t_idn]) if c8 == 0 else (),
                                inc=True if c8 == 7 else None)
                col = dst_cols_base + i * 128
                te = P.emit("vector", "tensor_copy",
                            dst_fm[:, cc * 8:(cc + 1) * 8, col:col + 128],
                            pa[:].rearrange("p (c m) -> p c m", c=8),
                            waits=[tp], inc=True)
                toks[("pafree", cc % 2)] = te
                toks[("evac", i, cc)] = te
            toks[("sqfree", bs)] = toks[("evac", i, 3)]
            if after_tile is not None:
                after_tile(i, toks)
        return toks

    kh_w_tok = {}
    v_w_tok = {}
    last_b_tokens = []

    for h in range(2):
        ctxA = ExitStack()
        subA = {
            "xt": [ctxA.enter_context(nc.sbuf_tensor(f"xtA{h}_{i}", [128, 4096], F32)) for i in range(2)],
            "h1t": [ctxA.enter_context(nc.sbuf_tensor(f"h1tA{h}_{i}", [128, 4096], BF16)) for i in range(2)],
            "nmr": ctxA.enter_context(nc.sbuf_tensor(f"nmrA{h}", [128, 2], F32)),
            "nmean": ctxA.enter_context(nc.sbuf_tensor(f"nmeanA{h}", [128, 2], F32)),
            "varsum": ctxA.enter_context(nc.sbuf_tensor(f"varsumA{h}", [128, 2], F32)),
            "stdv": ctxA.enter_context(nc.sbuf_tensor(f"stdvA{h}", [128, 2], F32)),
            "rstd": ctxA.enter_context(nc.sbuf_tensor(f"rstdA{h}", [128, 2], F32)),
            "pA": [ctxA.enter_context(nc.psum_tensor(f"pAA{h}_{i}", [128, 1024], BF16)) for i in range(2)],
        }
        toksA = ln_phase(lambda i: xh[h, i * 128:(i + 1) * 128, :], 8,
                         h1_fm, 0, subA)
        if h == 0:
            w = [toksA[("evac", i, cc)] for i in range(4) for cc in range(4)]
            P.dma("sync", h1own_scr.rearrange("p (c m) -> p c m", c=NC_),
                  h1_fm[:, :, 0:512], waits=w)
        P.barrier()
        ctxA.close()

        ctxB = ExitStack()
        wkb = [ctxB.enter_context(nc.sbuf_tensor(f"wkb{h}_{i}", [128, NC_, 128], BF16)) for i in range(2)]
        wvb = [ctxB.enter_context(nc.sbuf_tensor(f"wvb{h}_{i}", [128, NC_, 256], BF16)) for i in range(2)]
        ksq = ctxB.enter_context(nc.sbuf_tensor(f"ksq{h}", [128, 512], BF16))
        pB = [ctxB.enter_context(nc.psum_tensor(f"pB{h}_{i}", [128, 512], F32)) for i in range(2)]
        pK2 = ctxB.enter_context(nc.psum_tensor(f"pK2{h}", [128, 512], F32))
        pRC = ctxB.enter_context(nc.psum_tensor(f"pRC{h}", [128, 1024], F32))

        t_rk = P.dma("sync", ropek_sb[:], ropek[h].rearrange("c p t -> p c t"))

        if h == 0:
            P.emit("vector", "memset", mk2row[:], 0.0, inc=True)

        kl_tok = {}
        for kh in range(NKV):
            wb = wkb[kh % 2]
            wfree = kl_tok.get(("wfree", kh % 2))
            tw = P.dma("gpsimd", wb[:],
                       wqkv_bf[:, 4096 + kh * 128:4096 + (kh + 1) * 128]
                       .rearrange("(c p) m -> p c m", p=128),
                       waits=[t_pckv] + ([wfree] if wfree else []))
            for g in range(2):
                pb = pB[(kh * 2 + g) % 2]
                pfree = kl_tok.get(("pfree", (kh * 2 + g) % 2))
                mm = None
                for c in range(NC_):
                    mm = P.emit("tensor", "matmul", pb[:],
                                wb[:, c, :], h1_fm[:, c, g * 512:(g + 1) * 512],
                                start=(c == 0), stop=(c == NC_ - 1),
                                waits=([tw] + ([pfree] if pfree else [])) if c == 0 else (),
                                inc=True if c == NC_ - 1 else None)
                kl_tok[("wfree", kh % 2)] = mm
                if h == 0:
                    precast_q(kh * 2 + g)
                csA = ropek_sb[:, 0, g * 512:(g + 1) * 512]
                csB = ropek_sb[:, 1, g * 512:(g + 1) * 512]
                col = h * 1024 + g * 512
                kc = kfm[:, kh, col:col + 512]
                kev = kfm[0:64, kh, col:col + 512]
                kod = kfm[64:128, kh, col:col + 512]
                rcfree = kl_tok.get(("rcfree",), (None, None))
                t4, (e1, e2) = rope_emit(pb, csA, csB, kev, kod, [mm, t_rk],
                                         pRC[:, 0:512], pRC[:, 512:1024],
                                         [rcfree[0]], [rcfree[1]])
                kl_tok[("pfree", (kh * 2 + g) % 2)] = t4
                kl_tok[("rcfree",)] = (e1, e2)
                t7 = P.emit("vector", "tensor_mul", ksq[:], kc, kc,
                            waits=[e1, e2], inc=True)
                pk2free = kl_tok.get(("pk2free",))
                t8 = P.emit("tensor", "matmul", pK2[0:1, :], onesb[:, 0:1], ksq[:],
                            start=True, stop=True,
                            waits=[t7, t_ones] + ([pk2free] if pk2free else []),
                            inc=True)
                t9 = P.emit("vector", "reduce_max", brow[0:1, 0:1], pK2[0:1, :],
                            axis=AX.X, waits=[t8], inc=True)
                t10 = P.emit("vector", "tensor_max", mk2row[0:1, kh:kh + 1],
                             mk2row[0:1, kh:kh + 1], brow[0:1, 0:1], waits=[t9], inc=True)
                kl_tok[("k2", kh, g)] = t10
                kl_tok[("pk2free",)] = t9

        for g in range(4):
            wb = wvb[g % 2]
            wfree = v_w_tok.get(("wfree", h, g % 2))
            tw = P.dma("gpsimd", wb[:],
                       wqkv_bf[:, 5120 + g * 256:5120 + (g + 1) * 256]
                       .rearrange("(c p) m -> p c m", p=128),
                       waits=[t_pckv] + ([wfree] if wfree else []))
            for m in range(8):
                pb = pB[(g * 8 + m) % 2]
                pfree = v_w_tok.get(("pfree", h, (g * 8 + m) % 2))
                mm = None
                for c in range(NC_):
                    mm = P.emit("tensor", "matmul", pb[:, 0:256],
                                h1_fm[:, c, m * 128:(m + 1) * 128],
                                wb[:, c, :],
                                start=(c == 0), stop=(c == NC_ - 1),
                                waits=([tw] + ([pfree] if pfree else [])) if c == 0 else (),
                                inc=True if c == NC_ - 1 else None)
                v_w_tok[("wfree", h, g % 2)] = mm
                te = P.emit("vector", "tensor_copy",
                            vtok[:, h * 8 + m, g * 256:(g + 1) * 256],
                            pb[:, 0:256], waits=[mm], inc=True)
                v_w_tok[("pfree", h, (g * 8 + m) % 2)] = te
                last_b_tokens.append(te)
        P.barrier()
        ctxB.close()

    ctxC = ExitStack()

    def sbC(name, shape, dt):
        return ctxC.enter_context(nc.sbuf_tensor(name, shape, dt))

    masks_sb = sbC("masks_sb", [128, 2, 8, 512], BF16)
    ropeq_sb = sbC("ropeq_sb", [128, 2, 2, 512], F32)
    wqb = [sbC(f"wqb{i}", [128, NC_, 128], BF16) for i in range(2)]
    qbuf = [sbC(f"qbuf{i}", [128, 1024], BF16) for i in range(2)]
    qsq = sbC("qsq", [128, 1024], BF16)
    NPS = 4
    probs = [sbC(f"probs{i}", [128, 512], BF16) for i in range(NPS)]
    rb = sbC("rb", [128, 512], F32)
    atout = [sbC(f"atout{i}", [128, 512], BF16) for i in range(2)]
    psco = [ctxC.enter_context(nc.psum_tensor(f"psco{i}", [128, 512], F32)) for i in range(NPS)]
    pat = ctxC.enter_context(nc.psum_tensor("pat", [128, 512], F32))
    prs = ctxC.enter_context(nc.psum_tensor("prs", [128, 512], F32))
    pQ = [ctxC.enter_context(nc.psum_tensor(f"pQ{i}", [128, 512], F32)) for i in range(2)]

    t_mask = P.dma("sync", masks_sb[:], masks.rearrange("s t p m -> p s t m"))
    t_rq = P.dma("sync", ropeq_sb[:], ropeq.rearrange("s c p m -> p s c m"))
    t_h1o = P.dma("sync", h1_fm[:, :, 512:1024],
                  h1own_scr.rearrange("p (c m) -> p c m", c=NC_))

    ct = {}

    def emit_qprod(hd):
        kh = hd // 4
        wb = wqb[hd % 2]
        qb = qbuf[hd % 2]
        wfree = ct.get(("wqfree", hd % 2))
        tw = P.dma("gpsimd", wb[:],
                   wqkv_bf[:, hd * 128:(hd + 1) * 128]
                   .rearrange("(c p) m -> p c m", p=128),
                   waits=[t_pcq] + ([wfree] if wfree else []))
        precast_wowf(hd)
        yield
        qg_tok = []
        ropetoks = []
        for g in range(2):
            pq = pQ[g]
            pqfree = ct.get(("pqfree", g))
            pcA = pQ[1 - g]
            pcA_free = ct.get(("pqA_free", 1 - g))
            mm = None
            for cc in range(4):
                for c8 in range(8):
                    c = cc * 8 + c8
                    rhs = h1_fm[:, c, 512:1024] if g == 0 else h1_fm[:, c, 0:512]
                    mm = P.emit("tensor", "matmul", pq[:], wb[:, c, :], rhs,
                                start=(c == 0), stop=(c == NC_ - 1),
                                waits=([tw, t_h1o] + ([pqfree] if pqfree else [])) if c == 0 else (),
                                inc=True if c == NC_ - 1 else None)
                yield
            ct[("wqfree", hd % 2)] = mm
            csA = ropeq_sb[:, g, 0, :]
            csB = ropeq_sb[:, g, 1, :]
            qev = qb[0:64, g * 512:(g + 1) * 512]
            qod = qb[64:128, g * 512:(g + 1) * 512]
            qfree = ct.get(("qfree", hd % 2))
            w0 = [mm, t_rq] + ([qfree] if qfree and g == 0 else [])
            t4, (e1, e2) = rope_emit(pq, csA, csB, qev, qod, w0,
                                     pcA, pq, [pcA_free], [])
            yield
            ropetoks.append(t4)
            qg_tok += [e1, e2]
            if g == 0:
                ct[("pqfree", 1)] = e1
            ct[("eA", g)] = e1
            ct[("eB", g)] = e2
            ct[("qready", hd, g)] = (e1, e2)

        tq1 = P.emit("vector", "tensor_mul", qsq[:], qb[:], qb[:],
                     waits=qg_tok, inc=True)
        ta = P.emit("tensor", "matmul", pQ[0][0:1, :], onesb[:, 0:1], qsq[:, 0:512],
                    start=True, stop=True, waits=[tq1, ct[("eA", 1)]], inc=True)
        tb = P.emit("tensor", "matmul", pQ[1][0:1, :], onesb[:, 0:1], qsq[:, 512:1024],
                    start=True, stop=True, waits=[tq1, ct[("eB", 1)]], inc=True)
        yield
        tm1 = P.emit("vector", "reduce_max", brow[0:1, 0:1], pQ[0][0:1, :],
                     axis=AX.X, waits=[ta], inc=True)
        tm2 = P.emit("vector", "reduce_max", brow[0:1, 1:2], pQ[1][0:1, :],
                     axis=AX.X, waits=[tb], inc=True)
        tm3 = P.emit("vector", "tensor_max", brow[0:1, 0:1], brow[0:1, 0:1],
                     brow[0:1, 1:2], waits=[tm2], inc=True)
        tm4 = P.emit("vector", "tensor_scalar", brow[0:1, 2:3], brow[0:1, 0:1],
                     mk2row[0:1, kh:kh + 1], -0.5, ALU.add, ALU.mult,
                     waits=[tm3], inc=True)
        tm6 = P.emit("vector", "tensor_scalar_max", brow[0:1, 2:3], brow[0:1, 2:3],
                     -80.0, waits=[tm4], inc=True)
        tbc = P.emit("tensor", "matmul", pQ[1][:, 0:1], onesf[0:1, 0:128],
                     brow[0:1, 2:3], start=True, stop=True,
                     waits=[tm6, t_onesf, tm2], inc=True)
        tbc2 = P.emit("vector", "tensor_copy", biascol[:, hd:hd + 1],
                      pQ[1][:, 0:1], waits=[tbc], inc=True)
        ct[("pqfree", 0)] = tm1
        ct[("pqA_free", 1)] = tbc2
        ct[("tbc2", hd)] = tbc2

    def emit_scores(hd, s, t, qb):
        kh = hd // 4
        pc = psco[t % NPS]
        pcfree = ct.get(("pscofree", t % NPS))
        tbc2 = ct[("tbc2", hd)]
        w0 = [tbc2] + list(ct[("qready", hd, s)])
        if pcfree is not None:
            w0.append(pcfree)
        return P.emit("tensor", "matmul", pc[:],
                      kfm[:, kh, t * 128:(t + 1) * 128],
                      qb[:, s * 512:(s + 1) * 512],
                      start=True, stop=True, waits=w0, inc=True)

    def emit_attention(hd, feed=None):
        kh = hd // 4
        qb = qbuf[hd % 2]
        LOOK = 3
        vlast = None
        for s in range(2):
            D = DS[s]
            sc_tok = {}
            for t in range(min(LOOK + 1, D)):
                sc_tok[t] = emit_scores(hd, s, t, qb)
            rs_tok = None
            for t in range(D):
                if feed is not None:
                    next(feed, None)
                pc = psco[t % NPS]
                pbuf = probs[t % NPS]
                pbfree = ct.get(("probsfree", t % NPS))
                wexp = [sc_tok[t]]
                if pbfree is not None:
                    wexp.append(pbfree)
                te = P.emit("scalar", "activation", pbuf[:], pc[:], AF.Exp,
                            bias=biascol[:, hd:hd + 1], scale=1.0,
                            waits=wexp, inc=True)
                ct[("pscofree", t % NPS)] = te
                if t + LOOK + 1 < D:
                    sc_tok[t + LOOK + 1] = emit_scores(hd, s, t + LOOK + 1, qb)
                masked = (s == 0) or (t >= 8)
                tok_ready = te
                if masked:
                    moff = 0 if s == 0 else 8
                    wm = [te, t_mask]
                    if pbfree is not None:
                        wm.append(pbfree)
                    meng = "vector" if t % 2 == 0 else "gpsimd"
                    tok_ready = P.emit(meng, "tensor_mul", pbuf[:], pbuf[:],
                                       masks_sb[:, s, t - moff, :],
                                       waits=wm, inc=True)
                w0 = [tok_ready, te]
                if t == 0:
                    if ("patfree2",) in ct:
                        w0.append(ct[("patfree2",)])
                    if ("prsfree2",) in ct:
                        w0.append(ct[("prsfree2",)])
                vlast = P.emit("tensor", "matmul", pat[:],
                               vtok[:, t, kh * 128:(kh + 1) * 128],
                               pbuf[:],
                               start=(t == 0), stop=(t == D - 1),
                               waits=w0,
                               inc=True if t == D - 1 else None)
                rs_tok = P.emit("tensor", "matmul", prs[:],
                                onesb[:], pbuf[:],
                                start=(t == 0), stop=(t == D - 1),
                                inc=True)
                ct[("probsfree", t % NPS)] = rs_tok
            tr = P.emit("vector", "reciprocal", rrow[0:1, :], prs[0:1, :],
                        waits=[rs_tok], inc=True)
            ct[("prsfree2",)] = tr
            bb = D % NPS
            tb2 = P.emit("tensor", "matmul", psco[bb][:], onesf[0:1, 0:128],
                         rrow[0:1, :], start=True, stop=True,
                         waits=[tr, ct[("pscofree", bb)]], inc=True)
            tc = P.emit("vector", "tensor_copy", rb[:], psco[bb][:],
                        waits=[tb2], inc=True)
            ct[("pscofree", bb)] = tc
            ao = atout[s]
            aofree = ct.get(("aofree", s))
            tm = P.emit("vector", "tensor_mul", ao[:], pat[:], rb[:],
                        waits=[tc, vlast] + ([aofree] if aofree else []), inc=True)
            ct[("patfree2",)] = tm
            ts_ = P.dma("sync", attn_scr[hd, :, s * 512:(s + 1) * 512], ao[:],
                        waits=[tm])
            ct[("aofree", s)] = ts_
        ct[("qfree", hd % 2)] = vlast

    gen = emit_qprod(0)
    for _ in gen:
        pass
    for hd in range(NH):
        feed = emit_qprod(hd + 1) if hd + 1 < NH else None
        emit_attention(hd, feed)
        if feed is not None:
            for _ in feed:
                pass
    P.barrier()
    ctxC.close()
    ctx1.close()

    ctxD = ExitStack()

    def sbD(name, shape, dt):
        return ctxD.enter_context(nc.sbuf_tensor(name, shape, dt))

    wob = [sbD(f"wob{i}", [128, NC_, 512], BF16) for i in range(3)]
    atb = [sbD(f"atb{i}", [128, NC_, 512], BF16) for i in range(2)]
    xrb = [sbD(f"xrb{i}", [128, 512], F32) for i in range(2)]
    x1b = [sbD(f"x1b{i}", [128, 512], F32) for i in range(2)]
    pD = [ctxD.enter_context(nc.psum_tensor(f"pD{i}", [128, 512], F32)) for i in range(2)]

    dt_ = {}
    it = 0
    tal = [P.dma("sync", atb[hf][:],
                 attn_scr[:, :, hf * 512:(hf + 1) * 512]
                 .rearrange("c p m -> p c m")) for hf in range(2)]
    for n in range(8):
        wb = wob[n % 3]
        wfree = dt_.get(("wofree", n % 3))
        tws = [P.dma("gpsimd", wb[:],
                     wo_bf[:, n * 512:(n + 1) * 512]
                     .rearrange("(c p) m -> p c m", p=128),
                     waits=[t_pcwo] + ([wfree] if wfree else []))]
        for hf in range(2):
            ab = atb[hf]
            for ml in range(4):
                m = hf * 4 + ml
                pb = pD[it % 2]
                pfree = dt_.get(("pDfree", it % 2))
                mm = None
                for c in range(NC_):
                    w0 = []
                    if c == 0:
                        w0 = [tal[hf], tws[-1]] + ([pfree] if pfree else [])
                    mm = P.emit("tensor", "matmul", pb[:],
                                ab[:, c, ml * 128:(ml + 1) * 128],
                                wb[:, c, :],
                                start=(c == 0), stop=(c == NC_ - 1),
                                waits=w0, inc=True if c == NC_ - 1 else None)
                dt_[("wofree", n % 3)] = mm
                xb_ = xrb[it % 2]
                xfree = dt_.get(("xrfree", it % 2))
                tx = P.dma("sync", xb_[:],
                           xh[hf, ml * 128:(ml + 1) * 128, n * 512:(n + 1) * 512],
                           waits=[xfree] if xfree else [])
                ob = x1b[it % 2]
                ofree = dt_.get(("x1free", it % 2))
                tadd = P.emit("vector", "tensor_add", ob[:], pb[:], xb_[:],
                              waits=[mm, tx] + ([ofree] if ofree else []), inc=True)
                dt_[("pDfree", it % 2)] = tadd
                dt_[("xrfree", it % 2)] = tadd
                tst = P.dma("sync", x1_scr[m * 128:(m + 1) * 128, n * 512:(n + 1) * 512],
                            ob[:], waits=[tadd])
                dt_[("x1free", it % 2)] = tst
                it += 1
    P.barrier()
    ctxD.close()

    ctxE = ExitStack()

    def sbE(name, shape, dt):
        return ctxE.enter_context(nc.sbuf_tensor(name, shape, dt))

    h2_fm = sbE("h2_fm", [128, NC_, 1024], BF16)
    subE = {
        "xt": [sbE(f"xtE{i}", [128, 4096], F32) for i in range(2)],
        "h1t": [sbE(f"h1tE{i}", [128, 4096], BF16) for i in range(2)],
        "nmr": sbE("nmrE", [128, 2], F32),
        "nmean": sbE("nmeanE", [128, 2], F32),
        "varsum": sbE("varsumE", [128, 2], F32),
        "stdv": sbE("stdvE", [128, 2], F32),
        "rstd": sbE("rstdE", [128, 2], F32),
        "pA": [ctxE.enter_context(nc.psum_tensor(f"pAE{i}", [128, 1024], BF16)) for i in range(2)],
    }
    wfb = [sbE(f"wfb{i}", [128, NC_, 512], BF16) for i in range(2)]
    biasb = sbE("biasb", [128, 4096], BF16)
    x1r = [sbE(f"x1r{i}", [128, 512], F32) for i in range(2)]
    outb = [sbE(f"outb{i}", [128, 512], F32) for i in range(2)]
    pE = [ctxE.enter_context(nc.psum_tensor(f"pE{i}", [128, 512], F32)) for i in range(2)]

    t_bias = P.dma("gpsimd", biasb[:], bias8)
    et = {"it": 0}

    def load_wf(n):
        wfree = et.get(("wffree", n % 2))
        return P.dma("gpsimd", wfb[n % 2][:],
                     wf_bf[:, n * 512:(n + 1) * 512]
                     .rearrange("(c p) m -> p c m", p=128),
                     waits=[t_pcwf] + ([wfree] if wfree else []))

    def emit_e2(n, m, tw, toks):
        it = et["it"]
        wb = wfb[n % 2]
        pb = pE[it % 2]
        pfree = et.get(("pEfree", it % 2))
        mm = None
        for c in range(NC_):
            w0 = []
            if c == 0:
                w0 = [tw]
                if pfree:
                    w0.append(pfree)
                w0 += [toks[("evac", m, cc)] for cc in range(4)]
            mm = P.emit("tensor", "matmul", pb[:],
                        h2_fm[:, c, m * 128:(m + 1) * 128],
                        wb[:, c, :],
                        start=(c == 0), stop=False,
                        waits=w0, inc=None)
        mm = P.emit("tensor", "matmul", pb[:], onesb[0:1, :],
                    biasb[0:1, n * 512:(n + 1) * 512],
                    start=False, stop=True, waits=[t_bias], inc=True)
        et[("wffree", n % 2)] = mm
        xb_ = x1r[it % 2]
        xfree = et.get(("x1rfree", it % 2))
        tx = P.dma("sync", xb_[:],
                   x1_scr[m * 128:(m + 1) * 128, n * 512:(n + 1) * 512],
                   waits=[xfree] if xfree else [])
        ob = outb[it % 2]
        ofree = et.get(("outfree", it % 2))
        tst2 = P.emit("vector", "scalar_tensor_tensor", ob[:], pb[:], 0.01,
                      xb_[:], ALU.mult, ALU.add,
                      waits=[mm, tx] + ([ofree] if ofree else []), inc=True)
        et[("pEfree", it % 2)] = tst2
        et[("x1rfree", it % 2)] = tst2
        tso = P.dma("sync", out_sh[m * 128:(m + 1) * 128, n * 512:(n + 1) * 512],
                    ob[:], waits=[tst2])
        et[("outfree", it % 2)] = tso
        et["it"] = it + 1

    tw0 = load_wf(0)
    toksE = ln_phase(lambda i: x1_scr[i * 128:(i + 1) * 128, :], 8, h2_fm, 0,
                     subE, after_tile=lambda i, toks: emit_e2(0, i, tw0, toks))
    for n in range(1, 8):
        tw = load_wf(n)
        for m in range(8):
            emit_e2(n, m, tw, toksE)

    P.emit("sync", "wait_ge_final")

    for s in P.sems.values():
        s.handle = nc.alloc_semaphore(name=s.name)

    final_waits = [(P.sems[n], P.sems[n].count) for n in sorted(P.store_sems)]

    with nc.Block() as block:
        @block.sync
        def _(sync):
            for op in P.ops["sync"]:
                if op.method == "wait_ge_final":
                    for (s, v) in final_waits:
                        sync.wait_ge(s.handle, v)
                    continue
                for (s, v) in op.waits:
                    sync.wait_ge(s.handle, v)
                if op.method == "wait_ge":
                    continue
                inst = getattr(sync, op.method)(*op.args, **op.kwargs)
                for (s, nn) in op.incs:
                    inst.then_inc(s.handle, nn)

        @block.gpsimd
        def _(gpsimd):
            _replay(P, "gpsimd", gpsimd)

        @block.tensor
        def _(tensor):
            _replay(P, "tensor", tensor)

        @block.vector
        def _(vector):
            _replay(P, "vector", vector)

        @block.scalar
        def _(scalar):
            _replay(P, "scalar", scalar)

    ctxE.close()
    ctx.close()
    return nc



def _perm_cols():
    perm = []
    for hd in range(NH):
        base = hd * HD
        perm += [base + 2 * j for j in range(64)] + [base + 2 * j + 1 for j in range(64)]
    for kh in range(NKV):
        base = 4096 + kh * HD
        perm += [base + 2 * j for j in range(64)] + [base + 2 * j + 1 for j in range(64)]
    perm += list(range(5120, 6144))
    return np.array(perm, dtype=np.int64)


def _core_meta(p):
    own = [0, 3] if p == 0 else [1, 2]
    other = [1, 2] if p == 0 else [0, 3]
    return own, other


def _rope_tables(pos):
    j = np.arange(64, dtype=np.float64)
    inv = 1.0 / (10000.0 ** (j / 64.0))
    ang = pos[:, None].astype(np.float64) * inv[None, :]
    cos = (np.cos(ang).T * S4).astype(np.float32)
    sin = (np.sin(ang).T * S4).astype(np.float32)
    csA = np.concatenate([cos, sin], axis=0)
    csB = np.concatenate([sin, cos], axis=0)
    return np.stack([csA, csB])


def kernel(L0_attn_x, L0_attn_w_qkv, L0_attn_w_o, L0_ffn_quantized_weight,
           L0_ffn_quantized_bias):
    x = np.asarray(L0_attn_x, dtype=np.float32)
    wqkv = np.ascontiguousarray(np.asarray(L0_attn_w_qkv, np.float32)[:, _perm_cols()])
    wo = np.ascontiguousarray(np.asarray(L0_attn_w_o, np.float32))
    wf = np.ascontiguousarray(np.asarray(L0_ffn_quantized_weight, np.int8).T)
    bias8 = np.ascontiguousarray(
        np.broadcast_to(np.asarray(L0_ffn_quantized_bias, np.int8).reshape(1, 4096),
                        (128, 4096)))
    rcomb = np.zeros((128, 128), np.float32)
    for j in range(64):
        rcomb[j, j] = 1.0
        rcomb[j + 64, j] = -1.0
        rcomb[j, j + 64] = 1.0
        rcomb[j + 64, j + 64] = 1.0
    cid = np.stack([np.eye(128), np.ones((128, 128)), rcomb]).astype(ml_dtypes.bfloat16)
    onesf = np.ones((128, 128), np.float32)

    nc = build_nc()
    in_maps = []
    metas = []
    for c in range(8):
        b, p = c >> 1, c & 1
        own, other = _core_meta(p)
        xb = x[b]
        xh = np.stack([
            np.concatenate([xb[own[h] * 512:(own[h] + 1) * 512],
                            xb[other[h] * 512:(other[h] + 1) * 512]])
            for h in range(2)])
        posk = np.concatenate([
            own[0] * 512 + np.arange(512), other[0] * 512 + np.arange(512),
            own[1] * 512 + np.arange(512), other[1] * 512 + np.arange(512)])
        masks = np.zeros((2, 8, 128, 512), np.float32)
        for s in range(2):
            posq = own[s] * 512 + np.arange(512)
            t0 = 0 if s == 0 else 8
            for jr in range(8):
                t = t0 + jr
                masks[s, jr] = (posk[t * 128:(t + 1) * 128, None] <= posq[None, :])
        ropek = np.stack([_rope_tables(posk[:1024]), _rope_tables(posk[1024:])])
        ropeq = np.stack([_rope_tables(own[s] * 512 + np.arange(512)) for s in range(2)])
        in_maps.append({
            "xh": np.ascontiguousarray(xh),
            "wqkv": wqkv, "wo": wo, "wf": wf, "bias8": bias8,
            "masks": masks.astype(ml_dtypes.bfloat16),
            "ropek": np.ascontiguousarray(ropek),
            "ropeq": np.ascontiguousarray(ropeq),
            "cid": cid, "onesf": onesf,
        })
        metas.append((b, own))

    res = run_bass_kernel_spmd(nc, in_maps, core_ids=list(range(8)))
    out = np.empty((4, SEQ, D_MODEL), np.float32)
    for c in range(8):
        b, own = metas[c]
        sh = res.results[c]["out_sh"]
        out[b, own[0] * 512:(own[0] + 1) * 512] = sh[0:512]
        out[b, own[1] * 512:(own[1] + 1) * 512] = sh[512:1024]
    return out


# version 19
# speedup vs baseline: 1.1851x; 1.0038x over previous
import numpy as np
import ml_dtypes

import concourse.bass as bass
import concourse.mybir as mybir
from concourse.bass_utils import run_bass_kernel_spmd
from contextlib import ExitStack
from concourse.bass_primitives import MemorySpace

F32 = mybir.dt.float32
BF16 = mybir.dt.bfloat16
I8 = mybir.dt.int8
AF = mybir.ActivationFunctionType
ALU = mybir.AluOpType
AX = mybir.AxisListType

D_MODEL = 4096
SEQ = 2048
NH = 32
NKV = 8
HD = 128
NC_ = 32
DS = (8, 16)
S4 = float(128.0 ** -0.25)
EPS = 1e-5

ENGS = ("sync", "gpsimd", "tensor", "vector", "scalar")


class Sem:
    __slots__ = ("name", "count", "handle")

    def __init__(self, name):
        self.name = name
        self.count = 0
        self.handle = None


class Op:
    __slots__ = ("method", "args", "kwargs", "waits", "incs")

    def __init__(self, method, args, kwargs, waits, incs):
        self.method = method
        self.args = args
        self.kwargs = kwargs
        self.waits = waits
        self.incs = incs


class Plan:
    def __init__(self):
        self.ops = {e: [] for e in ENGS}
        self.sems = {}
        self.wm = {e: {} for e in ENGS}
        self.psem = {
            "sync": self.sem("sLD"),
            "gpsimd": self.sem("sGP"),
            "tensor": self.sem("sPE"),
            "vector": self.sem("sDV"),
            "scalar": self.sem("sAC"),
        }
        self.sst = self.sem("sST")
        self.store_sems = set()

    def sem(self, name):
        if name not in self.sems:
            self.sems[name] = Sem(name)
        return self.sems[name]

    def emit(self, eng, method, *args, waits=(), inc=None, inc_n=1, **kwargs):
        wl = []
        wm = self.wm[eng]
        for wv in waits:
            if wv is None:
                continue
            s, v = wv
            if wm.get(s.name, 0) >= v:
                continue
            wm[s.name] = v
            wl.append((s, v))
        il = []
        tok = None
        if inc is not None:
            s = self.psem[eng] if inc is True else inc
            s.count += inc_n
            il.append((s, inc_n))
            tok = (s, s.count)
        self.ops[eng].append(Op(method, args, kwargs, wl, il))
        return tok

    def dma(self, eng, out, in_, waits=(), sem=None, **kw):
        if sem is None:
            is_store = out.space != MemorySpace.SBUF
            name = (in_ if is_store else out).tensor.name
            sem = self.sem("b_" + name)
            if is_store:
                self.store_sems.add(sem.name)
        return self.emit(eng, "dma_start", waits=waits, inc=sem, inc_n=16,
                         out=out, in_=in_, **kw)

    def barrier(self):
        for e in ENGS:
            ops = self.ops[e]
            last = None
            for op in reversed(ops):
                if op.method not in ("wait_ge", "wait_ge_final"):
                    last = op
                    break
            if last is None:
                continue
            if not last.incs:
                s = self.psem[e]
                s.count += 1
                last.incs.append((s, 1))
        targets = [(s, s.count) for s in self.sems.values() if s.count > 0]
        for e in ENGS:
            wm = self.wm[e]
            for (s, v) in targets:
                if v > 0 and wm.get(s.name, 0) < v:
                    wm[s.name] = v
                    self.ops[e].append(Op("wait_ge", (), {}, [(s, v)], []))


def _replay(plan, eng_name, eng):
    for op in plan.ops[eng_name]:
        for (s, v) in op.waits:
            eng.wait_ge(s.handle, v)
        if op.method == "wait_ge":
            continue
        inst = getattr(eng, op.method)(*op.args, **op.kwargs)
        for (s, n) in op.incs:
            inst.then_inc(s.handle, n)


def build_nc():
    nc = bass.Bass("TRN2", target_bir_lowering=False)
    P = Plan()
    sLD = P.psem["sync"]
    sGP = P.psem["gpsimd"]
    sPE = P.psem["tensor"]
    sDV = P.psem["vector"]
    sAC = P.psem["scalar"]
    sST = P.sst

    xh = nc.dram_tensor("xh", [2, 1024, D_MODEL], F32, kind="ExternalInput").ap()
    wqkv = nc.dram_tensor("wqkv", [D_MODEL, 6144], F32, kind="ExternalInput").ap()
    wo = nc.dram_tensor("wo", [D_MODEL, D_MODEL], F32, kind="ExternalInput").ap()
    wf = nc.dram_tensor("wf", [D_MODEL, D_MODEL], I8, kind="ExternalInput").ap()
    bias8 = nc.dram_tensor("bias8", [128, D_MODEL], I8, kind="ExternalInput").ap()
    masks = nc.dram_tensor("masks", [2, 8, 128, 512], BF16, kind="ExternalInput").ap()
    ropek = nc.dram_tensor("ropek", [2, 2, 128, 1024], F32, kind="ExternalInput").ap()
    ropeq = nc.dram_tensor("ropeq", [2, 2, 128, 512], F32, kind="ExternalInput").ap()
    cid = nc.dram_tensor("cid", [3, 128, 128], BF16, kind="ExternalInput").ap()
    onesf_d = nc.dram_tensor("onesf", [128, 128], F32, kind="ExternalInput").ap()
    out_sh = nc.dram_tensor("out_sh", [1024, D_MODEL], F32, kind="ExternalOutput").ap()
    attn_scr = nc.dram_tensor("attn_scr", [NH, 128, 1024], BF16, kind="Internal").ap()
    x1_scr = nc.dram_tensor("x1_scr", [1024, D_MODEL], F32, kind="Internal").ap()
    h1own_scr = nc.dram_tensor("h1own_scr", [128, NC_ * 512], BF16, kind="Internal").ap()
    wqkv_bf = nc.dram_tensor("wqkv_bf", [D_MODEL, 6144], BF16, kind="Internal").ap()
    wo_bf = nc.dram_tensor("wo_bf", [D_MODEL, D_MODEL], BF16, kind="Internal").ap()
    wf_bf = nc.dram_tensor("wf_bf", [D_MODEL, D_MODEL], BF16, kind="Internal").ap()

    ctx = ExitStack()

    def sb(name, shape, dt):
        return ctx.enter_context(nc.sbuf_tensor(name, shape, dt))

    idn = sb("idn", [128, 128], BF16)
    onesb = sb("onesb", [128, 128], BF16)
    onesf = sb("onesf_t", [128, 128], F32)
    rcomb = sb("rcomb", [128, 128], BF16)
    epscol = sb("epscol", [128, 1], F32)
    biascol = sb("biascol", [128, NH], F32)
    mk2row = sb("mk2row", [1, NKV], F32)
    rrow = sb("rrow", [1, 512], F32)
    brow = sb("brow", [1, 4], F32)

    s_pckv = P.sem("pc_kv")
    s_pcq = P.sem("pc_q")
    s_pcwo = P.sem("pc_wo")
    s_pcwf = P.sem("pc_wf")
    for i in range(2):
        P.dma("gpsimd", wqkv_bf[:, 4096 + i * 1024:4096 + (i + 1) * 1024],
              wqkv[:, 4096 + i * 1024:4096 + (i + 1) * 1024], sem=s_pckv)
    t_pckv = (s_pckv, s_pckv.count)
    t_pcq = (s_pcq, 16 * 16)
    t_pcwo = (s_pcwo, 16 * 16)
    t_pcwf = (s_pcwf, 16 * 16)

    def precast_q(i):
        P.dma("gpsimd", wqkv_bf[:, i * 256:(i + 1) * 256],
              wqkv[:, i * 256:(i + 1) * 256], sem=s_pcq)

    def precast_wowf(i):
        if i < 16:
            P.dma("gpsimd", wo_bf[:, i * 256:(i + 1) * 256],
                  wo[:, i * 256:(i + 1) * 256], sem=s_pcwo)
        elif i < 32:
            j = i - 16
            P.dma("gpsimd", wf_bf[:, j * 256:(j + 1) * 256],
                  wf[:, j * 256:(j + 1) * 256], sem=s_pcwf)

    t_idn = P.dma("sync", idn[:], cid[0])
    t_ones = P.dma("sync", onesb[:], cid[1])
    t_rcomb = P.dma("sync", rcomb[:], cid[2])
    t_onesf = P.dma("sync", onesf[:], onesf_d)
    P.emit("vector", "memset", epscol[:], EPS, inc=True)

    ctx1 = ExitStack()

    def sb1(name, shape, dt):
        return ctx1.enter_context(nc.sbuf_tensor(name, shape, dt))

    h1_fm = sb1("h1_fm", [128, NC_, 1024], BF16)
    kfm = sb1("kfm", [128, NKV, 2048], BF16)
    vtok = sb1("vtok", [128, 16, 1024], BF16)
    ropek_sb = sb1("ropek_sb", [128, 2, 1024], F32)
    tmpb = sb1("tmpb", [128, 2, 512], BF16)

    rope_st = {"c1": None, "c2": None}

    def rope_emit(pb, csA, csB, dst_ev, dst_od, first_waits, pcA, pcB, freeA, freeB):
        st = rope_st
        w1 = list(first_waits)
        if st["c1"]:
            w1.append(st["c1"])
        tA = P.emit("vector", "tensor_mul", tmpb[:, 0, :], pb[:], csA,
                    waits=w1, inc=True)
        w2 = [st["c2"]] if st["c2"] else []
        t4 = P.emit("vector", "tensor_mul", tmpb[:, 1, :], pb[:], csB,
                    waits=w2, inc=True)
        wc = [tA, t4, t_rcomb] + [t for t in freeA if t]
        mm1 = P.emit("tensor", "matmul", pcA[:], rcomb[:], tmpb[:, 0, :],
                     start=True, stop=True, waits=wc, inc=True)
        mm2 = P.emit("tensor", "matmul", pcB[:], rcomb[:], tmpb[:, 1, :],
                     start=True, stop=True,
                     waits=[t4] + [t for t in freeB if t], inc=True)
        e1 = P.emit("scalar", "copy", dst_ev, pcA[0:64, :],
                    waits=[mm1], inc=True)
        e2 = P.emit("scalar", "copy", dst_od, pcB[64:128, :],
                    waits=[mm2], inc=True)
        st.update(c1=mm1, c2=mm2)
        return t4, (e1, e2)

    def ln_phase(src_rows_ap, n_tiles, dst_fm, dst_cols_base, sub, after_tile=None):
        xt = sub["xt"]
        h1t = sub["h1t"]
        nmr = sub["nmr"]
        nmean = sub["nmean"]
        varsum = sub["varsum"]
        stdv = sub["stdv"]
        rstd = sub["rstd"]
        pA = sub["pA"]
        toks = sub.setdefault("toks", {})
        for i in range(n_tiles):
            bs = i % 2
            src = src_rows_ap(i)
            prev = toks.get(("free", bs))
            waits = prev if prev else []
            tl = P.dma("sync", xt[bs][:], src, waits=waits)
            t1 = P.emit("vector", "reduce_sum", nmr[:, bs:bs + 1], xt[bs][:],
                        axis=AX.X, negate=True, waits=[tl], inc=True)
            t2 = P.emit("vector", "tensor_scalar_mul", nmean[:, bs:bs + 1],
                        nmr[:, bs:bs + 1], 1.0 / D_MODEL, waits=[t1], inc=True)
            wsq = [t2]
            if ("sqfree", bs) in toks:
                wsq.append(toks[("sqfree", bs)])
            t3 = P.emit("scalar", "activation", h1t[bs][:, 0:4096], xt[bs][:],
                        AF.Square, bias=nmean[:, bs:bs + 1], scale=1.0,
                        accum_out=varsum[:, bs:bs + 1], waits=wsq, inc=True)
            t4 = P.emit("scalar", "activation", stdv[:, bs:bs + 1],
                        varsum[:, bs:bs + 1], AF.Sqrt,
                        bias=epscol[:, 0:1], scale=1.0 / D_MODEL,
                        waits=[t3], inc=True)
            t5 = P.emit("vector", "reciprocal", rstd[:, bs:bs + 1],
                        stdv[:, bs:bs + 1], waits=[t4], inc=True)
            t6 = P.emit("vector", "tensor_scalar", h1t[bs][:, 0:4096], xt[bs][:],
                        nmean[:, bs:bs + 1], rstd[:, bs:bs + 1],
                        ALU.add, ALU.mult, waits=[t5, t3], inc=True)
            toks[("free", bs)] = [t6, t3]
            for cc in range(4):
                pa = pA[cc % 2]
                wtp = [t6]
                if ("pafree", cc % 2) in toks:
                    wtp.append(toks[("pafree", cc % 2)])
                tp = None
                for c8 in range(8):
                    c = cc * 8 + c8
                    tp = P.emit("tensor", "transpose",
                                pa[:, c8 * 128:(c8 + 1) * 128],
                                h1t[bs][:, c * 128:(c + 1) * 128], idn[:],
                                waits=(wtp + [t_idn]) if c8 == 0 else (),
                                inc=True if c8 == 7 else None)
                col = dst_cols_base + i * 128
                te = P.emit("vector", "tensor_copy",
                            dst_fm[:, cc * 8:(cc + 1) * 8, col:col + 128],
                            pa[:].rearrange("p (c m) -> p c m", c=8),
                            waits=[tp], inc=True)
                toks[("pafree", cc % 2)] = te
                toks[("evac", i, cc)] = te
            toks[("sqfree", bs)] = toks[("evac", i, 3)]
            if after_tile is not None:
                after_tile(i, toks)
        return toks

    kh_w_tok = {}
    v_w_tok = {}
    last_b_tokens = []

    for h in range(2):
        ctxA = ExitStack()
        subA = {
            "xt": [ctxA.enter_context(nc.sbuf_tensor(f"xtA{h}_{i}", [128, 4096], F32)) for i in range(2)],
            "h1t": [ctxA.enter_context(nc.sbuf_tensor(f"h1tA{h}_{i}", [128, 4096], BF16)) for i in range(2)],
            "nmr": ctxA.enter_context(nc.sbuf_tensor(f"nmrA{h}", [128, 2], F32)),
            "nmean": ctxA.enter_context(nc.sbuf_tensor(f"nmeanA{h}", [128, 2], F32)),
            "varsum": ctxA.enter_context(nc.sbuf_tensor(f"varsumA{h}", [128, 2], F32)),
            "stdv": ctxA.enter_context(nc.sbuf_tensor(f"stdvA{h}", [128, 2], F32)),
            "rstd": ctxA.enter_context(nc.sbuf_tensor(f"rstdA{h}", [128, 2], F32)),
            "pA": [ctxA.enter_context(nc.psum_tensor(f"pAA{h}_{i}", [128, 1024], BF16)) for i in range(2)],
        }
        toksA = ln_phase(lambda i: xh[h, i * 128:(i + 1) * 128, :], 8,
                         h1_fm, 0, subA)
        if h == 0:
            w = [toksA[("evac", i, cc)] for i in range(4) for cc in range(4)]
            P.dma("sync", h1own_scr.rearrange("p (c m) -> p c m", c=NC_),
                  h1_fm[:, :, 0:512], waits=w)
        P.barrier()
        ctxA.close()

        ctxB = ExitStack()
        wkb = [ctxB.enter_context(nc.sbuf_tensor(f"wkb{h}_{i}", [128, NC_, 128], BF16)) for i in range(2)]
        wvb = [ctxB.enter_context(nc.sbuf_tensor(f"wvb{h}_{i}", [128, NC_, 256], BF16)) for i in range(2)]
        ksq = ctxB.enter_context(nc.sbuf_tensor(f"ksq{h}", [128, 512], BF16))
        pB = [ctxB.enter_context(nc.psum_tensor(f"pB{h}_{i}", [128, 512], F32)) for i in range(2)]
        pK2 = ctxB.enter_context(nc.psum_tensor(f"pK2{h}", [128, 512], F32))
        pRC = ctxB.enter_context(nc.psum_tensor(f"pRC{h}", [128, 1024], F32))

        t_rk = P.dma("sync", ropek_sb[:], ropek[h].rearrange("c p t -> p c t"))

        if h == 0:
            P.emit("vector", "memset", mk2row[:], 0.0, inc=True)

        kl_tok = {}
        for kh in range(NKV):
            wb = wkb[kh % 2]
            wfree = kl_tok.get(("wfree", kh % 2))
            tw = P.dma("gpsimd", wb[:],
                       wqkv_bf[:, 4096 + kh * 128:4096 + (kh + 1) * 128]
                       .rearrange("(c p) m -> p c m", p=128),
                       waits=[t_pckv] + ([wfree] if wfree else []))
            for g in range(2):
                pb = pB[(kh * 2 + g) % 2]
                pfree = kl_tok.get(("pfree", (kh * 2 + g) % 2))
                mm = None
                for c in range(NC_):
                    mm = P.emit("tensor", "matmul", pb[:],
                                wb[:, c, :], h1_fm[:, c, g * 512:(g + 1) * 512],
                                start=(c == 0), stop=(c == NC_ - 1),
                                waits=([tw] + ([pfree] if pfree else [])) if c == 0 else (),
                                inc=True if c == NC_ - 1 else None)
                kl_tok[("wfree", kh % 2)] = mm
                if h == 0:
                    precast_q(kh * 2 + g)
                csA = ropek_sb[:, 0, g * 512:(g + 1) * 512]
                csB = ropek_sb[:, 1, g * 512:(g + 1) * 512]
                col = h * 1024 + g * 512
                kc = kfm[:, kh, col:col + 512]
                kev = kfm[0:64, kh, col:col + 512]
                kod = kfm[64:128, kh, col:col + 512]
                rcfree = kl_tok.get(("rcfree",), (None, None))
                t4, (e1, e2) = rope_emit(pb, csA, csB, kev, kod, [mm, t_rk],
                                         pRC[:, 0:512], pRC[:, 512:1024],
                                         [rcfree[0]], [rcfree[1]])
                kl_tok[("pfree", (kh * 2 + g) % 2)] = t4
                kl_tok[("rcfree",)] = (e1, e2)
                t7 = P.emit("vector", "tensor_mul", ksq[:], kc, kc,
                            waits=[e1, e2], inc=True)
                pk2free = kl_tok.get(("pk2free",))
                t8 = P.emit("tensor", "matmul", pK2[0:1, :], onesb[:, 0:1], ksq[:],
                            start=True, stop=True,
                            waits=[t7, t_ones] + ([pk2free] if pk2free else []),
                            inc=True)
                t9 = P.emit("vector", "reduce_max", brow[0:1, 0:1], pK2[0:1, :],
                            axis=AX.X, waits=[t8], inc=True)
                t10 = P.emit("vector", "tensor_max", mk2row[0:1, kh:kh + 1],
                             mk2row[0:1, kh:kh + 1], brow[0:1, 0:1], waits=[t9], inc=True)
                kl_tok[("k2", kh, g)] = t10
                kl_tok[("pk2free",)] = t9

        for g in range(4):
            wb = wvb[g % 2]
            wfree = v_w_tok.get(("wfree", h, g % 2))
            tw = P.dma("gpsimd", wb[:],
                       wqkv_bf[:, 5120 + g * 256:5120 + (g + 1) * 256]
                       .rearrange("(c p) m -> p c m", p=128),
                       waits=[t_pckv] + ([wfree] if wfree else []))
            for m in range(8):
                pb = pB[(g * 8 + m) % 2]
                pfree = v_w_tok.get(("pfree", h, (g * 8 + m) % 2))
                mm = None
                for c in range(NC_):
                    mm = P.emit("tensor", "matmul", pb[:, 0:256],
                                h1_fm[:, c, m * 128:(m + 1) * 128],
                                wb[:, c, :],
                                start=(c == 0), stop=(c == NC_ - 1),
                                waits=([tw] + ([pfree] if pfree else [])) if c == 0 else (),
                                inc=True if c == NC_ - 1 else None)
                v_w_tok[("wfree", h, g % 2)] = mm
                te = P.emit("vector", "tensor_copy",
                            vtok[:, h * 8 + m, g * 256:(g + 1) * 256],
                            pb[:, 0:256], waits=[mm], inc=True)
                v_w_tok[("pfree", h, (g * 8 + m) % 2)] = te
                last_b_tokens.append(te)
        P.barrier()
        ctxB.close()

    ctxC = ExitStack()

    def sbC(name, shape, dt):
        return ctxC.enter_context(nc.sbuf_tensor(name, shape, dt))

    masks_sb = sbC("masks_sb", [128, 2, 8, 512], BF16)
    ropeq_sb = sbC("ropeq_sb", [128, 2, 2, 512], F32)
    wqb = [sbC(f"wqb{i}", [128, NC_, 128], BF16) for i in range(2)]
    qbuf = [sbC(f"qbuf{i}", [128, 1024], BF16) for i in range(2)]
    qsq = sbC("qsq", [128, 1024], BF16)
    NPS = 4
    probs = [sbC(f"probs{i}", [128, 512], BF16) for i in range(NPS)]
    rb = sbC("rb", [128, 512], F32)
    atout = [sbC(f"atout{i}", [128, 512], BF16) for i in range(2)]
    psco = [ctxC.enter_context(nc.psum_tensor(f"psco{i}", [128, 512], F32)) for i in range(NPS)]
    pat = ctxC.enter_context(nc.psum_tensor("pat", [128, 512], F32))
    prs = ctxC.enter_context(nc.psum_tensor("prs", [128, 512], F32))
    pQ = [ctxC.enter_context(nc.psum_tensor(f"pQ{i}", [128, 512], F32)) for i in range(2)]

    t_mask = P.dma("sync", masks_sb[:], masks.rearrange("s t p m -> p s t m"))
    t_rq = P.dma("sync", ropeq_sb[:], ropeq.rearrange("s c p m -> p s c m"))
    t_h1o = P.dma("sync", h1_fm[:, :, 512:1024],
                  h1own_scr.rearrange("p (c m) -> p c m", c=NC_))

    ct = {}

    def emit_qprod(hd):
        kh = hd // 4
        wb = wqb[hd % 2]
        qb = qbuf[hd % 2]
        wfree = ct.get(("wqfree", hd % 2))
        tw = P.dma("gpsimd", wb[:],
                   wqkv_bf[:, hd * 128:(hd + 1) * 128]
                   .rearrange("(c p) m -> p c m", p=128),
                   waits=[t_pcq] + ([wfree] if wfree else []))
        precast_wowf(hd)
        yield
        qg_tok = []
        ropetoks = []
        for g in range(2):
            pq = pQ[g]
            pqfree = ct.get(("pqfree", g))
            pcA = pQ[1 - g]
            pcA_free = ct.get(("pqA_free", 1 - g))
            mm = None
            for cc in range(4):
                for c8 in range(8):
                    c = cc * 8 + c8
                    rhs = h1_fm[:, c, 512:1024] if g == 0 else h1_fm[:, c, 0:512]
                    mm = P.emit("tensor", "matmul", pq[:], wb[:, c, :], rhs,
                                start=(c == 0), stop=(c == NC_ - 1),
                                waits=([tw, t_h1o] + ([pqfree] if pqfree else [])) if c == 0 else (),
                                inc=True if c == NC_ - 1 else None)
                yield
            ct[("wqfree", hd % 2)] = mm
            csA = ropeq_sb[:, g, 0, :]
            csB = ropeq_sb[:, g, 1, :]
            qev = qb[0:64, g * 512:(g + 1) * 512]
            qod = qb[64:128, g * 512:(g + 1) * 512]
            qfree = ct.get(("qfree", hd % 2))
            w0 = [mm, t_rq] + ([qfree] if qfree and g == 0 else [])
            t4, (e1, e2) = rope_emit(pq, csA, csB, qev, qod, w0,
                                     pcA, pq, [pcA_free], [])
            yield
            ropetoks.append(t4)
            qg_tok += [e1, e2]
            if g == 0:
                ct[("pqfree", 1)] = e1
            ct[("eA", g)] = e1
            ct[("eB", g)] = e2
            ct[("qready", hd, g)] = (e1, e2)

        tq1 = P.emit("vector", "tensor_mul", qsq[:], qb[:], qb[:],
                     waits=qg_tok, inc=True)
        ta = P.emit("tensor", "matmul", pQ[0][0:1, :], onesb[:, 0:1], qsq[:, 0:512],
                    start=True, stop=True, waits=[tq1, ct[("eA", 1)]], inc=True)
        tb = P.emit("tensor", "matmul", pQ[1][0:1, :], onesb[:, 0:1], qsq[:, 512:1024],
                    start=True, stop=True, waits=[tq1, ct[("eB", 1)]], inc=True)
        yield
        tm1 = P.emit("vector", "reduce_max", brow[0:1, 0:1], pQ[0][0:1, :],
                     axis=AX.X, waits=[ta], inc=True)
        tm2 = P.emit("vector", "reduce_max", brow[0:1, 1:2], pQ[1][0:1, :],
                     axis=AX.X, waits=[tb], inc=True)
        tm3 = P.emit("vector", "tensor_max", brow[0:1, 0:1], brow[0:1, 0:1],
                     brow[0:1, 1:2], waits=[tm2], inc=True)
        tm4 = P.emit("vector", "tensor_scalar", brow[0:1, 2:3], brow[0:1, 0:1],
                     mk2row[0:1, kh:kh + 1], -0.5, ALU.add, ALU.mult,
                     waits=[tm3], inc=True)
        tm6 = P.emit("vector", "tensor_scalar_max", brow[0:1, 2:3], brow[0:1, 2:3],
                     -80.0, waits=[tm4], inc=True)
        tbc = P.emit("tensor", "matmul", pQ[1][:, 0:1], onesf[0:1, 0:128],
                     brow[0:1, 2:3], start=True, stop=True,
                     waits=[tm6, t_onesf, tm2], inc=True)
        tbc2 = P.emit("vector", "tensor_copy", biascol[:, hd:hd + 1],
                      pQ[1][:, 0:1], waits=[tbc], inc=True)
        ct[("pqfree", 0)] = tm1
        ct[("pqA_free", 1)] = tbc2
        ct[("tbc2", hd)] = tbc2

    def emit_scores(hd, s, t, qb):
        kh = hd // 4
        pc = psco[t % NPS]
        pcfree = ct.get(("pscofree", t % NPS))
        tbc2 = ct[("tbc2", hd)]
        w0 = [tbc2] + list(ct[("qready", hd, s)])
        if pcfree is not None:
            w0.append(pcfree)
        return P.emit("tensor", "matmul", pc[:],
                      kfm[:, kh, t * 128:(t + 1) * 128],
                      qb[:, s * 512:(s + 1) * 512],
                      start=True, stop=True, waits=w0, inc=True)

    def emit_attention(hd, feed=None):
        kh = hd // 4
        qb = qbuf[hd % 2]
        LOOK = 3
        vlast = None
        for s in range(2):
            D = DS[s]
            sc_tok = {}
            for t in range(min(LOOK + 1, D)):
                sc_tok[t] = emit_scores(hd, s, t, qb)
            rs_tok = None
            for t in range(D):
                if feed is not None:
                    next(feed, None)
                pc = psco[t % NPS]
                pbuf = probs[t % NPS]
                pbfree = ct.get(("probsfree", t % NPS))
                wexp = [sc_tok[t]]
                if pbfree is not None:
                    wexp.append(pbfree)
                te = P.emit("scalar", "activation", pbuf[:], pc[:], AF.Exp,
                            bias=biascol[:, hd:hd + 1], scale=1.0,
                            waits=wexp, inc=True)
                ct[("pscofree", t % NPS)] = te
                if t + LOOK + 1 < D:
                    sc_tok[t + LOOK + 1] = emit_scores(hd, s, t + LOOK + 1, qb)
                masked = (s == 0) or (t >= 8)
                tok_ready = te
                if masked:
                    moff = 0 if s == 0 else 8
                    wm = [te, t_mask]
                    if pbfree is not None:
                        wm.append(pbfree)
                    tok_ready = P.emit("vector", "tensor_mul", pbuf[:], pbuf[:],
                                       masks_sb[:, s, t - moff, :],
                                       waits=wm, inc=True)
                w0 = [tok_ready, te]
                if t == 0:
                    if ("patfree2",) in ct:
                        w0.append(ct[("patfree2",)])
                    if ("prsfree2",) in ct:
                        w0.append(ct[("prsfree2",)])
                vlast = P.emit("tensor", "matmul", pat[:],
                               vtok[:, t, kh * 128:(kh + 1) * 128],
                               pbuf[:],
                               start=(t == 0), stop=(t == D - 1),
                               waits=w0,
                               inc=True if t == D - 1 else None)
                rs_tok = P.emit("tensor", "matmul", prs[:],
                                onesb[:], pbuf[:],
                                start=(t == 0), stop=(t == D - 1),
                                inc=True)
                ct[("probsfree", t % NPS)] = rs_tok
            tr = P.emit("vector", "reciprocal", rrow[0:1, :], prs[0:1, :],
                        waits=[rs_tok], inc=True)
            ct[("prsfree2",)] = tr
            bb = D % NPS
            tb2 = P.emit("tensor", "matmul", psco[bb][:], onesf[0:1, 0:128],
                         rrow[0:1, :], start=True, stop=True,
                         waits=[tr, ct[("pscofree", bb)]], inc=True)
            tc = P.emit("vector", "tensor_copy", rb[:], psco[bb][:],
                        waits=[tb2], inc=True)
            ct[("pscofree", bb)] = tc
            ao = atout[s]
            aofree = ct.get(("aofree", s))
            tm = P.emit("vector", "tensor_mul", ao[:], pat[:], rb[:],
                        waits=[tc, vlast] + ([aofree] if aofree else []), inc=True)
            ct[("patfree2",)] = tm
            ts_ = P.dma("sync", attn_scr[hd, :, s * 512:(s + 1) * 512], ao[:],
                        waits=[tm])
            ct[("aofree", s)] = ts_
        ct[("qfree", hd % 2)] = vlast

    gen = emit_qprod(0)
    for _ in gen:
        pass
    for hd in range(NH):
        feed = emit_qprod(hd + 1) if hd + 1 < NH else None
        emit_attention(hd, feed)
        if feed is not None:
            for _ in feed:
                pass
    P.barrier()
    ctxC.close()
    ctx1.close()

    ctxD = ExitStack()

    def sbD(name, shape, dt):
        return ctxD.enter_context(nc.sbuf_tensor(name, shape, dt))

    wob = [sbD(f"wob{i}", [128, NC_, 512], BF16) for i in range(3)]
    atb = [sbD(f"atb{i}", [128, NC_, 512], BF16) for i in range(2)]
    xrb = [sbD(f"xrb{i}", [128, 512], F32) for i in range(2)]
    x1b = [sbD(f"x1b{i}", [128, 512], F32) for i in range(2)]
    pD = [ctxD.enter_context(nc.psum_tensor(f"pD{i}", [128, 512], F32)) for i in range(2)]

    dt_ = {}
    it = 0
    tal = [P.dma("sync", atb[hf][:],
                 attn_scr[:, :, hf * 512:(hf + 1) * 512]
                 .rearrange("c p m -> p c m")) for hf in range(2)]
    for n in range(8):
        wb = wob[n % 3]
        wfree = dt_.get(("wofree", n % 3))
        tws = [P.dma("gpsimd", wb[:],
                     wo_bf[:, n * 512:(n + 1) * 512]
                     .rearrange("(c p) m -> p c m", p=128),
                     waits=[t_pcwo] + ([wfree] if wfree else []))]
        for hf in range(2):
            ab = atb[hf]
            for ml in range(4):
                m = hf * 4 + ml
                pb = pD[it % 2]
                pfree = dt_.get(("pDfree", it % 2))
                mm = None
                for c in range(NC_):
                    w0 = []
                    if c == 0:
                        w0 = [tal[hf], tws[-1]] + ([pfree] if pfree else [])
                    mm = P.emit("tensor", "matmul", pb[:],
                                ab[:, c, ml * 128:(ml + 1) * 128],
                                wb[:, c, :],
                                start=(c == 0), stop=(c == NC_ - 1),
                                waits=w0, inc=True if c == NC_ - 1 else None)
                dt_[("wofree", n % 3)] = mm
                xb_ = xrb[it % 2]
                xfree = dt_.get(("xrfree", it % 2))
                tx = P.dma("sync", xb_[:],
                           xh[hf, ml * 128:(ml + 1) * 128, n * 512:(n + 1) * 512],
                           waits=[xfree] if xfree else [])
                ob = x1b[it % 2]
                ofree = dt_.get(("x1free", it % 2))
                tadd = P.emit("vector", "tensor_add", ob[:], pb[:], xb_[:],
                              waits=[mm, tx] + ([ofree] if ofree else []), inc=True)
                dt_[("pDfree", it % 2)] = tadd
                dt_[("xrfree", it % 2)] = tadd
                tst = P.dma("sync", x1_scr[m * 128:(m + 1) * 128, n * 512:(n + 1) * 512],
                            ob[:], waits=[tadd])
                dt_[("x1free", it % 2)] = tst
                it += 1
    P.barrier()
    ctxD.close()

    ctxE = ExitStack()

    def sbE(name, shape, dt):
        return ctxE.enter_context(nc.sbuf_tensor(name, shape, dt))

    h2_fm = sbE("h2_fm", [128, NC_, 1024], BF16)
    subE = {
        "xt": [sbE(f"xtE{i}", [128, 4096], F32) for i in range(2)],
        "h1t": [sbE(f"h1tE{i}", [128, 4096], BF16) for i in range(2)],
        "nmr": sbE("nmrE", [128, 2], F32),
        "nmean": sbE("nmeanE", [128, 2], F32),
        "varsum": sbE("varsumE", [128, 2], F32),
        "stdv": sbE("stdvE", [128, 2], F32),
        "rstd": sbE("rstdE", [128, 2], F32),
        "pA": [ctxE.enter_context(nc.psum_tensor(f"pAE{i}", [128, 1024], BF16)) for i in range(2)],
    }
    wfb = [sbE(f"wfb{i}", [128, NC_, 512], BF16) for i in range(2)]
    biasb = sbE("biasb", [128, 4096], BF16)
    x1r = [sbE(f"x1r{i}", [128, 512], F32) for i in range(2)]
    outb = [sbE(f"outb{i}", [128, 512], F32) for i in range(2)]
    pE = [ctxE.enter_context(nc.psum_tensor(f"pE{i}", [128, 512], F32)) for i in range(2)]

    t_bias = P.dma("gpsimd", biasb[:], bias8)
    et = {"it": 0}

    def load_wf(n):
        wfree = et.get(("wffree", n % 2))
        return P.dma("gpsimd", wfb[n % 2][:],
                     wf_bf[:, n * 512:(n + 1) * 512]
                     .rearrange("(c p) m -> p c m", p=128),
                     waits=[t_pcwf] + ([wfree] if wfree else []))

    def emit_e2(n, m, tw, toks):
        it = et["it"]
        wb = wfb[n % 2]
        pb = pE[it % 2]
        pfree = et.get(("pEfree", it % 2))
        mm = None
        for c in range(NC_):
            w0 = []
            if c == 0:
                w0 = [tw]
                if pfree:
                    w0.append(pfree)
                w0 += [toks[("evac", m, cc)] for cc in range(4)]
            mm = P.emit("tensor", "matmul", pb[:],
                        h2_fm[:, c, m * 128:(m + 1) * 128],
                        wb[:, c, :],
                        start=(c == 0), stop=False,
                        waits=w0, inc=None)
        mm = P.emit("tensor", "matmul", pb[:], onesb[0:1, :],
                    biasb[0:1, n * 512:(n + 1) * 512],
                    start=False, stop=True, waits=[t_bias], inc=True)
        et[("wffree", n % 2)] = mm
        xb_ = x1r[it % 2]
        xfree = et.get(("x1rfree", it % 2))
        tx = P.dma("sync", xb_[:],
                   x1_scr[m * 128:(m + 1) * 128, n * 512:(n + 1) * 512],
                   waits=[xfree] if xfree else [])
        ob = outb[it % 2]
        ofree = et.get(("outfree", it % 2))
        tst2 = P.emit("vector", "scalar_tensor_tensor", ob[:], pb[:], 0.01,
                      xb_[:], ALU.mult, ALU.add,
                      waits=[mm, tx] + ([ofree] if ofree else []), inc=True)
        et[("pEfree", it % 2)] = tst2
        et[("x1rfree", it % 2)] = tst2
        tso = P.dma("sync", out_sh[m * 128:(m + 1) * 128, n * 512:(n + 1) * 512],
                    ob[:], waits=[tst2])
        et[("outfree", it % 2)] = tso
        et["it"] = it + 1

    tw0 = load_wf(0)
    toksE = ln_phase(lambda i: x1_scr[i * 128:(i + 1) * 128, :], 8, h2_fm, 0,
                     subE, after_tile=lambda i, toks: emit_e2(0, i, tw0, toks))
    for n in range(1, 8):
        tw = load_wf(n)
        for m in range(8):
            emit_e2(n, m, tw, toksE)

    P.emit("sync", "wait_ge_final")

    for s in P.sems.values():
        s.handle = nc.alloc_semaphore(name=s.name)

    final_waits = [(P.sems[n], P.sems[n].count) for n in sorted(P.store_sems)]

    with nc.Block() as block:
        @block.sync
        def _(sync):
            for op in P.ops["sync"]:
                if op.method == "wait_ge_final":
                    for (s, v) in final_waits:
                        sync.wait_ge(s.handle, v)
                    continue
                for (s, v) in op.waits:
                    sync.wait_ge(s.handle, v)
                if op.method == "wait_ge":
                    continue
                inst = getattr(sync, op.method)(*op.args, **op.kwargs)
                for (s, nn) in op.incs:
                    inst.then_inc(s.handle, nn)

        @block.gpsimd
        def _(gpsimd):
            _replay(P, "gpsimd", gpsimd)

        @block.tensor
        def _(tensor):
            _replay(P, "tensor", tensor)

        @block.vector
        def _(vector):
            _replay(P, "vector", vector)

        @block.scalar
        def _(scalar):
            _replay(P, "scalar", scalar)

    ctxE.close()
    ctx.close()
    return nc



def _perm_cols():
    perm = []
    for hd in range(NH):
        base = hd * HD
        perm += [base + 2 * j for j in range(64)] + [base + 2 * j + 1 for j in range(64)]
    for kh in range(NKV):
        base = 4096 + kh * HD
        perm += [base + 2 * j for j in range(64)] + [base + 2 * j + 1 for j in range(64)]
    perm += list(range(5120, 6144))
    return np.array(perm, dtype=np.int64)


def _core_meta(p):
    own = [0, 3] if p == 0 else [1, 2]
    other = [1, 2] if p == 0 else [0, 3]
    return own, other


def _rope_tables(pos):
    j = np.arange(64, dtype=np.float64)
    inv = 1.0 / (10000.0 ** (j / 64.0))
    ang = pos[:, None].astype(np.float64) * inv[None, :]
    cos = (np.cos(ang).T * S4).astype(np.float32)
    sin = (np.sin(ang).T * S4).astype(np.float32)
    csA = np.concatenate([cos, sin], axis=0)
    csB = np.concatenate([sin, cos], axis=0)
    return np.stack([csA, csB])


def kernel(L0_attn_x, L0_attn_w_qkv, L0_attn_w_o, L0_ffn_quantized_weight,
           L0_ffn_quantized_bias):
    x = np.asarray(L0_attn_x, dtype=np.float32)
    wqkv = np.ascontiguousarray(np.asarray(L0_attn_w_qkv, np.float32)[:, _perm_cols()])
    wo = np.ascontiguousarray(np.asarray(L0_attn_w_o, np.float32))
    wf = np.ascontiguousarray(np.asarray(L0_ffn_quantized_weight, np.int8).T)
    bias8 = np.ascontiguousarray(
        np.broadcast_to(np.asarray(L0_ffn_quantized_bias, np.int8).reshape(1, 4096),
                        (128, 4096)))
    rcomb = np.zeros((128, 128), np.float32)
    for j in range(64):
        rcomb[j, j] = 1.0
        rcomb[j + 64, j] = -1.0
        rcomb[j, j + 64] = 1.0
        rcomb[j + 64, j + 64] = 1.0
    cid = np.stack([np.eye(128), np.ones((128, 128)), rcomb]).astype(ml_dtypes.bfloat16)
    onesf = np.ones((128, 128), np.float32)

    nc = build_nc()
    in_maps = []
    metas = []
    for c in range(8):
        b, p = c >> 1, c & 1
        own, other = _core_meta(p)
        xb = x[b]
        xh = np.stack([
            np.concatenate([xb[own[h] * 512:(own[h] + 1) * 512],
                            xb[other[h] * 512:(other[h] + 1) * 512]])
            for h in range(2)])
        posk = np.concatenate([
            own[0] * 512 + np.arange(512), other[0] * 512 + np.arange(512),
            own[1] * 512 + np.arange(512), other[1] * 512 + np.arange(512)])
        masks = np.zeros((2, 8, 128, 512), np.float32)
        for s in range(2):
            posq = own[s] * 512 + np.arange(512)
            t0 = 0 if s == 0 else 8
            for jr in range(8):
                t = t0 + jr
                masks[s, jr] = (posk[t * 128:(t + 1) * 128, None] <= posq[None, :])
        ropek = np.stack([_rope_tables(posk[:1024]), _rope_tables(posk[1024:])])
        ropeq = np.stack([_rope_tables(own[s] * 512 + np.arange(512)) for s in range(2)])
        in_maps.append({
            "xh": np.ascontiguousarray(xh),
            "wqkv": wqkv, "wo": wo, "wf": wf, "bias8": bias8,
            "masks": masks.astype(ml_dtypes.bfloat16),
            "ropek": np.ascontiguousarray(ropek),
            "ropeq": np.ascontiguousarray(ropeq),
            "cid": cid, "onesf": onesf,
        })
        metas.append((b, own))

    res = run_bass_kernel_spmd(nc, in_maps, core_ids=list(range(8)))
    out = np.empty((4, SEQ, D_MODEL), np.float32)
    for c in range(8):
        b, own = metas[c]
        sh = res.results[c]["out_sh"]
        out[b, own[0] * 512:(own[0] + 1) * 512] = sh[0:512]
        out[b, own[1] * 512:(own[1] + 1) * 512] = sh[512:1024]
    return out


# version 20
# speedup vs baseline: 1.2552x; 1.0591x over previous
import numpy as np
import ml_dtypes

import concourse.bass as bass
import concourse.mybir as mybir
from concourse.bass_utils import run_bass_kernel_spmd
from contextlib import ExitStack
from concourse.bass_primitives import MemorySpace

F32 = mybir.dt.float32
BF16 = mybir.dt.bfloat16
I8 = mybir.dt.int8
AF = mybir.ActivationFunctionType
ALU = mybir.AluOpType
AX = mybir.AxisListType

D_MODEL = 4096
SEQ = 2048
NH = 32
NKV = 8
HD = 128
NC_ = 32
DS = (8, 16)
S4 = float(128.0 ** -0.25)
EPS = 1e-5

ENGS = ("sync", "gpsimd", "tensor", "vector", "scalar")


class Sem:
    __slots__ = ("name", "count", "handle")

    def __init__(self, name):
        self.name = name
        self.count = 0
        self.handle = None


class Op:
    __slots__ = ("method", "args", "kwargs", "waits", "incs")

    def __init__(self, method, args, kwargs, waits, incs):
        self.method = method
        self.args = args
        self.kwargs = kwargs
        self.waits = waits
        self.incs = incs


class Plan:
    def __init__(self):
        self.ops = {e: [] for e in ENGS}
        self.sems = {}
        self.wm = {e: {} for e in ENGS}
        self.psem = {
            "sync": self.sem("sLD"),
            "gpsimd": self.sem("sGP"),
            "tensor": self.sem("sPE"),
            "vector": self.sem("sDV"),
            "scalar": self.sem("sAC"),
        }
        self.sst = self.sem("sST")
        self.store_sems = set()

    def sem(self, name):
        if name not in self.sems:
            self.sems[name] = Sem(name)
        return self.sems[name]

    def emit(self, eng, method, *args, waits=(), inc=None, inc_n=1, **kwargs):
        wl = []
        wm = self.wm[eng]
        for wv in waits:
            if wv is None:
                continue
            s, v = wv
            if wm.get(s.name, 0) >= v:
                continue
            wm[s.name] = v
            wl.append((s, v))
        il = []
        tok = None
        if inc is not None:
            s = self.psem[eng] if inc is True else inc
            s.count += inc_n
            il.append((s, inc_n))
            tok = (s, s.count)
        self.ops[eng].append(Op(method, args, kwargs, wl, il))
        return tok

    def dma(self, eng, out, in_, waits=(), sem=None, **kw):
        if sem is None:
            is_store = out.space != MemorySpace.SBUF
            name = (in_ if is_store else out).tensor.name
            sem = self.sem("b_" + name)
            if is_store:
                self.store_sems.add(sem.name)
        return self.emit(eng, "dma_start", waits=waits, inc=sem, inc_n=16,
                         out=out, in_=in_, **kw)

    def barrier(self):
        for e in ENGS:
            ops = self.ops[e]
            last = None
            for op in reversed(ops):
                if op.method not in ("wait_ge", "wait_ge_final"):
                    last = op
                    break
            if last is None:
                continue
            if not last.incs:
                s = self.psem[e]
                s.count += 1
                last.incs.append((s, 1))
        targets = [(s, s.count) for s in self.sems.values() if s.count > 0]
        for e in ENGS:
            wm = self.wm[e]
            for (s, v) in targets:
                if v > 0 and wm.get(s.name, 0) < v:
                    wm[s.name] = v
                    self.ops[e].append(Op("wait_ge", (), {}, [(s, v)], []))


def _replay(plan, eng_name, eng):
    for op in plan.ops[eng_name]:
        for (s, v) in op.waits:
            eng.wait_ge(s.handle, v)
        if op.method == "wait_ge":
            continue
        inst = getattr(eng, op.method)(*op.args, **op.kwargs)
        for (s, n) in op.incs:
            inst.then_inc(s.handle, n)


def build_nc():
    nc = bass.Bass("TRN2", target_bir_lowering=False)
    P = Plan()
    sLD = P.psem["sync"]
    sGP = P.psem["gpsimd"]
    sPE = P.psem["tensor"]
    sDV = P.psem["vector"]
    sAC = P.psem["scalar"]
    sST = P.sst

    xh = nc.dram_tensor("xh", [2, 1024, D_MODEL], F32, kind="ExternalInput").ap()
    wqkv = nc.dram_tensor("wqkv", [D_MODEL, 6144], F32, kind="ExternalInput").ap()
    wo = nc.dram_tensor("wo", [D_MODEL, D_MODEL], F32, kind="ExternalInput").ap()
    wf = nc.dram_tensor("wf", [D_MODEL, D_MODEL], I8, kind="ExternalInput").ap()
    bias8 = nc.dram_tensor("bias8", [128, D_MODEL], I8, kind="ExternalInput").ap()
    masks = nc.dram_tensor("masks", [2, 8, 128, 512], BF16, kind="ExternalInput").ap()
    ropek = nc.dram_tensor("ropek", [2, 2, 128, 1024], F32, kind="ExternalInput").ap()
    ropeq = nc.dram_tensor("ropeq", [2, 2, 128, 512], F32, kind="ExternalInput").ap()
    cid = nc.dram_tensor("cid", [3, 128, 128], BF16, kind="ExternalInput").ap()
    onesf_d = nc.dram_tensor("onesf", [128, 128], F32, kind="ExternalInput").ap()
    out_sh = nc.dram_tensor("out_sh", [1024, D_MODEL], F32, kind="ExternalOutput").ap()
    attn_scr = nc.dram_tensor("attn_scr", [NH, 128, 1024], BF16, kind="Internal").ap()
    x1_scr = nc.dram_tensor("x1_scr", [1024, D_MODEL], F32, kind="Internal").ap()
    h1own_scr = nc.dram_tensor("h1own_scr", [128, NC_ * 512], BF16, kind="Internal").ap()
    wqkv_bf = nc.dram_tensor("wqkv_bf", [D_MODEL, 6144], BF16, kind="Internal").ap()
    wo_bf = nc.dram_tensor("wo_bf", [D_MODEL, D_MODEL], BF16, kind="Internal").ap()
    wf_bf = nc.dram_tensor("wf_bf", [D_MODEL, D_MODEL], BF16, kind="Internal").ap()

    ctx = ExitStack()

    def sb(name, shape, dt):
        return ctx.enter_context(nc.sbuf_tensor(name, shape, dt))

    idn = sb("idn", [128, 128], BF16)
    onesb = sb("onesb", [128, 128], BF16)
    onesf = sb("onesf_t", [128, 128], F32)
    rcomb = sb("rcomb", [128, 128], BF16)
    epscol = sb("epscol", [128, 1], F32)
    biascol = sb("biascol", [128, NH], F32)
    mk2row = sb("mk2row", [1, NKV], F32)
    rrow = sb("rrow", [1, 512], F32)
    brow = sb("brow", [1, 4], F32)

    s_pckv = P.sem("pc_kv")
    s_pcq = P.sem("pc_q")
    s_pcwo = P.sem("pc_wo")
    s_pcwf = P.sem("pc_wf")
    for i in range(2):
        P.dma("gpsimd", wqkv_bf[:, 4096 + i * 1024:4096 + (i + 1) * 1024],
              wqkv[:, 4096 + i * 1024:4096 + (i + 1) * 1024], sem=s_pckv)
    t_pckv = (s_pckv, s_pckv.count)
    t_pcq = (s_pcq, 16 * 16)
    t_pcwo = (s_pcwo, 16 * 16)
    t_pcwf = (s_pcwf, 16 * 16)

    def precast_q(i):
        P.dma("gpsimd", wqkv_bf[:, i * 256:(i + 1) * 256],
              wqkv[:, i * 256:(i + 1) * 256], sem=s_pcq)

    def precast_wowf(i):
        if i < 16:
            P.dma("gpsimd", wo_bf[:, i * 256:(i + 1) * 256],
                  wo[:, i * 256:(i + 1) * 256], sem=s_pcwo)
        elif i < 32:
            j = i - 16
            P.dma("gpsimd", wf_bf[:, j * 256:(j + 1) * 256],
                  wf[:, j * 256:(j + 1) * 256], sem=s_pcwf)

    t_idn = P.dma("sync", idn[:], cid[0])
    t_ones = P.dma("sync", onesb[:], cid[1])
    t_rcomb = P.dma("sync", rcomb[:], cid[2])
    t_onesf = P.dma("sync", onesf[:], onesf_d)
    P.emit("vector", "memset", epscol[:], EPS, inc=True)

    ctx1 = ExitStack()

    def sb1(name, shape, dt):
        return ctx1.enter_context(nc.sbuf_tensor(name, shape, dt))

    h1_fm = sb1("h1_fm", [128, NC_, 1024], BF16)
    kfm = sb1("kfm", [128, NKV, 2048], BF16)
    vtok = sb1("vtok", [128, 16, 1024], BF16)
    ropek_sb = sb1("ropek_sb", [128, 2, 1024], F32)
    tmpb = sb1("tmpb", [128, 2, 512], BF16)

    rope_st = {"c1": None, "c2": None}

    def rope_emit(pb, csA, csB, dst_ev, dst_od, first_waits, pcA, pcB, freeA, freeB):
        st = rope_st
        w1 = list(first_waits)
        if st["c1"]:
            w1.append(st["c1"])
        tA = P.emit("vector", "tensor_mul", tmpb[:, 0, :], pb[:], csA,
                    waits=w1, inc=True)
        w2 = [st["c2"]] if st["c2"] else []
        t4 = P.emit("vector", "tensor_mul", tmpb[:, 1, :], pb[:], csB,
                    waits=w2, inc=True)
        wc = [tA, t4, t_rcomb] + [t for t in freeA if t]
        mm1 = P.emit("tensor", "matmul", pcA[:], rcomb[:], tmpb[:, 0, :],
                     start=True, stop=True, waits=wc, inc=True)
        mm2 = P.emit("tensor", "matmul", pcB[:], rcomb[:], tmpb[:, 1, :],
                     start=True, stop=True,
                     waits=[t4] + [t for t in freeB if t], inc=True)
        e1 = P.emit("vector", "tensor_copy", dst_ev, pcA[0:64, :],
                    waits=[mm1], inc=True)
        e2 = P.emit("vector", "tensor_copy", dst_od, pcB[64:128, :],
                    waits=[mm2], inc=True)
        st.update(c1=mm1, c2=mm2)
        return t4, (e1, e2)

    def rope_emit_ct(pb, csA, csB, dst, first_waits, efree):
        st = rope_st
        w1 = list(first_waits)
        if st["c1"]:
            w1.append(st["c1"])
        tA = P.emit("vector", "tensor_mul", tmpb[:, 0, :], pb[:], csA,
                    waits=w1, inc=True)
        w2 = [st["c2"]] if st["c2"] else []
        t4 = P.emit("vector", "tensor_mul", tmpb[:, 1, :], pb[:], csB,
                    waits=w2, inc=True)
        mm1 = P.emit("tensor", "matmul", pb[0:64, :], rcomb[:, 0:64],
                     tmpb[:, 0, :], start=True, stop=True,
                     tile_position=(0, 0),
                     waits=[tA, t4, t_rcomb] + [t for t in efree if t], inc=True)
        mm2 = P.emit("tensor", "matmul", pb[64:128, :], rcomb[:, 64:128],
                     tmpb[:, 1, :], start=True, stop=True,
                     tile_position=(0, 64), waits=[t4], inc=True)
        e = P.emit("vector", "tensor_copy", dst, pb[:],
                   waits=[mm1, mm2], inc=True)
        st.update(c1=mm1, c2=mm2)
        return e

    def ln_phase(src_rows_ap, n_tiles, dst_fm, dst_cols_base, sub, after_tile=None):
        xt = sub["xt"]
        h1t = sub["h1t"]
        nmr = sub["nmr"]
        nmean = sub["nmean"]
        varsum = sub["varsum"]
        stdv = sub["stdv"]
        rstd = sub["rstd"]
        pA = sub["pA"]
        toks = sub.setdefault("toks", {})
        for i in range(n_tiles):
            bs = i % 2
            src = src_rows_ap(i)
            prev = toks.get(("free", bs))
            waits = prev if prev else []
            tl = P.dma("sync", xt[bs][:], src, waits=waits)
            t1 = P.emit("vector", "reduce_sum", nmr[:, bs:bs + 1], xt[bs][:],
                        axis=AX.X, negate=True, waits=[tl], inc=True)
            t2 = P.emit("vector", "tensor_scalar_mul", nmean[:, bs:bs + 1],
                        nmr[:, bs:bs + 1], 1.0 / D_MODEL, waits=[t1], inc=True)
            wsq = [t2]
            if ("sqfree", bs) in toks:
                wsq.append(toks[("sqfree", bs)])
            t3 = P.emit("scalar", "activation", h1t[bs][:, 0:4096], xt[bs][:],
                        AF.Square, bias=nmean[:, bs:bs + 1], scale=1.0,
                        accum_out=varsum[:, bs:bs + 1], waits=wsq, inc=True)
            t4 = P.emit("scalar", "activation", stdv[:, bs:bs + 1],
                        varsum[:, bs:bs + 1], AF.Sqrt,
                        bias=epscol[:, 0:1], scale=1.0 / D_MODEL,
                        waits=[t3], inc=True)
            t5 = P.emit("vector", "reciprocal", rstd[:, bs:bs + 1],
                        stdv[:, bs:bs + 1], waits=[t4], inc=True)
            t6 = P.emit("vector", "tensor_scalar", h1t[bs][:, 0:4096], xt[bs][:],
                        nmean[:, bs:bs + 1], rstd[:, bs:bs + 1],
                        ALU.add, ALU.mult, waits=[t5, t3], inc=True)
            toks[("free", bs)] = [t6, t3]
            for cc in range(4):
                pa = pA[cc % 2]
                wtp = [t6]
                if ("pafree", cc % 2) in toks:
                    wtp.append(toks[("pafree", cc % 2)])
                tp = None
                for c8 in range(8):
                    c = cc * 8 + c8
                    tp = P.emit("tensor", "transpose",
                                pa[:, c8 * 128:(c8 + 1) * 128],
                                h1t[bs][:, c * 128:(c + 1) * 128], idn[:],
                                waits=(wtp + [t_idn]) if c8 == 0 else (),
                                inc=True if c8 == 7 else None)
                col = dst_cols_base + i * 128
                te = P.emit("vector", "tensor_copy",
                            dst_fm[:, cc * 8:(cc + 1) * 8, col:col + 128],
                            pa[:].rearrange("p (c m) -> p c m", c=8),
                            waits=[tp], inc=True)
                toks[("pafree", cc % 2)] = te
                toks[("evac", i, cc)] = te
            toks[("sqfree", bs)] = toks[("evac", i, 3)]
            if after_tile is not None:
                after_tile(i, toks)
        return toks

    kh_w_tok = {}
    v_w_tok = {}
    last_b_tokens = []

    for h in range(2):
        ctxA = ExitStack()
        subA = {
            "xt": [ctxA.enter_context(nc.sbuf_tensor(f"xtA{h}_{i}", [128, 4096], F32)) for i in range(2)],
            "h1t": [ctxA.enter_context(nc.sbuf_tensor(f"h1tA{h}_{i}", [128, 4096], BF16)) for i in range(2)],
            "nmr": ctxA.enter_context(nc.sbuf_tensor(f"nmrA{h}", [128, 2], F32)),
            "nmean": ctxA.enter_context(nc.sbuf_tensor(f"nmeanA{h}", [128, 2], F32)),
            "varsum": ctxA.enter_context(nc.sbuf_tensor(f"varsumA{h}", [128, 2], F32)),
            "stdv": ctxA.enter_context(nc.sbuf_tensor(f"stdvA{h}", [128, 2], F32)),
            "rstd": ctxA.enter_context(nc.sbuf_tensor(f"rstdA{h}", [128, 2], F32)),
            "pA": [ctxA.enter_context(nc.psum_tensor(f"pAA{h}_{i}", [128, 1024], BF16)) for i in range(2)],
        }
        toksA = ln_phase(lambda i: xh[h, i * 128:(i + 1) * 128, :], 8,
                         h1_fm, 0, subA)
        if h == 0:
            w = [toksA[("evac", i, cc)] for i in range(4) for cc in range(4)]
            P.dma("sync", h1own_scr.rearrange("p (c m) -> p c m", c=NC_),
                  h1_fm[:, :, 0:512], waits=w)
        P.barrier()
        ctxA.close()

        ctxB = ExitStack()
        wkb = [ctxB.enter_context(nc.sbuf_tensor(f"wkb{h}_{i}", [128, NC_, 128], BF16)) for i in range(2)]
        wvb = [ctxB.enter_context(nc.sbuf_tensor(f"wvb{h}_{i}", [128, NC_, 256], BF16)) for i in range(2)]
        ksq = ctxB.enter_context(nc.sbuf_tensor(f"ksq{h}", [128, 512], BF16))
        pB = [ctxB.enter_context(nc.psum_tensor(f"pB{h}_{i}", [128, 512], F32)) for i in range(2)]
        pK2 = ctxB.enter_context(nc.psum_tensor(f"pK2{h}", [128, 512], F32))
        pRC = ctxB.enter_context(nc.psum_tensor(f"pRC{h}", [128, 1024], F32))

        t_rk = P.dma("sync", ropek_sb[:], ropek[h].rearrange("c p t -> p c t"))

        if h == 0:
            P.emit("vector", "memset", mk2row[:], 0.0, inc=True)

        kl_tok = {}
        for kh in range(NKV):
            wb = wkb[kh % 2]
            wfree = kl_tok.get(("wfree", kh % 2))
            tw = P.dma("gpsimd", wb[:],
                       wqkv_bf[:, 4096 + kh * 128:4096 + (kh + 1) * 128]
                       .rearrange("(c p) m -> p c m", p=128),
                       waits=[t_pckv] + ([wfree] if wfree else []))
            for g in range(2):
                pb = pB[(kh * 2 + g) % 2]
                pfree = kl_tok.get(("pfree", (kh * 2 + g) % 2))
                mm = None
                for c in range(NC_):
                    mm = P.emit("tensor", "matmul", pb[:],
                                wb[:, c, :], h1_fm[:, c, g * 512:(g + 1) * 512],
                                start=(c == 0), stop=(c == NC_ - 1),
                                waits=([tw] + ([pfree] if pfree else [])) if c == 0 else (),
                                inc=True if c == NC_ - 1 else None)
                kl_tok[("wfree", kh % 2)] = mm
                if h == 0:
                    precast_q(kh * 2 + g)
                csA = ropek_sb[:, 0, g * 512:(g + 1) * 512]
                csB = ropek_sb[:, 1, g * 512:(g + 1) * 512]
                col = h * 1024 + g * 512
                kc = kfm[:, kh, col:col + 512]
                kev = kfm[0:64, kh, col:col + 512]
                kod = kfm[64:128, kh, col:col + 512]
                rcfree = kl_tok.get(("rcfree",), (None, None))
                t4, (e1, e2) = rope_emit(pb, csA, csB, kev, kod, [mm, t_rk],
                                         pRC[:, 0:512], pRC[:, 512:1024],
                                         [rcfree[0]], [rcfree[1]])
                kl_tok[("pfree", (kh * 2 + g) % 2)] = t4
                kl_tok[("rcfree",)] = (e1, e2)
                t7 = P.emit("vector", "tensor_mul", ksq[:], kc, kc,
                            waits=[e1, e2], inc=True)
                pk2free = kl_tok.get(("pk2free",))
                t8 = P.emit("tensor", "matmul", pK2[0:1, :], onesb[:, 0:1], ksq[:],
                            start=True, stop=True,
                            waits=[t7, t_ones] + ([pk2free] if pk2free else []),
                            inc=True)
                t9 = P.emit("vector", "reduce_max", brow[0:1, 0:1], pK2[0:1, :],
                            axis=AX.X, waits=[t8], inc=True)
                t10 = P.emit("vector", "tensor_max", mk2row[0:1, kh:kh + 1],
                             mk2row[0:1, kh:kh + 1], brow[0:1, 0:1], waits=[t9], inc=True)
                kl_tok[("k2", kh, g)] = t10
                kl_tok[("pk2free",)] = t9

        for g in range(4):
            wb = wvb[g % 2]
            wfree = v_w_tok.get(("wfree", h, g % 2))
            tw = P.dma("gpsimd", wb[:],
                       wqkv_bf[:, 5120 + g * 256:5120 + (g + 1) * 256]
                       .rearrange("(c p) m -> p c m", p=128),
                       waits=[t_pckv] + ([wfree] if wfree else []))
            for m in range(8):
                pb = pB[(g * 8 + m) % 2]
                pfree = v_w_tok.get(("pfree", h, (g * 8 + m) % 2))
                mm = None
                for c in range(NC_):
                    mm = P.emit("tensor", "matmul", pb[:, 0:256],
                                h1_fm[:, c, m * 128:(m + 1) * 128],
                                wb[:, c, :],
                                start=(c == 0), stop=(c == NC_ - 1),
                                waits=([tw] + ([pfree] if pfree else [])) if c == 0 else (),
                                inc=True if c == NC_ - 1 else None)
                v_w_tok[("wfree", h, g % 2)] = mm
                te = P.emit("vector", "tensor_copy",
                            vtok[:, h * 8 + m, g * 256:(g + 1) * 256],
                            pb[:, 0:256], waits=[mm], inc=True)
                v_w_tok[("pfree", h, (g * 8 + m) % 2)] = te
                last_b_tokens.append(te)
        P.barrier()
        ctxB.close()

    ctxC = ExitStack()

    def sbC(name, shape, dt):
        return ctxC.enter_context(nc.sbuf_tensor(name, shape, dt))

    masks_sb = sbC("masks_sb", [128, 2, 8, 512], BF16)
    ropeq_sb = sbC("ropeq_sb", [128, 2, 2, 512], F32)
    wqb = [sbC(f"wqb{i}", [128, NC_, 128], BF16) for i in range(2)]
    qbuf = [sbC(f"qbuf{i}", [128, 1024], BF16) for i in range(2)]
    qsq = sbC("qsq", [128, 1024], BF16)
    NPS = 5
    probs = [sbC(f"probs{i}", [128, 512], BF16) for i in range(NPS)]
    rb = sbC("rb", [128, 512], F32)
    atout = [sbC(f"atout{i}", [128, 512], BF16) for i in range(2)]
    psco = [ctxC.enter_context(nc.psum_tensor(f"psco{i}", [128, 512], F32)) for i in range(NPS)]
    pat = ctxC.enter_context(nc.psum_tensor("pat", [128, 512], F32))
    prs = ctxC.enter_context(nc.psum_tensor("prs", [128, 512], F32))
    pQ = ctxC.enter_context(nc.psum_tensor("pQ", [128, 512], F32))

    t_mask = P.dma("sync", masks_sb[:], masks.rearrange("s t p m -> p s t m"))
    t_rq = P.dma("sync", ropeq_sb[:], ropeq.rearrange("s c p m -> p s c m"))
    t_h1o = P.dma("sync", h1_fm[:, :, 512:1024],
                  h1own_scr.rearrange("p (c m) -> p c m", c=NC_))

    ct = {}

    def emit_qprod(hd):
        kh = hd // 4
        wb = wqb[hd % 2]
        qb = qbuf[hd % 2]
        wfree = ct.get(("wqfree", hd % 2))
        tw = P.dma("gpsimd", wb[:],
                   wqkv_bf[:, hd * 128:(hd + 1) * 128]
                   .rearrange("(c p) m -> p c m", p=128),
                   waits=[t_pcq] + ([wfree] if wfree else []))
        precast_wowf(hd)
        yield
        etoks = []
        for g in range(2):
            pqfree = ct.get(("pqfree",))
            mm = None
            for cc in range(8):
                for c4 in range(4):
                    c = cc * 4 + c4
                    rhs = h1_fm[:, c, 512:1024] if g == 0 else h1_fm[:, c, 0:512]
                    mm = P.emit("tensor", "matmul", pQ[:], wb[:, c, :], rhs,
                                start=(c == 0), stop=(c == NC_ - 1),
                                waits=([tw, t_h1o] + ([pqfree] if pqfree else [])) if c == 0 else (),
                                inc=True if c == NC_ - 1 else None)
                yield
            ct[("wqfree", hd % 2)] = mm
            csA = ropeq_sb[:, g, 0, :]
            csB = ropeq_sb[:, g, 1, :]
            qfree = ct.get(("qfree", hd % 2))
            w0 = [mm, t_rq] + ([qfree] if qfree and g == 0 else [])
            e = rope_emit_ct(pQ, csA, csB, qb[:, g * 512:(g + 1) * 512], w0,
                             [ct.get(("pqe",))])
            ct[("pqfree",)] = e
            ct[("pqe",)] = e
            etoks.append(e)
            ct[("qready", hd, g)] = (e, e)
            yield

        tq1 = P.emit("vector", "tensor_mul", qsq[:], qb[:], qb[:],
                     waits=etoks, inc=True)
        ta = P.emit("tensor", "matmul", pQ[0:1, :], onesb[:, 0:1], qsq[:, 0:512],
                    start=True, stop=True, tile_position=(0, 0),
                    waits=[tq1, etoks[1]], inc=True)
        tb = P.emit("tensor", "matmul", pQ[64:65, :], onesb[:, 0:1], qsq[:, 512:1024],
                    start=True, stop=True, tile_position=(0, 64),
                    waits=[tq1], inc=True)
        yield
        tm1 = P.emit("vector", "reduce_max", brow[0:1, 0:1], pQ[0:1, :],
                     axis=AX.X, waits=[ta], inc=True)
        tm2 = P.emit("vector", "reduce_max", brow[0:1, 1:2], pQ[64:65, :],
                     axis=AX.X, waits=[tb], inc=True)
        tm3 = P.emit("vector", "tensor_max", brow[0:1, 0:1], brow[0:1, 0:1],
                     brow[0:1, 1:2], waits=[tm2], inc=True)
        tm4 = P.emit("vector", "tensor_scalar", brow[0:1, 2:3], brow[0:1, 0:1],
                     mk2row[0:1, kh:kh + 1], -0.5, ALU.add, ALU.mult,
                     waits=[tm3], inc=True)
        tm6 = P.emit("vector", "tensor_scalar_max", brow[0:1, 2:3], brow[0:1, 2:3],
                     -80.0, waits=[tm4], inc=True)
        tbc = P.emit("tensor", "matmul", pQ[:, 0:1], onesf[0:1, 0:128],
                     brow[0:1, 2:3], start=True, stop=True,
                     waits=[tm6, t_onesf, tm1, tm2], inc=True)
        tbc2 = P.emit("vector", "tensor_copy", biascol[:, hd:hd + 1],
                      pQ[:, 0:1], waits=[tbc], inc=True)
        ct[("pqfree",)] = tbc2
        ct[("pqe",)] = tbc2
        ct[("tbc2", hd)] = tbc2

    def emit_scores(hd, s, t, qb):
        kh = hd // 4
        pc = psco[t % NPS]
        pcfree = ct.get(("pscofree", t % NPS))
        tbc2 = ct[("tbc2", hd)]
        w0 = [tbc2] + list(ct[("qready", hd, s)])
        if pcfree is not None:
            w0.append(pcfree)
        return P.emit("tensor", "matmul", pc[:],
                      kfm[:, kh, t * 128:(t + 1) * 128],
                      qb[:, s * 512:(s + 1) * 512],
                      start=True, stop=True, waits=w0, inc=True)

    def emit_attention(hd, feed=None):
        kh = hd // 4
        qb = qbuf[hd % 2]
        LOOK = 4
        vlast = None
        for s in range(2):
            D = DS[s]
            sc_tok = {}
            for t in range(min(LOOK + 1, D)):
                sc_tok[t] = emit_scores(hd, s, t, qb)
            rs_tok = None
            for t in range(D):
                if feed is not None:
                    next(feed, None)
                pc = psco[t % NPS]
                pbuf = probs[t % NPS]
                pbfree = ct.get(("probsfree", t % NPS))
                wexp = [sc_tok[t]]
                if pbfree is not None:
                    wexp.append(pbfree)
                te = P.emit("scalar", "activation", pbuf[:], pc[:], AF.Exp,
                            bias=biascol[:, hd:hd + 1], scale=1.0,
                            waits=wexp, inc=True)
                ct[("pscofree", t % NPS)] = te
                if t + LOOK + 1 < D:
                    sc_tok[t + LOOK + 1] = emit_scores(hd, s, t + LOOK + 1, qb)
                masked = (s == 0) or (t >= 8)
                tok_ready = te
                if masked:
                    moff = 0 if s == 0 else 8
                    wm = [te, t_mask]
                    if pbfree is not None:
                        wm.append(pbfree)
                    tok_ready = P.emit("vector", "tensor_mul", pbuf[:], pbuf[:],
                                       masks_sb[:, s, t - moff, :],
                                       waits=wm, inc=True)
                w0 = [tok_ready, te]
                if t == 0:
                    if ("patfree2",) in ct:
                        w0.append(ct[("patfree2",)])
                    if ("prsfree2",) in ct:
                        w0.append(ct[("prsfree2",)])
                vlast = P.emit("tensor", "matmul", pat[:],
                               vtok[:, t, kh * 128:(kh + 1) * 128],
                               pbuf[:],
                               start=(t == 0), stop=(t == D - 1),
                               waits=w0,
                               inc=True if t == D - 1 else None)
                rs_tok = P.emit("tensor", "matmul", prs[:],
                                onesb[:], pbuf[:],
                                start=(t == 0), stop=(t == D - 1),
                                inc=True)
                ct[("probsfree", t % NPS)] = rs_tok
            tr = P.emit("vector", "reciprocal", rrow[0:1, :], prs[0:1, :],
                        waits=[rs_tok], inc=True)
            ct[("prsfree2",)] = tr
            bb = D % NPS
            tb2 = P.emit("tensor", "matmul", psco[bb][:], onesf[0:1, 0:128],
                         rrow[0:1, :], start=True, stop=True,
                         waits=[tr, ct[("pscofree", bb)]], inc=True)
            tc = P.emit("vector", "tensor_copy", rb[:], psco[bb][:],
                        waits=[tb2], inc=True)
            ct[("pscofree", bb)] = tc
            ao = atout[s]
            aofree = ct.get(("aofree", s))
            tm = P.emit("vector", "tensor_mul", ao[:], pat[:], rb[:],
                        waits=[tc, vlast] + ([aofree] if aofree else []), inc=True)
            ct[("patfree2",)] = tm
            ts_ = P.dma("sync", attn_scr[hd, :, s * 512:(s + 1) * 512], ao[:],
                        waits=[tm])
            ct[("aofree", s)] = ts_
        ct[("qfree", hd % 2)] = vlast

    gen = emit_qprod(0)
    for _ in gen:
        pass
    for hd in range(NH):
        feed = emit_qprod(hd + 1) if hd + 1 < NH else None
        emit_attention(hd, feed)
        if feed is not None:
            for _ in feed:
                pass
    P.barrier()
    ctxC.close()
    ctx1.close()

    ctxD = ExitStack()

    def sbD(name, shape, dt):
        return ctxD.enter_context(nc.sbuf_tensor(name, shape, dt))

    wob = [sbD(f"wob{i}", [128, NC_, 512], BF16) for i in range(3)]
    atb = [sbD(f"atb{i}", [128, NC_, 512], BF16) for i in range(2)]
    xrb = [sbD(f"xrb{i}", [128, 512], F32) for i in range(2)]
    x1b = [sbD(f"x1b{i}", [128, 512], F32) for i in range(2)]
    pD = [ctxD.enter_context(nc.psum_tensor(f"pD{i}", [128, 512], F32)) for i in range(2)]

    dt_ = {}
    it = 0
    tal = [P.dma("sync", atb[hf][:],
                 attn_scr[:, :, hf * 512:(hf + 1) * 512]
                 .rearrange("c p m -> p c m")) for hf in range(2)]
    for n in range(8):
        wb = wob[n % 3]
        wfree = dt_.get(("wofree", n % 3))
        tws = [P.dma("gpsimd", wb[:],
                     wo_bf[:, n * 512:(n + 1) * 512]
                     .rearrange("(c p) m -> p c m", p=128),
                     waits=[t_pcwo] + ([wfree] if wfree else []))]
        for hf in range(2):
            ab = atb[hf]
            for ml in range(4):
                m = hf * 4 + ml
                pb = pD[it % 2]
                pfree = dt_.get(("pDfree", it % 2))
                mm = None
                for c in range(NC_):
                    w0 = []
                    if c == 0:
                        w0 = [tal[hf], tws[-1]] + ([pfree] if pfree else [])
                    mm = P.emit("tensor", "matmul", pb[:],
                                ab[:, c, ml * 128:(ml + 1) * 128],
                                wb[:, c, :],
                                start=(c == 0), stop=(c == NC_ - 1),
                                waits=w0, inc=True if c == NC_ - 1 else None)
                dt_[("wofree", n % 3)] = mm
                xb_ = xrb[it % 2]
                xfree = dt_.get(("xrfree", it % 2))
                tx = P.dma("sync", xb_[:],
                           xh[hf, ml * 128:(ml + 1) * 128, n * 512:(n + 1) * 512],
                           waits=[xfree] if xfree else [])
                ob = x1b[it % 2]
                ofree = dt_.get(("x1free", it % 2))
                tadd = P.emit("vector", "tensor_add", ob[:], pb[:], xb_[:],
                              waits=[mm, tx] + ([ofree] if ofree else []), inc=True)
                dt_[("pDfree", it % 2)] = tadd
                dt_[("xrfree", it % 2)] = tadd
                tst = P.dma("sync", x1_scr[m * 128:(m + 1) * 128, n * 512:(n + 1) * 512],
                            ob[:], waits=[tadd])
                dt_[("x1free", it % 2)] = tst
                it += 1
    P.barrier()
    ctxD.close()

    ctxE = ExitStack()

    def sbE(name, shape, dt):
        return ctxE.enter_context(nc.sbuf_tensor(name, shape, dt))

    h2_fm = sbE("h2_fm", [128, NC_, 1024], BF16)
    subE = {
        "xt": [sbE(f"xtE{i}", [128, 4096], F32) for i in range(2)],
        "h1t": [sbE(f"h1tE{i}", [128, 4096], BF16) for i in range(2)],
        "nmr": sbE("nmrE", [128, 2], F32),
        "nmean": sbE("nmeanE", [128, 2], F32),
        "varsum": sbE("varsumE", [128, 2], F32),
        "stdv": sbE("stdvE", [128, 2], F32),
        "rstd": sbE("rstdE", [128, 2], F32),
        "pA": [ctxE.enter_context(nc.psum_tensor(f"pAE{i}", [128, 1024], BF16)) for i in range(2)],
    }
    wfb = [sbE(f"wfb{i}", [128, NC_, 512], BF16) for i in range(2)]
    biasb = sbE("biasb", [128, 4096], BF16)
    x1r = [sbE(f"x1r{i}", [128, 512], F32) for i in range(2)]
    outb = [sbE(f"outb{i}", [128, 512], F32) for i in range(2)]
    pE = [ctxE.enter_context(nc.psum_tensor(f"pE{i}", [128, 512], F32)) for i in range(2)]

    t_bias = P.dma("gpsimd", biasb[:], bias8)
    et = {"it": 0}

    def load_wf(n):
        wfree = et.get(("wffree", n % 2))
        return P.dma("gpsimd", wfb[n % 2][:],
                     wf_bf[:, n * 512:(n + 1) * 512]
                     .rearrange("(c p) m -> p c m", p=128),
                     waits=[t_pcwf] + ([wfree] if wfree else []))

    def emit_e2(n, m, tw, toks):
        it = et["it"]
        wb = wfb[n % 2]
        pb = pE[it % 2]
        pfree = et.get(("pEfree", it % 2))
        mm = None
        for c in range(NC_):
            w0 = []
            if c == 0:
                w0 = [tw]
                if pfree:
                    w0.append(pfree)
                w0 += [toks[("evac", m, cc)] for cc in range(4)]
            mm = P.emit("tensor", "matmul", pb[:],
                        h2_fm[:, c, m * 128:(m + 1) * 128],
                        wb[:, c, :],
                        start=(c == 0), stop=False,
                        waits=w0, inc=None)
        mm = P.emit("tensor", "matmul", pb[:], onesb[0:1, :],
                    biasb[0:1, n * 512:(n + 1) * 512],
                    start=False, stop=True, waits=[t_bias], inc=True)
        et[("wffree", n % 2)] = mm
        xb_ = x1r[it % 2]
        xfree = et.get(("x1rfree", it % 2))
        tx = P.dma("sync", xb_[:],
                   x1_scr[m * 128:(m + 1) * 128, n * 512:(n + 1) * 512],
                   waits=[xfree] if xfree else [])
        ob = outb[it % 2]
        ofree = et.get(("outfree", it % 2))
        tst2 = P.emit("vector", "scalar_tensor_tensor", ob[:], pb[:], 0.01,
                      xb_[:], ALU.mult, ALU.add,
                      waits=[mm, tx] + ([ofree] if ofree else []), inc=True)
        et[("pEfree", it % 2)] = tst2
        et[("x1rfree", it % 2)] = tst2
        tso = P.dma("sync", out_sh[m * 128:(m + 1) * 128, n * 512:(n + 1) * 512],
                    ob[:], waits=[tst2])
        et[("outfree", it % 2)] = tso
        et["it"] = it + 1

    tw0 = load_wf(0)
    toksE = ln_phase(lambda i: x1_scr[i * 128:(i + 1) * 128, :], 8, h2_fm, 0,
                     subE, after_tile=lambda i, toks: emit_e2(0, i, tw0, toks))
    for n in range(1, 8):
        tw = load_wf(n)
        for m in range(8):
            emit_e2(n, m, tw, toksE)

    P.emit("sync", "wait_ge_final")

    for s in P.sems.values():
        s.handle = nc.alloc_semaphore(name=s.name)

    final_waits = [(P.sems[n], P.sems[n].count) for n in sorted(P.store_sems)]

    with nc.Block() as block:
        @block.sync
        def _(sync):
            for op in P.ops["sync"]:
                if op.method == "wait_ge_final":
                    for (s, v) in final_waits:
                        sync.wait_ge(s.handle, v)
                    continue
                for (s, v) in op.waits:
                    sync.wait_ge(s.handle, v)
                if op.method == "wait_ge":
                    continue
                inst = getattr(sync, op.method)(*op.args, **op.kwargs)
                for (s, nn) in op.incs:
                    inst.then_inc(s.handle, nn)

        @block.gpsimd
        def _(gpsimd):
            _replay(P, "gpsimd", gpsimd)

        @block.tensor
        def _(tensor):
            _replay(P, "tensor", tensor)

        @block.vector
        def _(vector):
            _replay(P, "vector", vector)

        @block.scalar
        def _(scalar):
            _replay(P, "scalar", scalar)

    ctxE.close()
    ctx.close()
    return nc



def _perm_cols():
    perm = []
    for hd in range(NH):
        base = hd * HD
        perm += [base + 2 * j for j in range(64)] + [base + 2 * j + 1 for j in range(64)]
    for kh in range(NKV):
        base = 4096 + kh * HD
        perm += [base + 2 * j for j in range(64)] + [base + 2 * j + 1 for j in range(64)]
    perm += list(range(5120, 6144))
    return np.array(perm, dtype=np.int64)


def _core_meta(p):
    own = [0, 3] if p == 0 else [1, 2]
    other = [1, 2] if p == 0 else [0, 3]
    return own, other


def _rope_tables(pos):
    j = np.arange(64, dtype=np.float64)
    inv = 1.0 / (10000.0 ** (j / 64.0))
    ang = pos[:, None].astype(np.float64) * inv[None, :]
    cos = (np.cos(ang).T * S4).astype(np.float32)
    sin = (np.sin(ang).T * S4).astype(np.float32)
    csA = np.concatenate([cos, sin], axis=0)
    csB = np.concatenate([sin, cos], axis=0)
    return np.stack([csA, csB])


def kernel(L0_attn_x, L0_attn_w_qkv, L0_attn_w_o, L0_ffn_quantized_weight,
           L0_ffn_quantized_bias):
    x = np.asarray(L0_attn_x, dtype=np.float32)
    wqkv = np.ascontiguousarray(np.asarray(L0_attn_w_qkv, np.float32)[:, _perm_cols()])
    wo = np.ascontiguousarray(np.asarray(L0_attn_w_o, np.float32))
    wf = np.ascontiguousarray(np.asarray(L0_ffn_quantized_weight, np.int8).T)
    bias8 = np.ascontiguousarray(
        np.broadcast_to(np.asarray(L0_ffn_quantized_bias, np.int8).reshape(1, 4096),
                        (128, 4096)))
    rcomb = np.zeros((128, 128), np.float32)
    for j in range(64):
        rcomb[j, j] = 1.0
        rcomb[j + 64, j] = -1.0
        rcomb[j, j + 64] = 1.0
        rcomb[j + 64, j + 64] = 1.0
    cid = np.stack([np.eye(128), np.ones((128, 128)), rcomb]).astype(ml_dtypes.bfloat16)
    onesf = np.ones((128, 128), np.float32)

    nc = build_nc()
    in_maps = []
    metas = []
    for c in range(8):
        b, p = c >> 1, c & 1
        own, other = _core_meta(p)
        xb = x[b]
        xh = np.stack([
            np.concatenate([xb[own[h] * 512:(own[h] + 1) * 512],
                            xb[other[h] * 512:(other[h] + 1) * 512]])
            for h in range(2)])
        posk = np.concatenate([
            own[0] * 512 + np.arange(512), other[0] * 512 + np.arange(512),
            own[1] * 512 + np.arange(512), other[1] * 512 + np.arange(512)])
        masks = np.zeros((2, 8, 128, 512), np.float32)
        for s in range(2):
            posq = own[s] * 512 + np.arange(512)
            t0 = 0 if s == 0 else 8
            for jr in range(8):
                t = t0 + jr
                masks[s, jr] = (posk[t * 128:(t + 1) * 128, None] <= posq[None, :])
        ropek = np.stack([_rope_tables(posk[:1024]), _rope_tables(posk[1024:])])
        ropeq = np.stack([_rope_tables(own[s] * 512 + np.arange(512)) for s in range(2)])
        in_maps.append({
            "xh": np.ascontiguousarray(xh),
            "wqkv": wqkv, "wo": wo, "wf": wf, "bias8": bias8,
            "masks": masks.astype(ml_dtypes.bfloat16),
            "ropek": np.ascontiguousarray(ropek),
            "ropeq": np.ascontiguousarray(ropeq),
            "cid": cid, "onesf": onesf,
        })
        metas.append((b, own))

    res = run_bass_kernel_spmd(nc, in_maps, core_ids=list(range(8)))
    out = np.empty((4, SEQ, D_MODEL), np.float32)
    for c in range(8):
        b, own = metas[c]
        sh = res.results[c]["out_sh"]
        out[b, own[0] * 512:(own[0] + 1) * 512] = sh[0:512]
        out[b, own[1] * 512:(own[1] + 1) * 512] = sh[512:1024]
    return out


# version 21 (final)
# speedup vs baseline: 1.2655x; 1.0082x over previous
import numpy as np
import ml_dtypes

import concourse.bass as bass
import concourse.mybir as mybir
from concourse.bass_utils import run_bass_kernel_spmd
from contextlib import ExitStack
from concourse.bass_primitives import MemorySpace

F32 = mybir.dt.float32
BF16 = mybir.dt.bfloat16
I8 = mybir.dt.int8
AF = mybir.ActivationFunctionType
ALU = mybir.AluOpType
AX = mybir.AxisListType

D_MODEL = 4096
SEQ = 2048
NH = 32
NKV = 8
HD = 128
NC_ = 32
DS = (8, 16)
S4 = float(128.0 ** -0.25)
EPS = 1e-5

ENGS = ("sync", "gpsimd", "tensor", "vector", "scalar")


class Sem:
    __slots__ = ("name", "count", "handle")

    def __init__(self, name):
        self.name = name
        self.count = 0
        self.handle = None


class Op:
    __slots__ = ("method", "args", "kwargs", "waits", "incs")

    def __init__(self, method, args, kwargs, waits, incs):
        self.method = method
        self.args = args
        self.kwargs = kwargs
        self.waits = waits
        self.incs = incs


class Plan:
    def __init__(self):
        self.ops = {e: [] for e in ENGS}
        self.sems = {}
        self.wm = {e: {} for e in ENGS}
        self.psem = {
            "sync": self.sem("sLD"),
            "gpsimd": self.sem("sGP"),
            "tensor": self.sem("sPE"),
            "vector": self.sem("sDV"),
            "scalar": self.sem("sAC"),
        }
        self.sst = self.sem("sST")
        self.store_sems = set()

    def sem(self, name):
        if name not in self.sems:
            self.sems[name] = Sem(name)
        return self.sems[name]

    def emit(self, eng, method, *args, waits=(), inc=None, inc_n=1, **kwargs):
        wl = []
        wm = self.wm[eng]
        for wv in waits:
            if wv is None:
                continue
            s, v = wv
            if wm.get(s.name, 0) >= v:
                continue
            wm[s.name] = v
            wl.append((s, v))
        il = []
        tok = None
        if inc is not None:
            s = self.psem[eng] if inc is True else inc
            s.count += inc_n
            il.append((s, inc_n))
            tok = (s, s.count)
        self.ops[eng].append(Op(method, args, kwargs, wl, il))
        return tok

    def dma(self, eng, out, in_, waits=(), sem=None, **kw):
        if sem is None:
            is_store = out.space != MemorySpace.SBUF
            name = (in_ if is_store else out).tensor.name
            sem = self.sem("b_" + name)
            if is_store:
                self.store_sems.add(sem.name)
        return self.emit(eng, "dma_start", waits=waits, inc=sem, inc_n=16,
                         out=out, in_=in_, **kw)

    def barrier(self):
        for e in ENGS:
            ops = self.ops[e]
            last = None
            for op in reversed(ops):
                if op.method not in ("wait_ge", "wait_ge_final"):
                    last = op
                    break
            if last is None:
                continue
            if not last.incs:
                s = self.psem[e]
                s.count += 1
                last.incs.append((s, 1))
        targets = [(s, s.count) for s in self.sems.values() if s.count > 0]
        for e in ENGS:
            wm = self.wm[e]
            for (s, v) in targets:
                if v > 0 and wm.get(s.name, 0) < v:
                    wm[s.name] = v
                    self.ops[e].append(Op("wait_ge", (), {}, [(s, v)], []))


def _replay(plan, eng_name, eng):
    for op in plan.ops[eng_name]:
        for (s, v) in op.waits:
            eng.wait_ge(s.handle, v)
        if op.method == "wait_ge":
            continue
        inst = getattr(eng, op.method)(*op.args, **op.kwargs)
        for (s, n) in op.incs:
            inst.then_inc(s.handle, n)


def build_nc():
    nc = bass.Bass("TRN2", target_bir_lowering=False)
    P = Plan()
    sLD = P.psem["sync"]
    sGP = P.psem["gpsimd"]
    sPE = P.psem["tensor"]
    sDV = P.psem["vector"]
    sAC = P.psem["scalar"]
    sST = P.sst

    xh = nc.dram_tensor("xh", [2, 1024, D_MODEL], F32, kind="ExternalInput").ap()
    wqkv = nc.dram_tensor("wqkv", [D_MODEL, 6144], F32, kind="ExternalInput").ap()
    wo = nc.dram_tensor("wo", [D_MODEL, D_MODEL], F32, kind="ExternalInput").ap()
    wf = nc.dram_tensor("wf", [D_MODEL, D_MODEL], I8, kind="ExternalInput").ap()
    bias8 = nc.dram_tensor("bias8", [128, D_MODEL], I8, kind="ExternalInput").ap()
    masks = nc.dram_tensor("masks", [2, 8, 128, 512], BF16, kind="ExternalInput").ap()
    ropek = nc.dram_tensor("ropek", [2, 2, 128, 1024], F32, kind="ExternalInput").ap()
    ropeq = nc.dram_tensor("ropeq", [2, 2, 128, 512], F32, kind="ExternalInput").ap()
    cid = nc.dram_tensor("cid", [3, 128, 128], BF16, kind="ExternalInput").ap()
    onesf_d = nc.dram_tensor("onesf", [128, 128], F32, kind="ExternalInput").ap()
    out_sh = nc.dram_tensor("out_sh", [1024, D_MODEL], F32, kind="ExternalOutput").ap()
    attn_scr = nc.dram_tensor("attn_scr", [NH, 128, 1024], BF16, kind="Internal").ap()
    x1_scr = nc.dram_tensor("x1_scr", [1024, D_MODEL], F32, kind="Internal").ap()
    h1own_scr = nc.dram_tensor("h1own_scr", [128, NC_ * 512], BF16, kind="Internal").ap()
    wqkv_bf = nc.dram_tensor("wqkv_bf", [D_MODEL, 6144], BF16, kind="Internal").ap()
    wo_bf = nc.dram_tensor("wo_bf", [D_MODEL, D_MODEL], BF16, kind="Internal").ap()
    wf_bf = nc.dram_tensor("wf_bf", [D_MODEL, D_MODEL], BF16, kind="Internal").ap()

    ctx = ExitStack()

    def sb(name, shape, dt):
        return ctx.enter_context(nc.sbuf_tensor(name, shape, dt))

    idn = sb("idn", [128, 128], BF16)
    onesb = sb("onesb", [128, 128], BF16)
    onesf = sb("onesf_t", [128, 128], F32)
    rcomb = sb("rcomb", [128, 128], BF16)
    epscol = sb("epscol", [128, 1], F32)
    biascol = sb("biascol", [128, NH], F32)
    mk2row = sb("mk2row", [1, NKV], F32)
    rrow = sb("rrow", [1, 512], F32)
    brow = sb("brow", [1, 4], F32)

    s_pckv = P.sem("pc_kv")
    s_pcq = P.sem("pc_q")
    s_pcwo = P.sem("pc_wo")
    s_pcwf = P.sem("pc_wf")
    for i in range(2):
        P.dma("gpsimd", wqkv_bf[:, 4096 + i * 1024:4096 + (i + 1) * 1024],
              wqkv[:, 4096 + i * 1024:4096 + (i + 1) * 1024], sem=s_pckv)
    t_pckv = (s_pckv, s_pckv.count)
    t_pcq = (s_pcq, 16 * 16)
    t_pcwo = (s_pcwo, 16 * 16)
    t_pcwf = (s_pcwf, 16 * 16)

    def precast_q(i):
        P.dma("gpsimd", wqkv_bf[:, i * 256:(i + 1) * 256],
              wqkv[:, i * 256:(i + 1) * 256], sem=s_pcq)

    def precast_wowf(i):
        if i < 16:
            P.dma("gpsimd", wo_bf[:, i * 256:(i + 1) * 256],
                  wo[:, i * 256:(i + 1) * 256], sem=s_pcwo)
        elif i < 32:
            j = i - 16
            P.dma("gpsimd", wf_bf[:, j * 256:(j + 1) * 256],
                  wf[:, j * 256:(j + 1) * 256], sem=s_pcwf)

    t_idn = P.dma("sync", idn[:], cid[0])
    t_ones = P.dma("sync", onesb[:], cid[1])
    t_rcomb = P.dma("sync", rcomb[:], cid[2])
    t_onesf = P.dma("sync", onesf[:], onesf_d)
    P.emit("vector", "memset", epscol[:], EPS, inc=True)

    ctx1 = ExitStack()

    def sb1(name, shape, dt):
        return ctx1.enter_context(nc.sbuf_tensor(name, shape, dt))

    h1_fm = sb1("h1_fm", [128, NC_, 1024], BF16)
    kfm = sb1("kfm", [128, NKV, 2048], BF16)
    vtok = sb1("vtok", [128, 16, 1024], BF16)
    ropek_sb = sb1("ropek_sb", [128, 2, 1024], F32)
    tmpb = sb1("tmpb", [128, 2, 512], BF16)

    rope_st = {"c1": None, "c2": None}

    def rope_emit(pb, csA, csB, dst_ev, dst_od, first_waits, pcA, pcB, freeA, freeB):
        st = rope_st
        w1 = list(first_waits)
        if st["c1"]:
            w1.append(st["c1"])
        tA = P.emit("vector", "tensor_mul", tmpb[:, 0, :], pb[:], csA,
                    waits=w1, inc=True)
        w2 = [st["c2"]] if st["c2"] else []
        t4 = P.emit("vector", "tensor_mul", tmpb[:, 1, :], pb[:], csB,
                    waits=w2, inc=True)
        wc = [tA, t4, t_rcomb] + [t for t in freeA if t]
        mm1 = P.emit("tensor", "matmul", pcA[:], rcomb[:], tmpb[:, 0, :],
                     start=True, stop=True, waits=wc, inc=True)
        mm2 = P.emit("tensor", "matmul", pcB[:], rcomb[:], tmpb[:, 1, :],
                     start=True, stop=True,
                     waits=[t4] + [t for t in freeB if t], inc=True)
        e1 = P.emit("vector", "tensor_copy", dst_ev, pcA[0:64, :],
                    waits=[mm1], inc=True)
        e2 = P.emit("vector", "tensor_copy", dst_od, pcB[64:128, :],
                    waits=[mm2], inc=True)
        st.update(c1=mm1, c2=mm2)
        return t4, (e1, e2)

    def rope_emit_ct(pb, csA, csB, dst, first_waits, efree):
        st = rope_st
        w1 = list(first_waits)
        if st["c1"]:
            w1.append(st["c1"])
        tA = P.emit("vector", "tensor_mul", tmpb[:, 0, :], pb[:], csA,
                    waits=w1, inc=True)
        w2 = [st["c2"]] if st["c2"] else []
        t4 = P.emit("vector", "tensor_mul", tmpb[:, 1, :], pb[:], csB,
                    waits=w2, inc=True)
        mm1 = P.emit("tensor", "matmul", pb[0:64, :], rcomb[:, 0:64],
                     tmpb[:, 0, :], start=True, stop=True,
                     tile_position=(0, 0),
                     waits=[tA, t4, t_rcomb] + [t for t in efree if t], inc=True)
        mm2 = P.emit("tensor", "matmul", pb[64:128, :], rcomb[:, 64:128],
                     tmpb[:, 1, :], start=True, stop=True,
                     tile_position=(0, 64), waits=[t4], inc=True)
        e = P.emit("vector", "tensor_copy", dst, pb[:],
                   waits=[mm1, mm2], inc=True)
        st.update(c1=mm1, c2=mm2)
        return e

    def ln_phase(src_rows_ap, n_tiles, dst_fm, dst_cols_base, sub, after_tile=None):
        xt = sub["xt"]
        h1t = sub["h1t"]
        nmr = sub["nmr"]
        nmean = sub["nmean"]
        varsum = sub["varsum"]
        stdv = sub["stdv"]
        rstd = sub["rstd"]
        pA = sub["pA"]
        toks = sub.setdefault("toks", {})

        def front(i):
            bs = i % 2
            src = src_rows_ap(i)
            prev = toks.get(("free", bs))
            tl = P.dma("sync", xt[bs][:], src, waits=prev if prev else [])
            t1 = P.emit("vector", "reduce_sum", nmr[:, bs:bs + 1], xt[bs][:],
                        axis=AX.X, negate=True, waits=[tl], inc=True)
            t2 = P.emit("vector", "tensor_scalar_mul", nmean[:, bs:bs + 1],
                        nmr[:, bs:bs + 1], 1.0 / D_MODEL, waits=[t1], inc=True)
            wsq = [t2]
            if ("sqfree", bs) in toks:
                wsq.append(toks[("sqfree", bs)])
            t3 = P.emit("scalar", "activation", h1t[bs][:, 0:4096], xt[bs][:],
                        AF.Square, bias=nmean[:, bs:bs + 1], scale=1.0,
                        accum_out=varsum[:, bs:bs + 1], waits=wsq, inc=True)
            t4 = P.emit("scalar", "activation", stdv[:, bs:bs + 1],
                        varsum[:, bs:bs + 1], AF.Sqrt,
                        bias=epscol[:, 0:1], scale=1.0 / D_MODEL,
                        waits=[t3], inc=True)
            toks[("front", i)] = (t3, t4)

        def back(i):
            bs = i % 2
            t3, t4 = toks.pop(("front", i))
            t5 = P.emit("vector", "reciprocal", rstd[:, bs:bs + 1],
                        stdv[:, bs:bs + 1], waits=[t4], inc=True)
            t6 = P.emit("vector", "tensor_scalar", h1t[bs][:, 0:4096], xt[bs][:],
                        nmean[:, bs:bs + 1], rstd[:, bs:bs + 1],
                        ALU.add, ALU.mult, waits=[t5, t3], inc=True)
            toks[("free", bs)] = [t6, t3]
            for cc in range(4):
                pa = pA[cc % 2]
                wtp = [t6]
                if ("pafree", cc % 2) in toks:
                    wtp.append(toks[("pafree", cc % 2)])
                tp = None
                for c8 in range(8):
                    c = cc * 8 + c8
                    tp = P.emit("tensor", "transpose",
                                pa[:, c8 * 128:(c8 + 1) * 128],
                                h1t[bs][:, c * 128:(c + 1) * 128], idn[:],
                                waits=(wtp + [t_idn]) if c8 == 0 else (),
                                inc=True if c8 == 7 else None)
                col = dst_cols_base + i * 128
                te = P.emit("vector", "tensor_copy",
                            dst_fm[:, cc * 8:(cc + 1) * 8, col:col + 128],
                            pa[:].rearrange("p (c m) -> p c m", c=8),
                            waits=[tp], inc=True)
                toks[("pafree", cc % 2)] = te
                toks[("evac", i, cc)] = te
            toks[("sqfree", bs)] = toks[("evac", i, 3)]
            if after_tile is not None:
                after_tile(i, toks)

        for i in range(n_tiles + 1):
            if i < n_tiles:
                front(i)
            if i >= 1:
                back(i - 1)
        return toks

    kh_w_tok = {}
    v_w_tok = {}
    last_b_tokens = []

    for h in range(2):
        ctxA = ExitStack()
        subA = {
            "xt": [ctxA.enter_context(nc.sbuf_tensor(f"xtA{h}_{i}", [128, 4096], F32)) for i in range(2)],
            "h1t": [ctxA.enter_context(nc.sbuf_tensor(f"h1tA{h}_{i}", [128, 4096], BF16)) for i in range(2)],
            "nmr": ctxA.enter_context(nc.sbuf_tensor(f"nmrA{h}", [128, 2], F32)),
            "nmean": ctxA.enter_context(nc.sbuf_tensor(f"nmeanA{h}", [128, 2], F32)),
            "varsum": ctxA.enter_context(nc.sbuf_tensor(f"varsumA{h}", [128, 2], F32)),
            "stdv": ctxA.enter_context(nc.sbuf_tensor(f"stdvA{h}", [128, 2], F32)),
            "rstd": ctxA.enter_context(nc.sbuf_tensor(f"rstdA{h}", [128, 2], F32)),
            "pA": [ctxA.enter_context(nc.psum_tensor(f"pAA{h}_{i}", [128, 1024], BF16)) for i in range(2)],
        }
        toksA = ln_phase(lambda i: xh[h, i * 128:(i + 1) * 128, :], 8,
                         h1_fm, 0, subA)
        if h == 0:
            w = [toksA[("evac", i, cc)] for i in range(4) for cc in range(4)]
            P.dma("sync", h1own_scr.rearrange("p (c m) -> p c m", c=NC_),
                  h1_fm[:, :, 0:512], waits=w)
        P.barrier()
        ctxA.close()

        ctxB = ExitStack()
        wkb = [ctxB.enter_context(nc.sbuf_tensor(f"wkb{h}_{i}", [128, NC_, 128], BF16)) for i in range(2)]
        wvb = [ctxB.enter_context(nc.sbuf_tensor(f"wvb{h}_{i}", [128, NC_, 256], BF16)) for i in range(2)]
        ksq = ctxB.enter_context(nc.sbuf_tensor(f"ksq{h}", [128, 512], BF16))
        pB = [ctxB.enter_context(nc.psum_tensor(f"pB{h}_{i}", [128, 512], F32)) for i in range(2)]
        pK2 = ctxB.enter_context(nc.psum_tensor(f"pK2{h}", [128, 512], F32))
        pRC = ctxB.enter_context(nc.psum_tensor(f"pRC{h}", [128, 1024], F32))

        t_rk = P.dma("sync", ropek_sb[:], ropek[h].rearrange("c p t -> p c t"))

        if h == 0:
            P.emit("vector", "memset", mk2row[:], 0.0, inc=True)

        kl_tok = {}
        pend = []

        def emit_k_mm(kh, g):
            wb = wkb[kh % 2]
            if g == 0:
                wfree = kl_tok.get(("wfree", kh % 2))
                kl_tok[("tw", kh)] = P.dma(
                    "gpsimd", wb[:],
                    wqkv_bf[:, 4096 + kh * 128:4096 + (kh + 1) * 128]
                    .rearrange("(c p) m -> p c m", p=128),
                    waits=[t_pckv] + ([wfree] if wfree else []))
            tw = kl_tok[("tw", kh)]
            u = kh * 2 + g
            pb = pB[u % 2]
            pfree = kl_tok.get(("pfree", u % 2))
            mm = None
            for c in range(NC_):
                mm = P.emit("tensor", "matmul", pb[:],
                            wb[:, c, :], h1_fm[:, c, g * 512:(g + 1) * 512],
                            start=(c == 0), stop=(c == NC_ - 1),
                            waits=([tw] + ([pfree] if pfree else [])) if c == 0 else (),
                            inc=True if c == NC_ - 1 else None)
            kl_tok[("wfree", kh % 2)] = mm
            if h == 0:
                precast_q(u)
            return (kh, g, pb, mm)

        def emit_k_rope(unit):
            kh, g, pb, mm = unit
            u = kh * 2 + g
            csA = ropek_sb[:, 0, g * 512:(g + 1) * 512]
            csB = ropek_sb[:, 1, g * 512:(g + 1) * 512]
            col = h * 1024 + g * 512
            kc = kfm[:, kh, col:col + 512]
            kev = kfm[0:64, kh, col:col + 512]
            kod = kfm[64:128, kh, col:col + 512]
            rcfree = kl_tok.get(("rcfree",), (None, None))
            t4, (e1, e2) = rope_emit(pb, csA, csB, kev, kod, [mm, t_rk],
                                     pRC[:, 0:512], pRC[:, 512:1024],
                                     [rcfree[0]], [rcfree[1]])
            kl_tok[("pfree", u % 2)] = t4
            kl_tok[("rcfree",)] = (e1, e2)
            t7 = P.emit("vector", "tensor_mul", ksq[:], kc, kc,
                        waits=[e1, e2], inc=True)
            pk2free = kl_tok.get(("pk2free",))
            t8 = P.emit("tensor", "matmul", pK2[0:1, :], onesb[:, 0:1], ksq[:],
                        start=True, stop=True,
                        waits=[t7, t_ones] + ([pk2free] if pk2free else []),
                        inc=True)
            t9 = P.emit("vector", "reduce_max", brow[0:1, 0:1], pK2[0:1, :],
                        axis=AX.X, waits=[t8], inc=True)
            P.emit("vector", "tensor_max", mk2row[0:1, kh:kh + 1],
                   mk2row[0:1, kh:kh + 1], brow[0:1, 0:1], waits=[t9], inc=True)
            kl_tok[("pk2free",)] = t9

        for kh in range(NKV):
            for g in range(2):
                unit = emit_k_mm(kh, g)
                pend.append(unit)
                if len(pend) > 1:
                    emit_k_rope(pend.pop(0))
        emit_k_rope(pend.pop(0))

        for g in range(4):
            wb = wvb[g % 2]
            wfree = v_w_tok.get(("wfree", h, g % 2))
            tw = P.dma("gpsimd", wb[:],
                       wqkv_bf[:, 5120 + g * 256:5120 + (g + 1) * 256]
                       .rearrange("(c p) m -> p c m", p=128),
                       waits=[t_pckv] + ([wfree] if wfree else []))
            for m in range(8):
                pb = pB[(g * 8 + m) % 2]
                pfree = v_w_tok.get(("pfree", h, (g * 8 + m) % 2))
                mm = None
                for c in range(NC_):
                    mm = P.emit("tensor", "matmul", pb[:, 0:256],
                                h1_fm[:, c, m * 128:(m + 1) * 128],
                                wb[:, c, :],
                                start=(c == 0), stop=(c == NC_ - 1),
                                waits=([tw] + ([pfree] if pfree else [])) if c == 0 else (),
                                inc=True if c == NC_ - 1 else None)
                v_w_tok[("wfree", h, g % 2)] = mm
                te = P.emit("vector", "tensor_copy",
                            vtok[:, h * 8 + m, g * 256:(g + 1) * 256],
                            pb[:, 0:256], waits=[mm], inc=True)
                v_w_tok[("pfree", h, (g * 8 + m) % 2)] = te
                last_b_tokens.append(te)
        P.barrier()
        ctxB.close()

    ctxC = ExitStack()

    def sbC(name, shape, dt):
        return ctxC.enter_context(nc.sbuf_tensor(name, shape, dt))

    masks_sb = sbC("masks_sb", [128, 2, 8, 512], BF16)
    ropeq_sb = sbC("ropeq_sb", [128, 2, 2, 512], F32)
    wqb = [sbC(f"wqb{i}", [128, NC_, 128], BF16) for i in range(2)]
    qbuf = [sbC(f"qbuf{i}", [128, 1024], BF16) for i in range(2)]
    qsq = sbC("qsq", [128, 1024], BF16)
    NPS = 5
    probs = [sbC(f"probs{i}", [128, 512], BF16) for i in range(NPS)]
    rb = sbC("rb", [128, 512], F32)
    atout = [sbC(f"atout{i}", [128, 512], BF16) for i in range(2)]
    psco = [ctxC.enter_context(nc.psum_tensor(f"psco{i}", [128, 512], F32)) for i in range(NPS)]
    pat = ctxC.enter_context(nc.psum_tensor("pat", [128, 512], F32))
    prs = ctxC.enter_context(nc.psum_tensor("prs", [128, 512], F32))
    pQ = ctxC.enter_context(nc.psum_tensor("pQ", [128, 512], F32))

    t_mask = P.dma("sync", masks_sb[:], masks.rearrange("s t p m -> p s t m"))
    t_rq = P.dma("sync", ropeq_sb[:], ropeq.rearrange("s c p m -> p s c m"))
    t_h1o = P.dma("sync", h1_fm[:, :, 512:1024],
                  h1own_scr.rearrange("p (c m) -> p c m", c=NC_))

    ct = {}

    def emit_qprod(hd):
        kh = hd // 4
        wb = wqb[hd % 2]
        qb = qbuf[hd % 2]
        wfree = ct.get(("wqfree", hd % 2))
        tw = P.dma("gpsimd", wb[:],
                   wqkv_bf[:, hd * 128:(hd + 1) * 128]
                   .rearrange("(c p) m -> p c m", p=128),
                   waits=[t_pcq] + ([wfree] if wfree else []))
        precast_wowf(hd)
        yield
        etoks = []
        for g in range(2):
            pqfree = ct.get(("pqfree",))
            mm = None
            for cc in range(8):
                for c4 in range(4):
                    c = cc * 4 + c4
                    rhs = h1_fm[:, c, 512:1024] if g == 0 else h1_fm[:, c, 0:512]
                    mm = P.emit("tensor", "matmul", pQ[:], wb[:, c, :], rhs,
                                start=(c == 0), stop=(c == NC_ - 1),
                                waits=([tw, t_h1o] + ([pqfree] if pqfree else [])) if c == 0 else (),
                                inc=True if c == NC_ - 1 else None)
                yield
            ct[("wqfree", hd % 2)] = mm
            csA = ropeq_sb[:, g, 0, :]
            csB = ropeq_sb[:, g, 1, :]
            qfree = ct.get(("qfree", hd % 2))
            w0 = [mm, t_rq] + ([qfree] if qfree and g == 0 else [])
            e = rope_emit_ct(pQ, csA, csB, qb[:, g * 512:(g + 1) * 512], w0,
                             [ct.get(("pqe",))])
            ct[("pqfree",)] = e
            ct[("pqe",)] = e
            etoks.append(e)
            ct[("qready", hd, g)] = (e, e)
            yield

        tq1 = P.emit("vector", "tensor_mul", qsq[:], qb[:], qb[:],
                     waits=etoks, inc=True)
        ta = P.emit("tensor", "matmul", pQ[0:1, :], onesb[:, 0:1], qsq[:, 0:512],
                    start=True, stop=True, tile_position=(0, 0),
                    waits=[tq1, etoks[1]], inc=True)
        tb = P.emit("tensor", "matmul", pQ[64:65, :], onesb[:, 0:1], qsq[:, 512:1024],
                    start=True, stop=True, tile_position=(0, 64),
                    waits=[tq1], inc=True)
        yield
        tm1 = P.emit("vector", "reduce_max", brow[0:1, 0:1], pQ[0:1, :],
                     axis=AX.X, waits=[ta], inc=True)
        tm2 = P.emit("vector", "reduce_max", brow[0:1, 1:2], pQ[64:65, :],
                     axis=AX.X, waits=[tb], inc=True)
        tm3 = P.emit("vector", "tensor_max", brow[0:1, 0:1], brow[0:1, 0:1],
                     brow[0:1, 1:2], waits=[tm2], inc=True)
        tm4 = P.emit("vector", "tensor_scalar", brow[0:1, 2:3], brow[0:1, 0:1],
                     mk2row[0:1, kh:kh + 1], -0.5, ALU.add, ALU.mult,
                     waits=[tm3], inc=True)
        tm6 = P.emit("vector", "tensor_scalar_max", brow[0:1, 2:3], brow[0:1, 2:3],
                     -80.0, waits=[tm4], inc=True)
        tbc = P.emit("tensor", "matmul", pQ[:, 0:1], onesf[0:1, 0:128],
                     brow[0:1, 2:3], start=True, stop=True,
                     waits=[tm6, t_onesf, tm1, tm2], inc=True)
        tbc2 = P.emit("vector", "tensor_copy", biascol[:, hd:hd + 1],
                      pQ[:, 0:1], waits=[tbc], inc=True)
        ct[("pqfree",)] = tbc2
        ct[("pqe",)] = tbc2
        ct[("tbc2", hd)] = tbc2

    def emit_scores(hd, s, t, qb):
        kh = hd // 4
        pc = psco[t % NPS]
        pcfree = ct.get(("pscofree", t % NPS))
        tbc2 = ct[("tbc2", hd)]
        w0 = [tbc2] + list(ct[("qready", hd, s)])
        if pcfree is not None:
            w0.append(pcfree)
        return P.emit("tensor", "matmul", pc[:],
                      kfm[:, kh, t * 128:(t + 1) * 128],
                      qb[:, s * 512:(s + 1) * 512],
                      start=True, stop=True, waits=w0, inc=True)

    def emit_attention(hd, feed=None):
        kh = hd // 4
        qb = qbuf[hd % 2]
        LOOK = 4
        vlast = None
        for s in range(2):
            D = DS[s]
            sc_tok = {}
            for t in range(min(LOOK + 1, D)):
                sc_tok[t] = emit_scores(hd, s, t, qb)
            rs_tok = None
            for t in range(D):
                if feed is not None:
                    next(feed, None)
                pc = psco[t % NPS]
                pbuf = probs[t % NPS]
                pbfree = ct.get(("probsfree", t % NPS))
                wexp = [sc_tok[t]]
                if pbfree is not None:
                    wexp.append(pbfree)
                te = P.emit("scalar", "activation", pbuf[:], pc[:], AF.Exp,
                            bias=biascol[:, hd:hd + 1], scale=1.0,
                            waits=wexp, inc=True)
                ct[("pscofree", t % NPS)] = te
                if t + LOOK + 1 < D:
                    sc_tok[t + LOOK + 1] = emit_scores(hd, s, t + LOOK + 1, qb)
                masked = (s == 0) or (t >= 8)
                tok_ready = te
                if masked:
                    moff = 0 if s == 0 else 8
                    wm = [te, t_mask]
                    if pbfree is not None:
                        wm.append(pbfree)
                    tok_ready = P.emit("vector", "tensor_mul", pbuf[:], pbuf[:],
                                       masks_sb[:, s, t - moff, :],
                                       waits=wm, inc=True)
                w0 = [tok_ready, te]
                if t == 0:
                    if ("patfree2",) in ct:
                        w0.append(ct[("patfree2",)])
                    if ("prsfree2",) in ct:
                        w0.append(ct[("prsfree2",)])
                vlast = P.emit("tensor", "matmul", pat[:],
                               vtok[:, t, kh * 128:(kh + 1) * 128],
                               pbuf[:],
                               start=(t == 0), stop=(t == D - 1),
                               waits=w0,
                               inc=True if t == D - 1 else None)
                rs_tok = P.emit("tensor", "matmul", prs[:],
                                onesb[:], pbuf[:],
                                start=(t == 0), stop=(t == D - 1),
                                inc=True)
                ct[("probsfree", t % NPS)] = rs_tok
            tr = P.emit("vector", "reciprocal", rrow[0:1, :], prs[0:1, :],
                        waits=[rs_tok], inc=True)
            ct[("prsfree2",)] = tr
            bb = D % NPS
            tb2 = P.emit("tensor", "matmul", psco[bb][:], onesf[0:1, 0:128],
                         rrow[0:1, :], start=True, stop=True,
                         waits=[tr, ct[("pscofree", bb)]], inc=True)
            tc = P.emit("vector", "tensor_copy", rb[:], psco[bb][:],
                        waits=[tb2], inc=True)
            ct[("pscofree", bb)] = tc
            ao = atout[s]
            aofree = ct.get(("aofree", s))
            tm = P.emit("vector", "tensor_mul", ao[:], pat[:], rb[:],
                        waits=[tc, vlast] + ([aofree] if aofree else []), inc=True)
            ct[("patfree2",)] = tm
            ts_ = P.dma("sync", attn_scr[hd, :, s * 512:(s + 1) * 512], ao[:],
                        waits=[tm])
            ct[("aofree", s)] = ts_
        ct[("qfree", hd % 2)] = vlast

    gen = emit_qprod(0)
    for _ in gen:
        pass
    for hd in range(NH):
        feed = emit_qprod(hd + 1) if hd + 1 < NH else None
        emit_attention(hd, feed)
        if feed is not None:
            for _ in feed:
                pass
    P.barrier()
    ctxC.close()
    ctx1.close()

    ctxD = ExitStack()

    def sbD(name, shape, dt):
        return ctxD.enter_context(nc.sbuf_tensor(name, shape, dt))

    wob = [sbD(f"wob{i}", [128, NC_, 512], BF16) for i in range(3)]
    atb = [sbD(f"atb{i}", [128, NC_, 512], BF16) for i in range(2)]
    xrb = [sbD(f"xrb{i}", [128, 512], F32) for i in range(2)]
    x1b = [sbD(f"x1b{i}", [128, 512], F32) for i in range(2)]
    pD = [ctxD.enter_context(nc.psum_tensor(f"pD{i}", [128, 512], F32)) for i in range(2)]

    dt_ = {}
    it = 0
    tal = [P.dma("sync", atb[hf][:],
                 attn_scr[:, :, hf * 512:(hf + 1) * 512]
                 .rearrange("c p m -> p c m")) for hf in range(2)]
    for n in range(8):
        wb = wob[n % 3]
        wfree = dt_.get(("wofree", n % 3))
        tws = [P.dma("gpsimd", wb[:],
                     wo_bf[:, n * 512:(n + 1) * 512]
                     .rearrange("(c p) m -> p c m", p=128),
                     waits=[t_pcwo] + ([wfree] if wfree else []))]
        for hf in range(2):
            ab = atb[hf]
            for ml in range(4):
                m = hf * 4 + ml
                pb = pD[it % 2]
                pfree = dt_.get(("pDfree", it % 2))
                mm = None
                for c in range(NC_):
                    w0 = []
                    if c == 0:
                        w0 = [tal[hf], tws[-1]] + ([pfree] if pfree else [])
                    mm = P.emit("tensor", "matmul", pb[:],
                                ab[:, c, ml * 128:(ml + 1) * 128],
                                wb[:, c, :],
                                start=(c == 0), stop=(c == NC_ - 1),
                                waits=w0, inc=True if c == NC_ - 1 else None)
                dt_[("wofree", n % 3)] = mm
                xb_ = xrb[it % 2]
                xfree = dt_.get(("xrfree", it % 2))
                tx = P.dma("sync", xb_[:],
                           xh[hf, ml * 128:(ml + 1) * 128, n * 512:(n + 1) * 512],
                           waits=[xfree] if xfree else [])
                ob = x1b[it % 2]
                ofree = dt_.get(("x1free", it % 2))
                tadd = P.emit("vector", "tensor_add", ob[:], pb[:], xb_[:],
                              waits=[mm, tx] + ([ofree] if ofree else []), inc=True)
                dt_[("pDfree", it % 2)] = tadd
                dt_[("xrfree", it % 2)] = tadd
                tst = P.dma("sync", x1_scr[m * 128:(m + 1) * 128, n * 512:(n + 1) * 512],
                            ob[:], waits=[tadd])
                dt_[("x1free", it % 2)] = tst
                it += 1
    P.barrier()
    ctxD.close()

    ctxE = ExitStack()

    def sbE(name, shape, dt):
        return ctxE.enter_context(nc.sbuf_tensor(name, shape, dt))

    h2_fm = sbE("h2_fm", [128, NC_, 1024], BF16)
    subE = {
        "xt": [sbE(f"xtE{i}", [128, 4096], F32) for i in range(2)],
        "h1t": [sbE(f"h1tE{i}", [128, 4096], BF16) for i in range(2)],
        "nmr": sbE("nmrE", [128, 2], F32),
        "nmean": sbE("nmeanE", [128, 2], F32),
        "varsum": sbE("varsumE", [128, 2], F32),
        "stdv": sbE("stdvE", [128, 2], F32),
        "rstd": sbE("rstdE", [128, 2], F32),
        "pA": [ctxE.enter_context(nc.psum_tensor(f"pAE{i}", [128, 1024], BF16)) for i in range(2)],
    }
    wfb = [sbE(f"wfb{i}", [128, NC_, 512], BF16) for i in range(2)]
    biasb = sbE("biasb", [128, 4096], BF16)
    x1r = [sbE(f"x1r{i}", [128, 512], F32) for i in range(2)]
    outb = [sbE(f"outb{i}", [128, 512], F32) for i in range(2)]
    pE = [ctxE.enter_context(nc.psum_tensor(f"pE{i}", [128, 512], F32)) for i in range(2)]

    t_bias = P.dma("gpsimd", biasb[:], bias8)
    et = {"it": 0}

    def load_wf(n):
        wfree = et.get(("wffree", n % 2))
        return P.dma("gpsimd", wfb[n % 2][:],
                     wf_bf[:, n * 512:(n + 1) * 512]
                     .rearrange("(c p) m -> p c m", p=128),
                     waits=[t_pcwf] + ([wfree] if wfree else []))

    def emit_e2(n, m, tw, toks):
        it = et["it"]
        wb = wfb[n % 2]
        pb = pE[it % 2]
        pfree = et.get(("pEfree", it % 2))
        mm = None
        for c in range(NC_):
            w0 = []
            if c == 0:
                w0 = [tw]
                if pfree:
                    w0.append(pfree)
                w0 += [toks[("evac", m, cc)] for cc in range(4)]
            mm = P.emit("tensor", "matmul", pb[:],
                        h2_fm[:, c, m * 128:(m + 1) * 128],
                        wb[:, c, :],
                        start=(c == 0), stop=False,
                        waits=w0, inc=None)
        mm = P.emit("tensor", "matmul", pb[:], onesb[0:1, :],
                    biasb[0:1, n * 512:(n + 1) * 512],
                    start=False, stop=True, waits=[t_bias], inc=True)
        et[("wffree", n % 2)] = mm
        xb_ = x1r[it % 2]
        xfree = et.get(("x1rfree", it % 2))
        tx = P.dma("sync", xb_[:],
                   x1_scr[m * 128:(m + 1) * 128, n * 512:(n + 1) * 512],
                   waits=[xfree] if xfree else [])
        ob = outb[it % 2]
        ofree = et.get(("outfree", it % 2))
        tst2 = P.emit("vector", "scalar_tensor_tensor", ob[:], pb[:], 0.01,
                      xb_[:], ALU.mult, ALU.add,
                      waits=[mm, tx] + ([ofree] if ofree else []), inc=True)
        et[("pEfree", it % 2)] = tst2
        et[("x1rfree", it % 2)] = tst2
        tso = P.dma("sync", out_sh[m * 128:(m + 1) * 128, n * 512:(n + 1) * 512],
                    ob[:], waits=[tst2])
        et[("outfree", it % 2)] = tso
        et["it"] = it + 1

    tw0 = load_wf(0)
    toksE = ln_phase(lambda i: x1_scr[i * 128:(i + 1) * 128, :], 8, h2_fm, 0,
                     subE, after_tile=lambda i, toks: emit_e2(0, i, tw0, toks))
    for n in range(1, 8):
        tw = load_wf(n)
        for m in range(8):
            emit_e2(n, m, tw, toksE)

    P.emit("sync", "wait_ge_final")

    for s in P.sems.values():
        s.handle = nc.alloc_semaphore(name=s.name)

    final_waits = [(P.sems[n], P.sems[n].count) for n in sorted(P.store_sems)]

    with nc.Block() as block:
        @block.sync
        def _(sync):
            for op in P.ops["sync"]:
                if op.method == "wait_ge_final":
                    for (s, v) in final_waits:
                        sync.wait_ge(s.handle, v)
                    continue
                for (s, v) in op.waits:
                    sync.wait_ge(s.handle, v)
                if op.method == "wait_ge":
                    continue
                inst = getattr(sync, op.method)(*op.args, **op.kwargs)
                for (s, nn) in op.incs:
                    inst.then_inc(s.handle, nn)

        @block.gpsimd
        def _(gpsimd):
            _replay(P, "gpsimd", gpsimd)

        @block.tensor
        def _(tensor):
            _replay(P, "tensor", tensor)

        @block.vector
        def _(vector):
            _replay(P, "vector", vector)

        @block.scalar
        def _(scalar):
            _replay(P, "scalar", scalar)

    ctxE.close()
    ctx.close()
    return nc



def _perm_cols():
    perm = []
    for hd in range(NH):
        base = hd * HD
        perm += [base + 2 * j for j in range(64)] + [base + 2 * j + 1 for j in range(64)]
    for kh in range(NKV):
        base = 4096 + kh * HD
        perm += [base + 2 * j for j in range(64)] + [base + 2 * j + 1 for j in range(64)]
    perm += list(range(5120, 6144))
    return np.array(perm, dtype=np.int64)


def _core_meta(p):
    own = [0, 3] if p == 0 else [1, 2]
    other = [1, 2] if p == 0 else [0, 3]
    return own, other


def _rope_tables(pos):
    j = np.arange(64, dtype=np.float64)
    inv = 1.0 / (10000.0 ** (j / 64.0))
    ang = pos[:, None].astype(np.float64) * inv[None, :]
    cos = (np.cos(ang).T * S4).astype(np.float32)
    sin = (np.sin(ang).T * S4).astype(np.float32)
    csA = np.concatenate([cos, sin], axis=0)
    csB = np.concatenate([sin, cos], axis=0)
    return np.stack([csA, csB])


def kernel(L0_attn_x, L0_attn_w_qkv, L0_attn_w_o, L0_ffn_quantized_weight,
           L0_ffn_quantized_bias):
    x = np.asarray(L0_attn_x, dtype=np.float32)
    wqkv = np.ascontiguousarray(np.asarray(L0_attn_w_qkv, np.float32)[:, _perm_cols()])
    wo = np.ascontiguousarray(np.asarray(L0_attn_w_o, np.float32))
    wf = np.ascontiguousarray(np.asarray(L0_ffn_quantized_weight, np.int8).T)
    bias8 = np.ascontiguousarray(
        np.broadcast_to(np.asarray(L0_ffn_quantized_bias, np.int8).reshape(1, 4096),
                        (128, 4096)))
    rcomb = np.zeros((128, 128), np.float32)
    for j in range(64):
        rcomb[j, j] = 1.0
        rcomb[j + 64, j] = -1.0
        rcomb[j, j + 64] = 1.0
        rcomb[j + 64, j + 64] = 1.0
    cid = np.stack([np.eye(128), np.ones((128, 128)), rcomb]).astype(ml_dtypes.bfloat16)
    onesf = np.ones((128, 128), np.float32)

    nc = build_nc()
    in_maps = []
    metas = []
    for c in range(8):
        b, p = c >> 1, c & 1
        own, other = _core_meta(p)
        xb = x[b]
        xh = np.stack([
            np.concatenate([xb[own[h] * 512:(own[h] + 1) * 512],
                            xb[other[h] * 512:(other[h] + 1) * 512]])
            for h in range(2)])
        posk = np.concatenate([
            own[0] * 512 + np.arange(512), other[0] * 512 + np.arange(512),
            own[1] * 512 + np.arange(512), other[1] * 512 + np.arange(512)])
        masks = np.zeros((2, 8, 128, 512), np.float32)
        for s in range(2):
            posq = own[s] * 512 + np.arange(512)
            t0 = 0 if s == 0 else 8
            for jr in range(8):
                t = t0 + jr
                masks[s, jr] = (posk[t * 128:(t + 1) * 128, None] <= posq[None, :])
        ropek = np.stack([_rope_tables(posk[:1024]), _rope_tables(posk[1024:])])
        ropeq = np.stack([_rope_tables(own[s] * 512 + np.arange(512)) for s in range(2)])
        in_maps.append({
            "xh": np.ascontiguousarray(xh),
            "wqkv": wqkv, "wo": wo, "wf": wf, "bias8": bias8,
            "masks": masks.astype(ml_dtypes.bfloat16),
            "ropek": np.ascontiguousarray(ropek),
            "ropeq": np.ascontiguousarray(ropeq),
            "cid": cid, "onesf": onesf,
        })
        metas.append((b, own))

    res = run_bass_kernel_spmd(nc, in_maps, core_ids=list(range(8)))
    out = np.empty((4, SEQ, D_MODEL), np.float32)
    for c in range(8):
        b, own = metas[c]
        sh = res.results[c]["out_sh"]
        out[b, own[0] * 512:(own[0] + 1) * 512] = sh[0:512]
        out[b, own[1] * 512:(own[1] + 1) * 512] = sh[512:1024]
    return out
